# version 23
# baseline (speedup 1.0000x reference)
import numpy as np
import ml_dtypes
import concourse.bass as bass
import concourse.mybir as mybir
from concourse.bass_utils import run_bass_kernel_spmd
from contextlib import ExitStack

F32 = mybir.dt.float32
BF16 = mybir.dt.bfloat16
U16 = mybir.dt.uint16
AF = mybir.ActivationFunctionType
ALU = mybir.AluOpType
AX = mybir.AxisListType.X

C32 = 1.5 * 2.0 ** 23
D_MODEL = 1024
D_INNER = 2048
D_FFN = 3072
SEQ = 4096
TT = 512
NT = SEQ // TT
NC_D = D_MODEL // 128
NC_I = D_INNER // 128
NC_F = D_FFN // 128
NCORES = 8
QMAX = 32767.0
DBG = False
SUB = 8
NSUB = NC_F // SUB


class Sem:
    def __init__(self, cm, nc, name):
        self.h = cm.enter_context(nc.semaphore(name))
        self.n = 0

    def inc(self, amt=1):
        self.n += amt
        return self.n


class Op:
    __slots__ = ("eng", "waits", "emit", "incs")

    def __init__(self, eng, waits, emit, incs):
        self.eng, self.waits, self.emit, self.incs = eng, waits, emit, incs


class Sched:
    def __init__(self):
        self.ops = []

    def add(self, eng, emit, waits=(), inc=None, dma=False):
        incs = []
        if inc is not None:
            amt = 16 if dma else 1
            inc.inc(amt)
            incs.append((inc, amt))
        self.ops.append(Op(eng, [(s, v) for (s, v) in waits], emit, incs))


def build_nc():
    nc = bass.Bass("TRN2", target_bir_lowering=False, detect_race_conditions=False)
    cm = ExitStack()

    def din(name, shape, dt=F32):
        return nc.dram_tensor(name, shape, dt, kind="ExternalInput").ap()

    def dout(name, shape, dt=F32):
        return nc.dram_tensor(name, shape, dt, kind="ExternalOutput").ap()

    def dscr(name, shape, dt):
        if DBG:
            return nc.dram_tensor(name, shape, dt, kind="ExternalOutput").ap()
        return nc.dram_tensor(name, shape, dt).ap()

    x_fm = din("x_fm", [D_MODEL, SEQ])
    w_in = din("w_in", [D_MODEL, 2 * D_INNER])
    conv_pk = din("conv_pk", [128, NC_I * 4])
    w_b = din("w_b", [D_INNER, 16])
    w_c = din("w_c", [D_INNER, 16])
    a_vec = din("a_vec", [16, 1])
    d_pk = din("d_pk", [128, NC_I])
    w_out = din("w_out", [D_INNER, D_MODEL])
    w_up = din("w_up", [D_FFN, D_MODEL])
    w_gate = din("w_gate", [D_FFN, D_MODEL])
    w_down = din("w_down", [D_MODEL, D_FFN])
    ident_bf = din("ident_bf", [128, 128], BF16)
    ident_f = din("ident_f", [128, 128], F32)
    ones_col_bf = din("ones_col_bf", [128, 1], BF16)
    ones16_f = din("ones16_f", [16, 1])
    ones_row_f = din("ones_row_f", [1, 128])
    consts = din("consts", [128, 8])

    y_fm = dout("y_fm", [D_MODEL, SEQ])

    s_xm = dscr("s_xm", [D_INNER, SEQ], BF16)
    s_sres = dscr("s_sres", [D_INNER, SEQ], BF16)
    s_x2 = dscr("s_x2", [D_MODEL, SEQ], F32)
    s_x2n = dscr("s_x2n", [D_MODEL, SEQ], BF16)
    s_up = dscr("s_up", [D_FFN, SEQ], BF16)
    s_gate = dscr("s_gate", [D_FFN, SEQ], BF16)
    s_P = dscr("s_P", [D_FFN, SEQ], BF16)
    s_O1 = dscr("s_O1", [D_FFN, SEQ], BF16)
    s_wupT = dscr("s_wupT", [D_MODEL, D_FFN], BF16)
    s_wgateT = dscr("s_wgateT", [D_MODEL, D_FFN], BF16)
    s_wdownT = dscr("s_wdownT", [D_FFN, D_MODEL], BF16)
    cc_in = nc.dram_tensor("cc_in", [1, 2], F32).ap()
    cc_out = nc.dram_tensor("cc_out", [1, 2], F32).ap()

    g_es = ExitStack()

    def gsb(name, shape, dt):
        return g_es.enter_context(nc.sbuf_tensor(name, shape, dt))

    t_ident = gsb("t_ident", [128, 128], BF16)
    t_identf = gsb("t_identf", [128, 128], F32)
    t_ones_col = gsb("t_ones_col", [128, 1], BF16)
    t_ones16 = gsb("t_ones16", [16, 1], F32)
    t_ones_row = gsb("t_ones_row", [1, 128], F32)
    t_consts = gsb("t_consts", [128, 8], F32)
    t_sc1 = gsb("t_sc1", [1, 8], F32)
    t_scpp = gsb("t_scpp", [128, 8], F32)
    t_gmax = gsb("t_gmax", [1, 8], F32)
    t_ccin = gsb("t_ccin", [1, 2], F32)
    t_ccout = gsb("t_ccout", [1, 2], F32)
    t_amax = gsb("t_amax", [128, 1], F32)
    t_amax2 = gsb("t_amax2", [128, 1], F32)
    t_red = gsb("t_red", [128, 2], F32)

    ps_a = g_es.enter_context(nc.psum_tensor("ps_a", [128, TT], F32))
    ps_b = g_es.enter_context(nc.psum_tensor("ps_b", [128, TT], F32))
    ps_c = g_es.enter_context(nc.psum_tensor("ps_c", [128, TT], F32))
    ps_stat = g_es.enter_context(nc.psum_tensor("ps_stat", [1, 2 * TT], F32))
    ps_bc = g_es.enter_context(nc.psum_tensor("ps_bc", [128, TT], F32))
    ps_sm = g_es.enter_context(nc.psum_tensor("ps_sm", [64, TT], F32))
    ps_tr = g_es.enter_context(nc.psum_tensor("ps_tr", [128, 1024], BF16))

    S = Sched()
    SY = Sem(cm, nc, "d_sy")
    GP = Sem(cm, nc, "d_gp")
    SC = Sem(cm, nc, "d_sc")
    VE = Sem(cm, nc, "d_ve")
    TM = Sem(cm, nc, "t_mm")
    VD = Sem(cm, nc, "v_dn")
    SD = Sem(cm, nc, "s_dn")
    GD = Sem(cm, nc, "g_dn")

    def dma(queue_eng, qsem, out, in_, waits=()):
        S.add(queue_eng, lambda e, o=out, i=in_: e.dma_start(out=o, in_=i),
              waits=waits, inc=qsem, dma=True)

    def snap():
        return [(TM, TM.n), (VD, VD.n), (SD, SD.n), (GD, GD.n),
                (SY, SY.n), (GP, GP.n), (SC, SC.n), (VE, VE.n)]

    def csnap():
        return [(TM, TM.n), (VD, VD.n), (SD, SD.n)]

    w_es = ExitStack()

    def wsb(name, shape, dt):
        return w_es.enter_context(nc.sbuf_tensor(name, shape, dt))

    mx_es = ExitStack()

    def msb(name, shape, dt):
        return mx_es.enter_context(nc.sbuf_tensor(name, shape, dt))

    win_es = ExitStack()
    t_win = win_es.enter_context(nc.sbuf_tensor("t_win", [128, NC_D * 4096], BF16, side="right"))
    t_wout = msb("t_wout", [128, NC_I * 1024], BF16)
    t_wbc = msb("t_wbc", [128, NC_I * 32], BF16)
    t_conv = msb("t_conv", [128, NC_I * 4], F32)
    t_d = msb("t_d", [128, NC_I], F32)
    t_decay = msb("t_decay", [16, TT], F32)
    t_dec1 = msb("t_dec1", [16, 1], F32)

    for c in range(NC_D):
        dma("gpsimd", GP, t_win[:, c * 4096:(c + 1) * 4096],
            w_in[c * 128:(c + 1) * 128, :])
    for c in range(NC_I):
        dma("gpsimd", GP, t_wout[:, c * 1024:(c + 1) * 1024],
            w_out[c * 128:(c + 1) * 128, :])
        dma("gpsimd", GP, t_wbc[:, c * 32:c * 32 + 16],
            w_b[c * 128:(c + 1) * 128, :])
        dma("gpsimd", GP, t_wbc[:, c * 32 + 16:c * 32 + 32],
            w_c[c * 128:(c + 1) * 128, :])
    dma("sync", SY, t_conv[:], conv_pk)
    dma("sync", SY, t_d[:], d_pk)
    dma("sync", SY, t_dec1[:], a_vec)
    dma("sync", SY, t_ident[:], ident_bf)
    dma("sync", SY, t_identf[:], ident_f)
    dma("sync", SY, t_ones_col[:], ones_col_bf)
    dma("sync", SY, t_ones16[:], ones16_f)
    dma("sync", SY, t_ones_row[:], ones_row_f)
    dma("sync", SY, t_consts[:], consts)
    wl = [(SY, SY.n), (GP, GP.n)]

    S.add("scalar", lambda e: e.activation(out=t_dec1[:], in_=t_dec1[:],
                                           func=AF.Sigmoid, scale=-1.0),
          waits=wl, inc=SD)
    S.add("vector", lambda e: e.tensor_copy(out=t_decay[:],
                                            in_=t_dec1[:].broadcast_to([16, TT])),
          waits=[(SD, SD.n)], inc=VD)

    dq_mats = [(w_up, D_FFN, D_MODEL, s_wupT),
               (w_gate, D_FFN, D_MODEL, s_wgateT),
               (w_down, D_MODEL, D_FFN, s_wdownT)]
    dq_chunks = []
    for (q_src, q_M, q_K, q_dst) in dq_mats:
        for m in range(q_M // 128):
            dq_chunks.append((q_src, q_M, q_K, q_dst, m))

    dq_ld = {}
    dq_rd = {}
    dq_wqm = {}
    dq_sdc = {}
    dq_scq = {}
    dq_gg = [0]

    def dq_load(g):
        q_src, q_M, q_K, q_dst, m = dq_chunks[g]
        wb = (g % 2) * 3072
        dma("sync", SY, t_wrow[:, wb:wb + q_K],
            q_src[m * 128:(m + 1) * 128, :],
            waits=[(VD, dq_rd.get(g - 2, 0))])
        dq_ld[g] = (SY, SY.n)

    def dq_chain(g):
        q_src, q_M, q_K, q_dst, m = dq_chunks[g]
        NKB = q_K // 32
        wb = (g % 2) * 3072
        w3 = t_wrow[:, wb:wb + q_K].rearrange("p (g b) -> p g b", b=32)
        S.add("vector",
              lambda e, w3=w3, n=NKB: e.tensor_reduce(
                  out=t_bmax[:, 0:n].unsqueeze(2), in_=w3, axis=AX,
                  op=ALU.max, apply_absolute_value=True),
              waits=[dq_ld[g]], inc=VD)
        S.add("vector",
              lambda e, n=NKB: e.tensor_scalar(
                  out=t_sgm[:, 0:n], in0=t_bmax[:, 0:n],
                  scalar1=1.0 / 7.0, scalar2=1e-12, op0=ALU.mult, op1=ALU.max),
              waits=[(VD, VD.n)], inc=VD)
        S.add("vector",
              lambda e, n=NKB: e.reciprocal(out=t_rcs[:, 0:n],
                                            in_=t_sgm[:, 0:n]),
              waits=[(VD, VD.n)], inc=VD)
        rcs_bc = t_rcs[:, 0:NKB].unsqueeze(2).broadcast_to([128, NKB, 32])
        S.add("vector",
              lambda e, w3=w3, rb=rcs_bc: e.tensor_tensor(
                  out=w3, in0=w3, in1=rb, op=ALU.mult),
              waits=[(VD, VD.n)], inc=VD)
        S.add("vector",
              lambda e, wb=wb, K=q_K: e.tensor_scalar(
                  out=t_wrow[:, wb:wb + K], in0=t_wrow[:, wb:wb + K],
                  scalar1=C32, scalar2=None, op0=ALU.add),
              waits=[(VD, VD.n)], inc=VD)
        sgm_bc = t_sgm[:, 0:NKB].unsqueeze(2).broadcast_to([128, NKB, 32])
        S.add("vector",
              lambda e, w3=w3, sb_=sgm_bc, wb=wb, K=q_K: e.scalar_tensor_tensor(
                  out=t_wqm[:, wb:wb + K].rearrange("p (g b) -> p g b", b=32),
                  in0=w3, scalar=-C32, in1=sb_, op0=ALU.add, op1=ALU.mult),
              waits=[(VD, VD.n), (TM, dq_wqm.get(g - 2, 0))], inc=VD)
        dq_rd[g] = VD.n

    def dq_transposes(g):
        q_src, q_M, q_K, q_dst, m = dq_chunks[g]
        NK = q_K // 128
        qb = (g % 2) * 3072
        for grp in range(NK // 8):
            gg = dq_gg[0]
            dq_gg[0] += 1
            ts = (gg % 2) * 1024
            for kk in range(8):
                k = grp * 8 + kk
                S.add("tensor",
                      lambda e, qb=qb, k=k, kk=kk: e.transpose(
                          ps_tr[:, kk * 128:(kk + 1) * 128],
                          t_wqm[:, qb + k * 128:qb + (k + 1) * 128],
                          t_ident[:]),
                      waits=([(VD, dq_rd[g]), (SD, dq_sdc.get(gg - 1, 0))]
                             if kk == 0 else []),
                      inc=TM if kk == 7 else None)
            S.add("scalar",
                  lambda e, ts=ts: e.copy(out=t_trs[:, ts:ts + 1024],
                                          in_=ps_tr[:]),
                  waits=[(TM, TM.n), (SC, dq_scq.get(gg % 2, 0))], inc=SD)
            dq_sdc[gg] = SD.n
            out3 = q_dst[grp * 1024:(grp + 1) * 1024,
                         m * 128:(m + 1) * 128].rearrange(
                             "(k p) q -> p k q", p=128)
            in3 = t_trs[:, ts:ts + 1024].rearrange("p (k q) -> p k q", q=128)
            dma("scalar", SC, out3, in3, waits=[(SD, SD.n)])
            dq_scq[gg % 2] = SC.n
        dq_wqm[g] = TM.n

    def dq_gen():
        n = len(dq_chunks)
        dq_load(0)
        dq_load(1)
        for g in range(n):
            if g >= 1:
                dq_transposes(g - 1)
            dq_chain(g)
            if g + 2 < n:
                dq_load(g + 2)
            yield 1 if dq_chunks[g][2] == D_MODEL else 3
        dq_transposes(n - 1)

    dq_pump = dq_gen()
    dq_alive = [True]

    def pump(units):
        got = 0
        while dq_alive[0] and got < units:
            try:
                got += next(dq_pump)
            except StopIteration:
                dq_alive[0] = False

    def pump_all():
        while dq_alive[0]:
            pump(1000)

    w_es.close()

    ash_es = ExitStack()

    def hsb(name, shape, dt):
        return ash_es.enter_context(nc.sbuf_tensor(name, shape, dt))

    t_xsq = hsb("t_xsq", [128, NC_D * TT], BF16)
    st_mean = hsb("st_mean", [1, TT], F32)
    st_ex2 = hsb("st_ex2", [1, TT], F32)
    st_var = hsb("st_var", [1, TT], F32)
    st_r = hsb("st_r", [1, TT], F32)
    st_tmp = hsb("st_tmp", [1, TT], F32)
    t_rbc = hsb("t_rbc", [128, 2 * TT], BF16)
    t_mbc = hsb("t_mbc", [128, 2 * TT], BF16)

    dq_es = ExitStack()

    def qsb(name, shape, dt):
        return dq_es.enter_context(nc.sbuf_tensor(name, shape, dt))

    t_wrow = qsb("t_wrow", [128, 2 * 3072], F32)
    t_wqm = qsb("t_wqm", [128, 2 * 3072], BF16)
    t_trs = qsb("t_trs", [128, 2 * 1024], BF16)
    t_bmax = qsb("t_bmax", [128, 96], F32)
    t_sgm = qsb("t_sgm", [128, 96], F32)
    t_rcs = qsb("t_rcs", [128, 96], F32)

    def emit_rstd_chain(par, entry_waits=()):
        S.add("vector", lambda e: e.tensor_scalar(out=st_mean[:],
                                                  in0=ps_stat[0:1, 0:TT],
                                                  scalar1=1.0 / 1024.0, scalar2=None,
                                                  op0=ALU.mult),
              waits=list(entry_waits), inc=VD)
        S.add("vector", lambda e: e.tensor_scalar(out=st_ex2[:],
                                                  in0=ps_stat[0:1, TT:2 * TT],
                                                  scalar1=1.0 / 1024.0, scalar2=None,
                                                  op0=ALU.mult), inc=VD)
        psstat_free = VD.n
        S.add("vector", lambda e: e.tensor_tensor(out=st_tmp[:], in0=st_mean[:],
                                                  in1=st_mean[:], op=ALU.mult),
              inc=VD)
        S.add("vector", lambda e: e.tensor_tensor(out=st_var[:], in0=st_ex2[:],
                                                  in1=st_tmp[:], op=ALU.subtract),
              inc=VD)
        S.add("vector", lambda e: e.tensor_scalar(out=st_var[:], in0=st_var[:],
                                                  scalar1=0.0, scalar2=1e-5,
                                                  op0=ALU.max, op1=ALU.add), inc=VD)
        S.add("vector", lambda e: e.reciprocal(out=st_r[:], in_=st_var[:]),
              waits=[(VD, VD.n)], inc=VD)
        S.add("scalar", lambda e: e.activation(out=st_r[:], in_=st_r[:],
                                               func=AF.Sqrt),
              waits=[(VD, VD.n)], inc=SD)
        rdone = (SD, SD.n)
        return rdone, psstat_free

    def emit_rstd_bcast(par, rdone):
        rb = t_rbc[:, par * TT:(par + 1) * TT]
        mb = t_mbc[:, par * TT:(par + 1) * TT]
        S.add("tensor", lambda e: e.matmul(ps_bc[:], t_ones_row[:], st_r[:],
                                           start=True, stop=True),
              waits=[rdone, (SD, SD.n)], inc=TM)
        S.add("scalar", lambda e, rb=rb: e.copy(out=rb, in_=ps_bc[:]),
              waits=[(TM, TM.n), (VD, VD.n)], inc=SD)
        S.add("tensor", lambda e: e.matmul(ps_bc[:], t_ones_row[:], st_mean[:],
                                           start=True, stop=True),
              waits=[(SD, SD.n)], inc=TM)
        S.add("scalar", lambda e, mb=mb: e.copy(out=mb, in_=ps_bc[:]),
              waits=[(TM, TM.n)], inc=SD)
        return (SD, SD.n)

    def emit_rstd(par, entry_waits=()):
        rdone, psfree = emit_rstd_chain(par, entry_waits)
        return emit_rstd_bcast(par, rdone), psfree

    def emit_stats_block(buf, base, mm_waits=(), first_mm_waits=(),
                         sq_tm_wait=0):
        S.add("scalar",
              lambda e, buf=buf, base=base: e.activation(
                  out=t_xsq[:], in_=buf[:, base:base + NC_D * TT],
                  func=AF.Square),
              waits=list(mm_waits) + [(TM, sq_tm_wait)], inc=SD)
        sq = (SD, SD.n)
        for c in range(NC_D):
            S.add("tensor",
                  lambda e, buf=buf, base=base, c=c: e.matmul(
                      ps_stat[0:1, 0:TT], t_ones_col[:],
                      buf[:, base + c * TT:base + (c + 1) * TT],
                      start=(c == 0), stop=(c == NC_D - 1)),
                  waits=((list(mm_waits) + list(first_mm_waits))
                         if c == 0 else []),
                  inc=None)
        for c in range(NC_D):
            S.add("tensor",
                  lambda e, c=c: e.matmul(
                      ps_stat[0:1, TT:2 * TT], t_ones_col[:],
                      t_xsq[:, c * TT:(c + 1) * TT],
                      start=(c == 0), stop=(c == NC_D - 1)),
                  waits=[sq] if c == 0 else [],
                  inc=TM if c == NC_D - 1 else None)

    a1_es = ExitStack()

    def a1b(name, shape, dt):
        return a1_es.enter_context(nc.sbuf_tensor(name, shape, dt))

    t_xbf = a1b("t_xbf", [128, 2 * NC_D * TT], BF16)
    t_xm1 = a1b("t_xm1", [128, 4 * TT], BF16)
    t_sr1 = a1b("t_sr1", [128, 4 * TT], BF16)

    a1_xld = {}
    a1_stats_mm = {}
    a1_rbc = {}
    a1_psfree = {-1: 0}
    a1_xn_v = {-1: 0, -2: 0}
    a1_xr_tm = {-1: 0, -2: 0}

    def a1_loads(i):
        par = i % 2
        base = par * NC_D * TT
        for c in range(NC_D):
            dma("gpsimd", GP, t_xbf[:, base + c * TT:base + (c + 1) * TT],
                x_fm[c * 128:(c + 1) * 128, i * TT:(i + 1) * TT],
                waits=((wl + [(TM, a1_xr_tm[i - 2]), (VD, a1_xn_v[i - 2])])
                       if c == 0 else ()))
        a1_xld[i] = (GP, GP.n)

    def a1_stats(i):
        par = i % 2
        base = par * NC_D * TT
        emit_stats_block(t_xbf, base, mm_waits=[a1_xld[i]],
                         first_mm_waits=[(VD, a1_psfree[i - 1])],
                         sq_tm_wait=a1_stats_mm.get(i - 1, 0))
        a1_stats_mm[i] = TM.n

    def a1_loads_stats(i):
        a1_loads(i)
        a1_stats(i)

    a1_loads_stats(0)
    a1_rbc[0], a1_psfree[0] = emit_rstd(0, entry_waits=[(TM, a1_stats_mm[0])])

    def emit_xn(i):
        par = i % 2
        xnb = par * NC_D * TT
        mbc3 = t_mbc[:, par * TT:(par + 1) * TT].unsqueeze(1).broadcast_to(
            [128, NC_D, TT])
        rbc3 = t_rbc[:, par * TT:(par + 1) * TT].unsqueeze(1).broadcast_to(
            [128, NC_D, TT])
        xb3 = t_xbf[:, xnb:xnb + NC_D * TT].rearrange("p (c t) -> p c t", t=TT)
        S.add("vector", lambda e, a=xb3, m=mbc3: e.tensor_tensor(
            out=a, in0=a, in1=m, op=ALU.subtract),
            waits=[a1_rbc[i], (TM, a1_xr_tm[i - 2])], inc=VD)
        S.add("vector", lambda e, a=xb3, r=rbc3: e.tensor_tensor(
            out=a, in0=a, in1=r, op=ALU.mult), inc=VD)
        a1_xn_v[i] = VD.n
        return (VD, VD.n)

    xn_hoist = {}
    xm_scq = {}
    sr_scq = {}
    for i in range(NT):
        par = i % 2
        xnb = par * NC_D * TT
        if i + 1 < NT:
            a1_loads(i + 1)
        xn_done = xn_hoist.pop(i, None)
        if xn_done is None:
            xn_done = emit_xn(i)
        evac_at = {}
        for j in range(2 * NC_I):
            pbank = ps_a if j % 2 == 0 else ps_b
            bank_wait = ([(SD, evac_at[j - 2])] if j >= 2
                         else [(SD, SD.n)] if j == 0 else [])
            for k in range(NC_D):
                S.add("tensor",
                      lambda e, j=j, k=k, pb=pbank, xnb=xnb: e.matmul(
                          pb[:],
                          t_win[:, k * 4096 + j * 128:k * 4096 + (j + 1) * 128],
                          t_xbf[:, xnb + k * TT:xnb + (k + 1) * TT],
                          start=(k == 0), stop=(k == NC_D - 1)),
                      waits=([xn_done] + bank_wait if k == 0 else []),
                      inc=TM if k == NC_D - 1 else None)
            if j < NC_I:
                sl = (i * NC_I + j) % 4
                S.add("scalar",
                      lambda e, sl=sl, pb=pbank: e.copy(
                          out=t_xm1[:, sl * TT:(sl + 1) * TT], in_=pb[:]),
                      waits=[(TM, TM.n), (SC, xm_scq.get(sl, 0))],
                      inc=SD)
                evac_at[j] = SD.n
                dma("scalar", SC, s_xm[j * 128:(j + 1) * 128, i * TT:(i + 1) * TT],
                    t_xm1[:, sl * TT:(sl + 1) * TT],
                    waits=[(SD, SD.n)])
                xm_scq[sl] = SC.n
            else:
                jj = j - NC_I
                sl = (i * NC_I + jj) % 4
                S.add("scalar",
                      lambda e, sl=sl, pb=pbank: e.activation(
                          out=t_sr1[:, sl * TT:(sl + 1) * TT], in_=pb[:],
                          func=AF.Silu),
                      waits=[(TM, TM.n), (SC, sr_scq.get(sl, 0))], inc=SD)
                evac_at[j] = SD.n
                dma("scalar", SC,
                    s_sres[jj * 128:(jj + 1) * 128, i * TT:(i + 1) * TT],
                    t_sr1[:, sl * TT:(sl + 1) * TT], waits=[(SD, SD.n)])
                sr_scq[sl] = SC.n
            if j == 4 and i + 1 < NT:
                a1_stats(i + 1)
            if j == 6 and i + 1 < NT:
                a1_rdone, a1_psfree[i + 1] = emit_rstd_chain(
                    (i + 1) % 2, entry_waits=[(TM, a1_stats_mm[i + 1])])
            if j == 14 and i + 1 < NT:
                a1_rbc[i + 1] = (SD, 0)
                a1_rbc[i + 1] = (SD,
                                 emit_rstd_bcast((i + 1) % 2, a1_rdone)[1])
            if j == 15 and i + 1 < NT:
                xn_hoist[i + 1] = emit_xn(i + 1)
            if j in (16, 22, 28):
                pump(3)
        a1_xr_tm[i] = TM.n

    pump_all()
    dq3 = (SC, SC.n)
    a1_es.close()
    dq_es.close()
    win_es.close()

    a2_es = ExitStack()

    def a2b(name, shape, dt):
        return a2_es.enter_context(nc.sbuf_tensor(name, shape, dt))

    t_xmh = a2b("t_xmh", [128, 2 * NC_I * (TT + 2)], BF16)
    t_sres = a2b("t_sres", [128, 2 * NC_I * TT], BF16)
    t_xc = a2b("t_xc", [128, 2 * NC_I * TT], BF16)
    t_bt = a2b("t_bt", [16, TT], F32)
    t_ct = a2b("t_ct", [16, TT], F32)
    t_states = a2b("t_states", [16, TT], F32)
    t_carry = a2b("t_carry", [16, 1], F32)
    t_prod = a2b("t_prod", [16, TT], F32)
    t_ys = a2b("t_ys", [1, TT], F32)
    t_ysbc = a2b("t_ysbc", [128, 2 * TT], BF16)
    t_dbf = a2b("t_dbf", [128, NC_I], BF16)
    t_xfp = a2b("t_xfp", [128, NC_D * TT], F32)
    t_x2c = a2b("t_x2c", [128, 2 * TT], F32)
    t_x2bf = a2b("t_x2bf", [128, 2 * NC_D * TT], BF16)
    t_xn2c = a2b("t_xn2c", [128, 2 * TT], BF16)

    ysmm_sd = {-1: 0}
    x2st = {}
    xn2st = {}
    cvv = {-1: 0, -2: 0}
    cvg_rd = {-1: 0, -2: 0}
    cvs = {-1: 0, -2: 0}
    bcmm = {-1: 0, -2: 0}
    ychv = {-1: 0, -2: 0}
    st2mm = {-1: a1_stats_mm[NT - 1], -2: 0}
    xn2v = {-1: 0, -2: 0}
    nrm2_prev = {-1: ((SD, 0), a1_psfree[NT - 1]),
                 -2: ((SD, 0), 0), -3: ((SD, 0), 0)}
    a2_xml = {}
    a2_xmh = {}
    a2_xbf_sd = {}
    prodv = {-1: 0}
    ysmm = {-1: 0}
    wout_tm = {-1: 0, -2: 0}
    evac_sd = {0: 0, 1: 0}

    S.add("vector", lambda e: e.memset(t_carry[:], 0.0), waits=snap(), inc=VD)
    S.add("vector", lambda e: e.memset(t_amax[:], 0.0), inc=VD)
    S.add("vector", lambda e: e.tensor_copy(out=t_dbf[:], in_=t_d[:]), inc=VD)
    a2_start = snap()

    def a2_loads(i):
        par = i % 2
        hb = par * NC_I * (TT + 2)
        sb = par * NC_I * TT
        t0 = i * TT
        lo = max(t0 - 1, 0)
        hi = min(t0 + TT + 1, SEQ)
        for c in range(NC_I):
            off = hb + c * (TT + 2) + (1 if t0 == 0 else 0)
            dma("sync", SY, t_xmh[:, off:off + (hi - lo)],
                s_xm[c * 128:(c + 1) * 128, lo:hi],
                waits=((a2_start if i <= 1 else
                        [(VD, cvv[i - 2]), (SD, cvs[i - 2])]) if c == 0 else ()))
            if t0 == 0:
                S.add("vector", lambda e, c=c, hb=hb: e.memset(
                    t_xmh[:, hb + c * (TT + 2):hb + c * (TT + 2) + 1], 0.0),
                    inc=VD)
            if t0 + TT == SEQ:
                S.add("vector", lambda e, c=c, hb=hb: e.memset(
                    t_xmh[:, hb + c * (TT + 2) + TT + 1:
                          hb + c * (TT + 2) + TT + 2], 0.0),
                    inc=VD)
        for c in range(NC_I):
            dma("gpsimd", VE, t_sres[:, sb + c * TT:sb + (c + 1) * TT],
                s_sres[c * 128:(c + 1) * 128, t0:t0 + TT],
                waits=((a2_start if i <= 1 else [(VD, ychv[i - 2])])
                       if c == 0 else ()))
        a2_xml[i] = [(SY, SY.n), (VE, VE.n), (VD, VD.n)]
        a2_xmh[i] = [(SY, SY.n), (VD, VD.n)]

    silu_sd = {}

    def a2_conv1(i, c0, c1):
        par = i % 2
        hb = par * NC_I * (TT + 2)
        xcb = par * NC_I * TT
        for c in range(c0, c1):
            base = hb + c * (TT + 2)
            S.add("scalar",
                  lambda e, c=c, base=base, xcb=xcb: e.activation(
                      out=t_xc[:, xcb + c * TT:xcb + (c + 1) * TT],
                      in_=t_xmh[:, base + 1:base + 1 + TT], func=AF.Identity,
                      bias=t_conv[:, c * 4 + 3:c * 4 + 4],
                      scale=t_conv[:, c * 4 + 1:c * 4 + 2]),
                  waits=(a2_xmh[i] + [(TM, wout_tm[i - 2]),
                                      (VD, ychv[i - 2])]) if c == c0 else [],
                  inc=SD)
        cvs[i] = SD.n

    def a2_taps(i, c0, c1):
        par = i % 2
        hb = par * NC_I * (TT + 2)
        xcb = par * NC_I * TT
        sconv = (SD, cvs[i])
        for c in range(c0, c1):
            base = hb + c * (TT + 2)
            S.add("vector",
                  lambda e, c=c, base=base, xcb=xcb: e.scalar_tensor_tensor(
                      out=t_xc[:, xcb + c * TT:xcb + (c + 1) * TT],
                      in0=t_xmh[:, base:base + TT],
                      scalar=t_conv[:, c * 4 + 0:c * 4 + 1],
                      in1=t_xc[:, xcb + c * TT:xcb + (c + 1) * TT],
                      op0=ALU.mult, op1=ALU.add),
                  waits=[sconv] if c == c0 else [], inc=VD)
            S.add("vector",
                  lambda e, c=c, base=base, xcb=xcb: e.scalar_tensor_tensor(
                      out=t_xc[:, xcb + c * TT:xcb + (c + 1) * TT],
                      in0=t_xmh[:, base + 2:base + 2 + TT],
                      scalar=t_conv[:, c * 4 + 2:c * 4 + 3],
                      in1=t_xc[:, xcb + c * TT:xcb + (c + 1) * TT],
                      op0=ALU.mult, op1=ALU.add),
                  inc=VD)
        cvv[i] = VD.n

    def a2_silu(i, c0, c1):
        par = i % 2
        xcb = par * NC_I * TT
        S.add("scalar",
              lambda e, xcb=xcb, c0=c0, c1=c1: e.activation(
                  out=t_xc[:, xcb + c0 * TT:xcb + c1 * TT],
                  in_=t_xc[:, xcb + c0 * TT:xcb + c1 * TT], func=AF.Silu),
              waits=[(VD, cvv[i])], inc=SD)
        silu_sd[i] = SD.n

    def a2_chain(i):
        par = i % 2
        xcb = par * NC_I * TT
        silu_done = (SD, silu_sd[i])
        for k in range(NC_I):
            S.add("tensor",
                  lambda e, k=k, xcb=xcb: e.matmul(
                      ps_sm[0:16, :], t_wbc[:, k * 32:k * 32 + 16],
                      t_xc[:, xcb + k * TT:xcb + (k + 1) * TT],
                      start=(k == 0), stop=(k == NC_I - 1)),
                  waits=([silu_done, (SD, ysmm_sd[i - 1])]
                         if k == 0 else []),
                  inc=None)
            S.add("tensor",
                  lambda e, k=k, xcb=xcb: e.matmul(
                      ps_bc[0:16, :], t_wbc[:, k * 32 + 16:k * 32 + 32],
                      t_xc[:, xcb + k * TT:xcb + (k + 1) * TT],
                      start=(k == 0), stop=(k == NC_I - 1)),
                  waits=[(SD, nrm2_prev[i - 3][0][1])] if k == 0 else [],
                  inc=TM if k == NC_I - 1 else None)
        bcmm[i] = TM.n
        S.add("scalar", lambda e: e.copy(out=t_bt[:], in_=ps_sm[0:16, :]),
              waits=[(TM, TM.n), (VD, prodv[i - 1])], inc=SD)
        S.add("scalar", lambda e: e.copy(out=t_ct[:], in_=ps_bc[0:16, :]),
              inc=SD)
        S.add("vector", lambda e: e.tensor_tensor_scan(
            out=t_states[:], data0=t_decay[:], data1=t_bt[:],
            initial=t_carry[:], op0=ALU.mult, op1=ALU.add),
            waits=[(SD, SD.n)], inc=VD)
        S.add("vector", lambda e: e.tensor_copy(out=t_carry[:],
                                                in_=t_states[:, TT - 1:TT]),
              waits=[(VD, VD.n)], inc=VD)
        S.add("vector", lambda e: e.tensor_tensor(out=t_prod[:], in0=t_states[:],
                                                  in1=t_ct[:], op=ALU.mult),
              waits=[(VD, VD.n), (TM, ysmm[i - 1])], inc=VD)
        prodv[i] = VD.n
        S.add("tensor", lambda e: e.matmul(ps_sm[0:1, :], t_ones16[:], t_prod[:],
                                           start=True, stop=True),
              waits=[(VD, VD.n)], inc=TM)
        ysmm[i] = TM.n
        S.add("scalar", lambda e: e.copy(out=t_ys[:], in_=ps_sm[0:1, :]),
              waits=[(TM, TM.n)], inc=SD)
        S.add("tensor", lambda e: e.matmul(ps_bc[:], t_ones_row[:], t_ys[:],
                                           start=True, stop=True),
              waits=[(SD, SD.n)], inc=TM)
        S.add("scalar",
              lambda e, par=par: e.copy(
                  out=t_ysbc[:, par * TT:(par + 1) * TT], in_=ps_bc[:]),
              waits=[(TM, TM.n), (VD, ychv[i - 1])], inc=SD)
        ysmm_sd[i] = SD.n

    def a2_y(i):
        par = i % 2
        xcb = par * NC_I * TT
        xc3 = t_xc[:, xcb:xcb + NC_I * TT].rearrange("p (c t) -> p c t", t=TT)
        dbc3 = t_dbf[:, 0:NC_I].unsqueeze(2).broadcast_to([128, NC_I, TT])
        ysbc3 = t_ysbc[:, par * TT:(par + 1) * TT].unsqueeze(1).broadcast_to(
            [128, NC_I, TT])
        sres3 = t_sres[:, xcb:xcb + NC_I * TT].rearrange("p (c t) -> p c t",
                                                         t=TT)
        S.add("vector", lambda e, a=xc3, d=dbc3: e.tensor_tensor(
            out=a, in0=a, in1=d, op=ALU.mult),
            waits=[(SD, ysmm_sd[i])] + a2_xml[i], inc=VD)
        S.add("vector", lambda e, a=xc3, y=ysbc3: e.tensor_tensor(
            out=a, in0=a, in1=y, op=ALU.add), inc=VD)
        S.add("vector", lambda e, a=xc3, s=sres3: e.tensor_tensor(
            out=a, in0=a, in1=s, op=ALU.mult), inc=VD)
        ychv[i] = VD.n

    a2_rd = {}
    a2_psf = {}

    def a2_stats(i):
        par = i % 2
        xbb = par * NC_D * TT
        emit_stats_block(t_x2bf, xbb, mm_waits=[(SD, a2_xbf_sd[i])],
                         first_mm_waits=[(VD, nrm2_prev[i - 1][1])],
                         sq_tm_wait=st2mm[i - 1])
        st2mm[i] = TM.n

    def a2_rstd1(i):
        a2_rd[i], a2_psf[i] = emit_rstd_chain(
            i % 2, entry_waits=[(TM, st2mm[i])])

    def a2_rstd2(i):
        nrm2_prev[i] = ((SD, emit_rstd_bcast(i % 2, a2_rd[i])[1]), a2_psf[i])

    def a2_xn2(i):
        par = i % 2
        xbb = par * NC_D * TT
        t0 = i * TT
        m3 = t_mbc[:, par * TT:(par + 1) * TT].unsqueeze(1).broadcast_to(
            [128, NC_D, TT])
        r3 = t_rbc[:, par * TT:(par + 1) * TT].unsqueeze(1).broadcast_to(
            [128, NC_D, TT])
        xf3 = t_x2bf[:, xbb:xbb + NC_D * TT].rearrange("p (c t) -> p c t",
                                                       t=TT)
        S.add("vector",
              lambda e, a=xf3, m=m3: e.tensor_tensor(
                  out=a, in0=a, in1=m, op=ALU.subtract),
              waits=[nrm2_prev[i][0]], inc=VD)
        S.add("vector",
              lambda e, a=xf3, r=r3: e.tensor_tensor(
                  out=a, in0=a, in1=r, op=ALU.mult), inc=VD)
        xn2done = (VD, VD.n)
        for c in range(NC_D):
            dma("gpsimd", VE, s_x2n[c * 128:(c + 1) * 128, t0:t0 + TT],
                t_x2bf[:, xbb + c * TT:xbb + (c + 1) * TT],
                waits=[xn2done] if c == 0 else ())
        xn2st[par] = VE.n
        S.add("vector",
              lambda e, xbb=xbb: e.tensor_reduce(
                  out=t_red[:, 0:1], in_=t_x2bf[:, xbb:xbb + NC_D * TT],
                  axis=AX, op=ALU.max, apply_absolute_value=True),
              inc=VD)
        S.add("vector",
              lambda e: e.tensor_tensor(out=t_amax[:], in0=t_amax[:],
                                        in1=t_red[:, 0:1], op=ALU.max),
              waits=[(VD, VD.n)], inc=VD)
        xn2v[i] = VD.n

    a2_loads(0)
    a2_conv1(0, 0, NC_I)
    a2_taps(0, 0, NC_I)
    a2_silu(0, 0, NC_I)
    a2_chain(0)
    a2_y(0)

    for i in range(NT):
        par = i % 2
        xcb = par * NC_I * TT
        xbb = par * NC_D * TT
        t0 = i * TT
        if i + 1 < NT:
            a2_loads(i + 1)
        for c in range(NC_D):
            dma("sync", SY, t_xfp[:, c * TT:(c + 1) * TT],
                x_fm[c * 128:(c + 1) * 128, t0:t0 + TT],
                waits=[(TM, wout_tm[i - 1])] if c == 0 else ())
        xfp_ld = (SY, SY.n)

        psj = [ps_a, ps_b]
        for jj in range(NC_D):
            pb = psj[jj % 2]
            sl = jj % 2
            S.add("tensor",
                  lambda e, pb=pb, jj=jj: e.matmul(
                      pb[:], t_identf[:], t_xfp[:, jj * TT:(jj + 1) * TT],
                      start=True, stop=False),
                  waits=[xfp_ld, (SD, evac_sd.get(sl, 0))]
                        + ([(VD, ychv[i])] if jj == 0 else []),
                  inc=None)
            for k in range(NC_I):
                S.add("tensor",
                      lambda e, k=k, jj=jj, pb=pb, xcb=xcb: e.matmul(
                          pb[:],
                          t_wout[:, k * 1024 + jj * 128:
                                 k * 1024 + (jj + 1) * 128],
                          t_xc[:, xcb + k * TT:xcb + (k + 1) * TT],
                          start=False, stop=(k == NC_I - 1)),
                      inc=TM if k == NC_I - 1 else None)
            mm_done = (TM, TM.n)
            S.add("scalar",
                  lambda e, sl=sl, pb=pb: e.copy(
                      out=t_x2c[:, sl * TT:(sl + 1) * TT], in_=pb[:]),
                  waits=[mm_done, (VE, x2st.get(sl, 0))], inc=SD)
            dma("gpsimd", VE, s_x2[jj * 128:(jj + 1) * 128, t0:t0 + TT],
                t_x2c[:, sl * TT:(sl + 1) * TT], waits=[(SD, SD.n)])
            x2st[sl] = VE.n
            S.add("scalar",
                  lambda e, jj=jj, pb=pb, xbb=xbb: e.copy(
                      out=t_x2bf[:, xbb + jj * TT:xbb + (jj + 1) * TT],
                      in_=pb[:]),
                  waits=([(TM, st2mm[i - 2]), (VD, xn2v[i - 2]),
                          (VE, xn2st.get(par, 0))] if jj == 0 else []),
                  inc=SD)
            evac_sd[sl] = SD.n
            if jj == NC_D - 1:
                a2_xbf_sd[i] = SD.n
            if i + 1 < NT:
                if jj == 0:
                    a2_conv1(i + 1, 0, 8)
                elif jj == 1:
                    a2_conv1(i + 1, 8, NC_I)
                    a2_taps(i + 1, 0, 8)
                elif jj == 2:
                    a2_taps(i + 1, 8, NC_I)
                elif jj == 3:
                    a2_silu(i + 1, 0, 8)
                elif jj == 4:
                    a2_silu(i + 1, 8, NC_I)
                    a2_chain(i + 1)
                elif jj == 6:
                    a2_y(i + 1)
            if i >= 1:
                if jj == 2:
                    a2_stats(i - 1)
                elif jj == 3:
                    a2_rstd1(i - 1)
                elif jj == 5:
                    a2_rstd2(i - 1)
                elif jj == 7:
                    a2_xn2(i - 1)
        wout_tm[i] = TM.n

    a2_stats(NT - 1)
    a2_rstd1(NT - 1)
    a2_rstd2(NT - 1)
    a2_xn2(NT - 1)

    def part_max(src_col_ap, dst_col):
        S.add("tensor", lambda e: e.transpose(ps_sm[0:1, 0:128], src_col_ap,
                                              t_identf[:]),
              waits=[(VD, VD.n)], inc=TM)
        S.add("vector",
              lambda e, d=dst_col: e.tensor_reduce(out=t_gmax[:, d:d + 1],
                                                   in_=ps_sm[0:1, 0:128],
                                                   axis=AX, op=ALU.max),
              waits=[(TM, TM.n)], inc=VD)

    def do_allreduce(col0, col1):
        S.add("vector", lambda e, c=col0: e.tensor_copy(out=t_ccin[:, 0:1],
                                                        in_=t_gmax[:, c:c + 1]),
              waits=[(VD, VD.n), (GD, GD.n)], inc=VD)
        S.add("vector", lambda e, c=col1: e.tensor_copy(out=t_ccin[:, 1:2],
                                                        in_=t_gmax[:, c:c + 1]),
              inc=VD)
        dma("gpsimd", GP, cc_in, t_ccin[:], waits=[(VD, VD.n)])
        S.add("gpsimd",
              lambda e: e.collective_compute(
                  "AllReduce", ALU.max, replica_groups=[list(range(NCORES))],
                  ins=[cc_in.opt()], outs=[cc_out.opt()]),
              waits=[(GP, GP.n)], inc=GD)
        dma("gpsimd", GP, t_ccout[:], cc_out, waits=[(GD, GD.n)])
        return (GP, GP.n)

    def bcast_scalars():
        S.add("tensor", lambda e: e.matmul(ps_bc[:, 0:8], t_ones_row[:], t_sc1[:],
                                           start=True, stop=True),
              waits=[(VD, VD.n), (SD, SD.n)], inc=TM)
        S.add("scalar", lambda e: e.copy(out=t_scpp[:], in_=ps_bc[:, 0:8]),
              waits=[(TM, TM.n)], inc=SD)
        return (SD, SD.n)

    def vop(emit, waits=()):
        S.add("vector", emit, waits=waits, inc=VD)

    part_max(t_amax[:], 0)
    b0 = do_allreduce(0, 0)
    vop(lambda e: e.reciprocal(out=t_sc1[:, 0:1], in_=t_ccout[:, 0:1]),
        waits=[b0, (SD, SD.n)])
    vop(lambda e: e.tensor_scalar(out=t_sc1[:, 0:1], in0=t_sc1[:, 0:1],
                                  scalar1=QMAX, scalar2=None, op0=ALU.mult),
        waits=[(VD, VD.n)])
    vop(lambda e: e.tensor_scalar(out=t_sc1[:, 1:2], in0=t_ccout[:, 0:1],
                                  scalar1=1.0 / QMAX, scalar2=None, op0=ALU.mult))
    bar0 = bcast_scalars()

    a2_es.close()
    ash_es.close()
    mx_es.close()

    c_es = ExitStack()

    def csb(name, shape, dt):
        return c_es.enter_context(nc.sbuf_tensor(name, shape, dt))

    t_wT1 = csb("t_wT1", [128, NC_D * D_FFN], BF16)
    t_wT2 = csb("t_wT2", [128, NC_D * D_FFN], BF16)
    t_xn2t = csb("t_xn2t", [128, NC_D * TT], BF16)
    t_tbc = csb("t_tbc", [128, NC_D * TT], F32)
    t_qx = csb("t_qx", [128, 2 * NC_D * TT], BF16)
    t_upsb = csb("t_upsb", [128, NC_F * TT], BF16)
    t_gatesb = csb("t_gatesb", [128, NC_F * TT], BF16)

    c_start = snap()
    for k in range(NC_D):
        dma("scalar", SC, t_wT1[:, k * D_FFN:(k + 1) * D_FFN],
            s_wupT[k * 128:(k + 1) * 128, :],
            waits=([dq3] + c_start) if k == 0 else ())
        dma("scalar", SC, t_wT2[:, k * D_FFN:(k + 1) * D_FFN],
            s_wgateT[k * 128:(k + 1) * 128, :])
    wTl = (SC, SC.n)
    vop(lambda e: e.memset(t_amax[:], 0.0), waits=[bar0])
    vop(lambda e: e.memset(t_amax2[:], 0.0))

    cq_ld = {}
    cq_done = {}
    cq_mm = {-1: 0, -2: 0}

    def c_prep(i):
        par = i % 2
        t0 = i * TT
        for c in range(NC_D):
            dma("sync", SY, t_xn2t[:, c * TT:(c + 1) * TT],
                s_x2n[c * 128:(c + 1) * 128, t0:t0 + TT],
                waits=([(SD, SD.n), (VD, VD.n), (VE, VE.n)] if c == 0 else ()))
        cq_ld[i] = (SY, SY.n)
        S.add("scalar", lambda e: e.activation(out=t_tbc[:], in_=t_xn2t[:],
                                               func=AF.Identity,
                                               bias=t_consts[:, 0:1],
                                               scale=t_scpp[:, 0:1]),
              waits=[cq_ld[i], bar0], inc=SD)
        S.add("vector",
              lambda e, par=par: e.tensor_scalar(
                  out=t_qx[:, par * NC_D * TT:(par + 1) * NC_D * TT],
                  in0=t_tbc[:], scalar1=-C32, scalar2=None, op0=ALU.add),
              waits=[(SD, SD.n), (TM, cq_mm[i - 2])], inc=VD)
        cq_done[i] = (VD, VD.n)

    c_prep(0)
    evu = {m: 0 for m in range(-2, 0)}
    evg = {m: 0 for m in range(-2, 0)}
    for i in range(NT):
        par = i % 2
        qxb = par * NC_D * TT
        t0 = i * TT
        qx_done = cq_done[i]
        if i > 0:
            evu = {-2: evu[NC_F - 2], -1: evu[NC_F - 1]}
            evg = {-2: evg[NC_F - 2], -1: evg[NC_F - 1]}
        for m in range(NC_F):
            pu = ps_a if m % 2 == 0 else ps_b
            pg = ps_c if m % 2 == 0 else ps_bc
            uw = [(SD, evu[m - 2])]
            gw = [(VD, evg[m - 2])]
            for k in range(NC_D):
                S.add("tensor",
                      lambda e, m=m, k=k, pu=pu, qxb=qxb: e.matmul(
                          pu[:],
                          t_wT1[:, k * D_FFN + m * 128:k * D_FFN + (m + 1) * 128],
                          t_qx[:, qxb + k * TT:qxb + (k + 1) * TT],
                          start=(k == 0), stop=(k == NC_D - 1)),
                      waits=([qx_done, wTl] + uw if k == 0 else []),
                      inc=TM if k == NC_D - 1 else None)
            S.add("scalar",
                  lambda e, m=m, pu=pu: e.copy(out=t_upsb[:, m * TT:(m + 1) * TT],
                                               in_=pu[:]),
                  waits=[(TM, TM.n)] + ([(SC, SC.n)] if m == 0 else []), inc=SD)
            evu[m] = SD.n
            for k in range(NC_D):
                S.add("tensor",
                      lambda e, m=m, k=k, pg=pg, qxb=qxb: e.matmul(
                          pg[:],
                          t_wT2[:, k * D_FFN + m * 128:k * D_FFN + (m + 1) * 128],
                          t_qx[:, qxb + k * TT:qxb + (k + 1) * TT],
                          start=(k == 0), stop=(k == NC_D - 1)),
                      waits=gw if k == 0 else [],
                      inc=TM if k == NC_D - 1 else None)
            S.add("vector",
                  lambda e, m=m, pg=pg: e.tensor_copy(
                      out=t_gatesb[:, m * TT:(m + 1) * TT], in_=pg[:]),
                  waits=[(TM, TM.n)] + ([(VE, VE.n)] if m == 0 else []), inc=VD)
            evg[m] = VD.n
            if m % 3 == 2:
                lo = (m - 2) * TT
                hi = (m + 1) * TT
                vop(lambda e, lo=lo, hi=hi: e.tensor_reduce(
                    out=t_red[:, 0:1], in_=t_upsb[:, lo:hi], axis=AX,
                    op=ALU.max, apply_absolute_value=True),
                    waits=[(SD, evu[m])])
                vop(lambda e: e.tensor_tensor(out=t_amax[:], in0=t_amax[:],
                                              in1=t_red[:, 0:1], op=ALU.max),
                    waits=[(VD, VD.n)])
                vop(lambda e, lo=lo, hi=hi: e.tensor_reduce(
                    out=t_red[:, 1:2], in_=t_gatesb[:, lo:hi], axis=AX,
                    op=ALU.max, apply_absolute_value=True))
                vop(lambda e: e.tensor_tensor(out=t_amax2[:], in0=t_amax2[:],
                                              in1=t_red[:, 1:2], op=ALU.max),
                    waits=[(VD, VD.n)])
            if m == 15 and i + 1 < NT:
                c_prep(i + 1)
        cq_mm[i] = TM.n
        evac_done = [(SD, SD.n), (VD, VD.n)]
        for m in range(NC_F):
            dma("scalar", SC, s_up[m * 128:(m + 1) * 128, t0:t0 + TT],
                t_upsb[:, m * TT:(m + 1) * TT],
                waits=evac_done if m == 0 else ())
            dma("gpsimd", VE, s_gate[m * 128:(m + 1) * 128, t0:t0 + TT],
                t_gatesb[:, m * TT:(m + 1) * TT],
                waits=evac_done if m == 0 else ())

    part_max(t_amax[:], 1)
    part_max(t_amax2[:], 2)
    b1 = do_allreduce(1, 2)
    vop(lambda e: e.reciprocal(out=t_sc1[:, 2:3], in_=t_ccout[:, 0:1]),
        waits=[b1, (SD, SD.n)])
    vop(lambda e: e.reciprocal(out=t_sc1[:, 3:4], in_=t_ccout[:, 1:2]),
        waits=[(VD, VD.n)])
    vop(lambda e: e.tensor_scalar(out=t_sc1[:, 2:4], in0=t_sc1[:, 2:4],
                                  scalar1=QMAX, scalar2=None, op0=ALU.mult),
        waits=[(VD, VD.n)])
    vop(lambda e: e.tensor_tensor(out=t_sc1[:, 4:5], in0=t_ccout[:, 1:2],
                                  in1=t_sc1[:, 1:2], op=ALU.mult))
    vop(lambda e: e.tensor_scalar(out=t_sc1[:, 4:5], in0=t_sc1[:, 4:5],
                                  scalar1=1.0 / QMAX, scalar2=None, op0=ALU.mult),
        waits=[(VD, VD.n)])
    vop(lambda e: e.tensor_scalar(out=t_sc1[:, 5:6], in0=t_sc1[:, 4:5],
                                  scalar1=-C32, scalar2=None, op0=ALU.mult),
        waits=[(VD, VD.n)])
    bar1 = bcast_scalars()

    c_es.close()

    d_es = ExitStack()

    def dsb(name, shape, dt):
        return d_es.enter_context(nc.sbuf_tensor(name, shape, dt))

    t_gin = dsb("t_gin", [128, 2 * SUB * TT], BF16)
    t_uin = dsb("t_uin", [128, 2 * SUB * TT], BF16)
    t_tbg = dsb("t_tbg", [128, SUB * TT], F32)
    t_sig = dsb("t_sig", [128, SUB * TT], F32)
    t_tbsi = dsb("t_tbsi", [128, SUB * TT], F32)
    t_sibf = dsb("t_sibf", [128, SUB * TT], BF16)
    t_gqbf = dsb("t_gqbf", [128, SUB * TT], BF16)
    t_pout = dsb("t_pout", [128, 2 * SUB * TT], BF16)
    t_dabs = dsb("t_dabs", [128, 2 * SUB * TT], BF16)

    d_start = snap()
    vop(lambda e: e.memset(t_amax[:], 0.0), waits=[bar1])
    subs = [(i, sub) for i in range(NT) for sub in range(NSUB)]
    d_tb = {-1: 0, -2: 0}
    d_gq = {-1: 0}
    d_pst = {}

    def d1_load(s):
        i, sub = subs[s]
        gb = (s % 2) * SUB * TT
        for m in range(SUB):
            dma("sync", SY, t_gin[:, gb + m * TT:gb + (m + 1) * TT],
                s_gate[(sub * SUB + m) * 128:(sub * SUB + m + 1) * 128,
                       i * TT:(i + 1) * TT],
                waits=(([(SD, d_tb[s - 2])] + (d_start if s == 0 else []))
                       if m == 0 else ()))
        return (SY, SY.n)

    d_gld = {0: d1_load(0)}
    for s in range(len(subs)):
        i, sub = subs[s]
        t0 = i * TT
        m0 = sub * SUB
        gb = (s % 2) * SUB * TT
        pb = (s % 2) * SUB * TT
        if s + 1 < len(subs):
            d_gld[s + 1] = d1_load(s + 1)
        S.add("scalar", lambda e, gb=gb: e.activation(
            out=t_tbg[:], in_=t_gin[:, gb:gb + SUB * TT],
            func=AF.Identity, bias=t_consts[:, 0:1], scale=t_scpp[:, 3:4]),
            waits=[d_gld[s], bar1, (VD, d_gq[s - 1])], inc=SD)
        S.add("scalar", lambda e: e.activation(out=t_sig[:], in_=t_tbg[:],
                                               func=AF.Sigmoid,
                                               bias=t_scpp[:, 5:6],
                                               scale=t_scpp[:, 4:5]),
              inc=SD)
        S.add("scalar", lambda e: e.activation(out=t_tbsi[:], in_=t_sig[:],
                                               func=AF.Identity,
                                               bias=t_consts[:, 0:1],
                                               scale=t_consts[:, 3:4]),
              inc=SD)
        d_tb[s] = SD.n
        sdn = (SD, SD.n)
        vop(lambda e: e.tensor_scalar(out=t_sibf[:], in0=t_tbsi[:],
                                      scalar1=C32 + 65535.0, scalar2=C32,
                                      op0=ALU.min, op1=ALU.subtract),
            waits=[sdn])
        vop(lambda e: e.tensor_scalar(out=t_gqbf[:], in0=t_tbg[:],
                                      scalar1=-C32, scalar2=None, op0=ALU.add))
        d_gq[s] = VD.n
        vop(lambda e, pb=pb: e.tensor_tensor(
            out=t_pout[:, pb:pb + SUB * TT], in0=t_gqbf[:],
            in1=t_sibf[:], op=ALU.mult),
            waits=[(VD, VD.n), (VE, d_pst.get(s % 2, 0))])
        pdn = (VD, VD.n)
        for m in range(SUB):
            dma("gpsimd", VE,
                s_P[(m0 + m) * 128:(m0 + m + 1) * 128, t0:t0 + TT],
                t_pout[:, pb + m * TT:pb + (m + 1) * TT],
                waits=[pdn] if m == 0 else ())
        d_pst[s % 2] = VE.n
        vop(lambda e, pb=pb: e.tensor_reduce(
            out=t_red[:, 0:1], in_=t_pout[:, pb:pb + SUB * TT],
            axis=AX, op=ALU.max, apply_absolute_value=True),
            waits=[pdn])
        vop(lambda e: e.tensor_tensor(out=t_amax[:], in0=t_amax[:],
                                      in1=t_red[:, 0:1], op=ALU.max),
            waits=[(VD, VD.n)])

    part_max(t_amax[:], 3)
    b2 = do_allreduce(3, 3)
    vop(lambda e: e.reciprocal(out=t_sc1[:, 6:7], in_=t_ccout[:, 0:1]),
        waits=[b2, (SD, SD.n)])
    vop(lambda e: e.tensor_scalar(out=t_sc1[:, 6:7], in0=t_sc1[:, 6:7],
                                  scalar1=QMAX, scalar2=None, op0=ALU.mult),
        waits=[(VD, VD.n)])
    vop(lambda e: e.tensor_tensor(out=t_sc1[:, 7:8], in0=t_sc1[:, 4:5],
                                  in1=t_ccout[:, 0:1], op=ALU.mult))
    vop(lambda e: e.tensor_scalar(out=t_sc1[:, 7:8], in0=t_sc1[:, 7:8],
                                  scalar1=1.0 / 65536.0 / QMAX, scalar2=None,
                                  op0=ALU.mult),
        waits=[(VD, VD.n)])
    bar2 = bcast_scalars()

    vop(lambda e: e.memset(t_amax[:], 0.0), waits=[bar2])
    d2_start = [(SD, SD.n), (VE, VE.n), (VD, VD.n)]
    d_tb = {-1: 0, -2: 0}
    d_gq = {-1: 0}
    d_pst = {}

    def d2_load(s):
        i, sub = subs[s]
        gb = (s % 2) * SUB * TT
        for m in range(SUB):
            dma("sync", SY, t_gin[:, gb + m * TT:gb + (m + 1) * TT],
                s_P[(sub * SUB + m) * 128:(sub * SUB + m + 1) * 128,
                    i * TT:(i + 1) * TT],
                waits=(([(SD, d_tb[s - 2])] + (d2_start if s <= 1 else []))
                       if m == 0 else ()))
            dma("scalar", SC, t_uin[:, gb + m * TT:gb + (m + 1) * TT],
                s_up[(sub * SUB + m) * 128:(sub * SUB + m + 1) * 128,
                     i * TT:(i + 1) * TT],
                waits=[(SD, d_tb[s - 2])] if m == 0 else ())
        return [(SY, SY.n), (SC, SC.n)]

    d_gld = {0: d2_load(0)}
    for s in range(len(subs)):
        i, sub = subs[s]
        t0 = i * TT
        m0 = sub * SUB
        gb = (s % 2) * SUB * TT
        pb = (s % 2) * SUB * TT
        if s + 1 < len(subs):
            d_gld[s + 1] = d2_load(s + 1)
        S.add("scalar", lambda e, gb=gb: e.activation(
            out=t_tbg[:], in_=t_gin[:, gb:gb + SUB * TT],
            func=AF.Identity, bias=t_consts[:, 0:1], scale=t_scpp[:, 6:7]),
            waits=d_gld[s] + [bar2, (VD, d_gq[s - 1])], inc=SD)
        S.add("scalar", lambda e, gb=gb: e.activation(
            out=t_tbsi[:], in_=t_uin[:, gb:gb + SUB * TT],
            func=AF.Identity, bias=t_consts[:, 0:1], scale=t_scpp[:, 2:3]),
            inc=SD)
        d_tb[s] = SD.n
        sdn = (SD, SD.n)
        vop(lambda e: e.tensor_scalar(out=t_gqbf[:], in0=t_tbg[:],
                                      scalar1=-C32, scalar2=None, op0=ALU.add),
            waits=[sdn])
        vop(lambda e: e.tensor_scalar(out=t_sibf[:], in0=t_tbsi[:],
                                      scalar1=-C32, scalar2=None, op0=ALU.add))
        d_gq[s] = VD.n
        vop(lambda e, pb=pb: e.tensor_tensor(
            out=t_pout[:, pb:pb + SUB * TT], in0=t_gqbf[:],
            in1=t_sibf[:], op=ALU.mult),
            waits=[(VD, VD.n), (VE, d_pst.get(s % 2, 0))])
        pdn = (VD, VD.n)
        for m in range(SUB):
            dma("gpsimd", VE,
                s_O1[(m0 + m) * 128:(m0 + m + 1) * 128, t0:t0 + TT],
                t_pout[:, pb + m * TT:pb + (m + 1) * TT],
                waits=[pdn] if m == 0 else ())
        d_pst[s % 2] = VE.n
        vop(lambda e, pb=pb: e.tensor_reduce(
            out=t_red[:, 0:1], in_=t_pout[:, pb:pb + SUB * TT],
            axis=AX, op=ALU.max, apply_absolute_value=True),
            waits=[pdn])
        vop(lambda e: e.tensor_tensor(out=t_amax[:], in0=t_amax[:],
                                      in1=t_red[:, 0:1], op=ALU.max),
            waits=[(VD, VD.n)])

    part_max(t_amax[:], 4)
    b3 = do_allreduce(4, 4)
    vop(lambda e: e.reciprocal(out=t_gmax[:, 5:6], in_=t_sc1[:, 2:3]),
        waits=[b3, (SD, SD.n)])
    vop(lambda e: e.tensor_tensor(out=t_gmax[:, 6:7], in0=t_gmax[:, 5:6],
                                  in1=t_sc1[:, 1:2], op=ALU.mult),
        waits=[(VD, VD.n)])
    vop(lambda e: e.reciprocal(out=t_sc1[:, 2:3], in_=t_ccout[:, 0:1]),
        waits=[(VD, VD.n)])
    vop(lambda e: e.tensor_scalar(out=t_sc1[:, 2:3], in0=t_sc1[:, 2:3],
                                  scalar1=QMAX, scalar2=None, op0=ALU.mult),
        waits=[(VD, VD.n)])
    vop(lambda e: e.tensor_tensor(out=t_sc1[:, 3:4], in0=t_sc1[:, 7:8],
                                  in1=t_gmax[:, 6:7], op=ALU.mult))
    vop(lambda e: e.tensor_tensor(out=t_sc1[:, 3:4], in0=t_sc1[:, 3:4],
                                  in1=t_ccout[:, 0:1], op=ALU.mult),
        waits=[(VD, VD.n)])
    vop(lambda e: e.tensor_scalar(out=t_sc1[:, 3:4], in0=t_sc1[:, 3:4],
                                  scalar1=1.0 / QMAX, scalar2=None, op0=ALU.mult),
        waits=[(VD, VD.n)])
    bar3 = bcast_scalars()

    d_es.close()

    e_es = ExitStack()

    def esb(name, shape, dt):
        return e_es.enter_context(nc.sbuf_tensor(name, shape, dt))

    t_wdT = esb("t_wdT", [128, NC_F * 1024], BF16)
    t_o1 = esb("t_o1", [128, NC_F * TT], BF16)
    t_tbe = esb("t_tbe", [128, NC_F * TT], F32)
    t_qo = esb("t_qo", [128, 2 * NC_F * TT], BF16)
    t_xe = esb("t_xe", [128, 2 * TT], F32)
    t_oe = esb("t_oe", [128, 2 * TT], F32)

    e_start = snap()
    for k in range(NC_F):
        dma("scalar", SC, t_wdT[:, k * 1024:(k + 1) * 1024],
            s_wdownT[k * 128:(k + 1) * 128, :],
            waits=([bar3] + e_start) if k == 0 else ())
    wdl = (SC, SC.n)
    eq_ld = {}
    eq_done = {}
    eq_mm = {-1: 0, -2: 0}

    def e_prep(i):
        par = i % 2
        t0 = i * TT
        for m in range(NC_F):
            dma("sync", SY, t_o1[:, m * TT:(m + 1) * TT],
                s_O1[m * 128:(m + 1) * 128, t0:t0 + TT],
                waits=(([(SD, SD.n), (VE, VE.n)]
                        + (e_start if i == 0 else []))
                       if m == 0 else ()))
        eq_ld[i] = (SY, SY.n)
        S.add("scalar", lambda e: e.activation(out=t_tbe[:], in_=t_o1[:],
                                               func=AF.Identity,
                                               bias=t_consts[:, 0:1],
                                               scale=t_scpp[:, 2:3]),
              waits=[eq_ld[i], bar3], inc=SD)
        S.add("vector",
              lambda e, par=par: e.tensor_scalar(
                  out=t_qo[:, par * NC_F * TT:(par + 1) * NC_F * TT],
                  in0=t_tbe[:], scalar1=-C32, scalar2=None, op0=ALU.add),
              waits=[(SD, SD.n), (TM, eq_mm[i - 2])], inc=VD)
        eq_done[i] = (VD, VD.n)

    e_prep(0)
    eve = {m: 0 for m in range(-2, 0)}
    sve = {m: 0 for m in range(-2, 0)}
    for i in range(NT):
        par = i % 2
        qob = par * NC_F * TT
        t0 = i * TT
        qo_done = eq_done[i]
        if i > 0:
            eve = {-2: eve[NC_D - 2], -1: eve[NC_D - 1]}
            sve = {-2: sve[NC_D - 2], -1: sve[NC_D - 1]}
        for j in range(NC_D):
            pb = ps_a if j % 2 == 0 else ps_b
            jw = [(VD, eve[j - 2])]
            for k in range(NC_F):
                S.add("tensor",
                      lambda e, j=j, k=k, pb=pb, qob=qob: e.matmul(
                          pb[:], t_wdT[:, k * 1024 + j * 128:k * 1024 + (j + 1) * 128],
                          t_qo[:, qob + k * TT:qob + (k + 1) * TT],
                          start=(k == 0), stop=(k == NC_F - 1)),
                      waits=([qo_done, wdl] + jw if k == 0 else []),
                      inc=TM if k == NC_F - 1 else None)
            dma("sync", SY, t_xe[:, (j % 2) * TT:(j % 2 + 1) * TT],
                s_x2[j * 128:(j + 1) * 128, t0:t0 + TT],
                waits=jw)
            stw = [(VE, sve[j - 2])]
            S.add("vector",
                  lambda e, pb=pb, j=j: e.scalar_tensor_tensor(
                      out=t_oe[:, (j % 2) * TT:(j % 2 + 1) * TT], in0=pb[:],
                      scalar=t_scpp[:, 3:4],
                      in1=t_xe[:, (j % 2) * TT:(j % 2 + 1) * TT],
                      op0=ALU.mult, op1=ALU.add),
                  waits=[(TM, TM.n), (SY, SY.n)] + stw, inc=VD)
            eve[j] = VD.n
            dma("gpsimd", VE, y_fm[j * 128:(j + 1) * 128, t0:t0 + TT],
                t_oe[:, (j % 2) * TT:(j % 2 + 1) * TT],
                waits=[(VD, VD.n)])
            sve[j] = VE.n
            if j == 1 and i + 1 < NT:
                e_prep(i + 1)
            if j == NC_D - 1:
                eq_mm[i] = TM.n

    if DBG:
        dbg_sc = dout("dbg_sc", [128, 8])
        dbg_s1 = dout("dbg_s1", [1, 8])
        dbg_cc = dout("dbg_cc", [1, 2])
        dma("sync", SY, dbg_sc, t_scpp[:], waits=snap())
        dma("sync", SY, dbg_s1, t_sc1[:])
        dma("sync", SY, dbg_cc, t_ccout[:])

    S.add("sync", lambda e: e.nop(),
          waits=[(VE, VE.n), (SY, SY.n), (SC, SC.n), (GP, GP.n), (VD, VD.n),
                 (SD, SD.n), (TM, TM.n), (GD, GD.n)])

    with nc.Block() as block:
        for eng in ["sync", "scalar", "vector", "tensor", "gpsimd"]:
            @getattr(block, eng)
            def _(e, eng=eng):
                replay(S, eng, e)

    e_es.close()
    g_es.close()
    cm.close()
    return nc


def replay(S, eng_name, e):
    waited = {}
    for op in S.ops:
        if op.eng != eng_name:
            continue
        for (s, v) in op.waits:
            if waited.get(id(s), -1) >= v:
                continue
            e.wait_ge(s.h, v)
            waited[id(s)] = v
        inst = op.emit(e)
        for (s, amt) in op.incs:
            inst.then_inc(s.h, amt)


def kernel(L0_attn_x, L0_attn_W_in, L0_attn_conv_w, L0_attn_conv_b, L0_attn_W_B,
           L0_attn_W_C, L0_attn_A, L0_attn_D, L0_attn_W_out, L0_ffn_w_gate,
           L0_ffn_w_up, L0_ffn_w_down):
    x = np.asarray(L0_attn_x, np.float32)
    B = x.shape[0]
    conv_w = np.asarray(L0_attn_conv_w, np.float32)[:, 0, :]
    conv_b = np.asarray(L0_attn_conv_b, np.float32)
    cw = conv_w.reshape(NC_I, 128, 3).transpose(1, 0, 2)
    cb = conv_b.reshape(NC_I, 128).T[:, :, None]
    conv_pk = np.ascontiguousarray(
        np.concatenate([cw, cb], axis=2).reshape(128, NC_I * 4), np.float32)
    d_pk = np.ascontiguousarray(
        np.asarray(L0_attn_D, np.float32).reshape(NC_I, 128).T)
    a_vec = np.ascontiguousarray(np.asarray(L0_attn_A, np.float32).reshape(16, 1))
    consts = np.zeros((128, 8), np.float32)
    consts[:, 0] = C32
    consts[:, 1] = 1.0 / 1024.0
    consts[:, 2] = 1e-5
    consts[:, 3] = 65536.0
    consts[:, 4] = 1.0

    common = {
        "w_in": np.ascontiguousarray(np.asarray(L0_attn_W_in, np.float32)),
        "conv_pk": conv_pk,
        "w_b": np.ascontiguousarray(np.asarray(L0_attn_W_B, np.float32)),
        "w_c": np.ascontiguousarray(np.asarray(L0_attn_W_C, np.float32)),
        "a_vec": a_vec,
        "d_pk": d_pk,
        "w_out": np.ascontiguousarray(np.asarray(L0_attn_W_out, np.float32)),
        "w_up": np.ascontiguousarray(np.asarray(L0_ffn_w_up, np.float32)),
        "w_gate": np.ascontiguousarray(np.asarray(L0_ffn_w_gate, np.float32)),
        "w_down": np.ascontiguousarray(np.asarray(L0_ffn_w_down, np.float32)),
        "ident_bf": np.eye(128).astype(ml_dtypes.bfloat16),
        "ident_f": np.eye(128, dtype=np.float32),
        "ones_col_bf": np.ones((128, 1), ml_dtypes.bfloat16),
        "ones16_f": np.ones((16, 1), np.float32),
        "ones_row_f": np.ones((1, 128), np.float32),
        "consts": consts,
    }
    in_maps = []
    for b in range(B):
        m = dict(common)
        m["x_fm"] = np.ascontiguousarray(x[b].T)
        in_maps.append(m)

    nc = build_nc()
    res = run_bass_kernel_spmd(nc, in_maps, core_ids=list(range(B)))
    out = np.stack([np.asarray(r["y_fm"]).T for r in res.results], axis=0)
    return np.ascontiguousarray(out.astype(np.float32))




# version 24
# speedup vs baseline: 1.0350x; 1.0350x over previous
import numpy as np
import ml_dtypes
import concourse.bass as bass
import concourse.mybir as mybir
from concourse.bass_utils import run_bass_kernel_spmd
from contextlib import ExitStack

F32 = mybir.dt.float32
BF16 = mybir.dt.bfloat16
U16 = mybir.dt.uint16
AF = mybir.ActivationFunctionType
ALU = mybir.AluOpType
AX = mybir.AxisListType.X

C32 = 1.5 * 2.0 ** 23
D_MODEL = 1024
D_INNER = 2048
D_FFN = 3072
SEQ = 4096
TT = 512
NT = SEQ // TT
NC_D = D_MODEL // 128
NC_I = D_INNER // 128
NC_F = D_FFN // 128
NCORES = 8
QMAX = 32767.0
DBG = False
SUB = 8
NSUB = NC_F // SUB


class Sem:
    def __init__(self, cm, nc, name):
        self.h = cm.enter_context(nc.semaphore(name))
        self.n = 0

    def inc(self, amt=1):
        self.n += amt
        return self.n


class Op:
    __slots__ = ("eng", "waits", "emit", "incs")

    def __init__(self, eng, waits, emit, incs):
        self.eng, self.waits, self.emit, self.incs = eng, waits, emit, incs


class Sched:
    def __init__(self):
        self.ops = []

    def add(self, eng, emit, waits=(), inc=None, dma=False):
        incs = []
        if inc is not None:
            amt = 16 if dma else 1
            inc.inc(amt)
            incs.append((inc, amt))
        self.ops.append(Op(eng, [(s, v) for (s, v) in waits], emit, incs))


def build_nc():
    nc = bass.Bass("TRN2", target_bir_lowering=False, detect_race_conditions=False)
    cm = ExitStack()

    def din(name, shape, dt=F32):
        return nc.dram_tensor(name, shape, dt, kind="ExternalInput").ap()

    def dout(name, shape, dt=F32):
        return nc.dram_tensor(name, shape, dt, kind="ExternalOutput").ap()

    def dscr(name, shape, dt):
        if DBG:
            return nc.dram_tensor(name, shape, dt, kind="ExternalOutput").ap()
        return nc.dram_tensor(name, shape, dt).ap()

    x_fm = din("x_fm", [D_MODEL, SEQ])
    w_in = din("w_in", [D_MODEL, 2 * D_INNER])
    conv_pk = din("conv_pk", [128, NC_I * 4])
    w_b = din("w_b", [D_INNER, 16])
    w_c = din("w_c", [D_INNER, 16])
    a_vec = din("a_vec", [16, 1])
    d_pk = din("d_pk", [128, NC_I])
    w_out = din("w_out", [D_INNER, D_MODEL])
    w_up = din("w_up", [D_FFN, D_MODEL])
    w_gate = din("w_gate", [D_FFN, D_MODEL])
    w_down = din("w_down", [D_MODEL, D_FFN])
    ident_bf = din("ident_bf", [128, 128], BF16)
    ident_f = din("ident_f", [128, 128], F32)
    ones_col_bf = din("ones_col_bf", [128, 1], BF16)
    ones16_f = din("ones16_f", [16, 1])
    ones_row_f = din("ones_row_f", [1, 128])
    consts = din("consts", [128, 8])

    y_fm = dout("y_fm", [D_MODEL, SEQ])

    s_xm = dscr("s_xm", [D_INNER, SEQ], BF16)
    s_sres = dscr("s_sres", [D_INNER, SEQ], BF16)
    s_x2 = dscr("s_x2", [D_MODEL, SEQ], F32)
    s_x2n = dscr("s_x2n", [D_MODEL, SEQ], BF16)
    s_up = dscr("s_up", [D_FFN, SEQ], BF16)
    s_gate = dscr("s_gate", [D_FFN, SEQ], BF16)
    s_P = dscr("s_P", [D_FFN, SEQ], BF16)
    s_O1 = dscr("s_O1", [D_FFN, SEQ], BF16)
    s_wupT = dscr("s_wupT", [D_MODEL, D_FFN], BF16)
    s_wgateT = dscr("s_wgateT", [D_MODEL, D_FFN], BF16)
    s_wdownT = dscr("s_wdownT", [D_FFN, D_MODEL], BF16)
    cc_in = nc.dram_tensor("cc_in", [1, 2], F32).ap()
    cc_out = nc.dram_tensor("cc_out", [1, 2], F32).ap()

    g_es = ExitStack()

    def gsb(name, shape, dt):
        return g_es.enter_context(nc.sbuf_tensor(name, shape, dt))

    t_ident = gsb("t_ident", [128, 128], BF16)
    t_identf = gsb("t_identf", [128, 128], F32)
    t_ones_col = gsb("t_ones_col", [128, 1], BF16)
    t_ones16 = gsb("t_ones16", [16, 1], F32)
    t_ones_row = gsb("t_ones_row", [1, 128], F32)
    t_consts = gsb("t_consts", [128, 8], F32)
    t_sc1 = gsb("t_sc1", [1, 8], F32)
    t_scpp = gsb("t_scpp", [128, 8], F32)
    t_gmax = gsb("t_gmax", [1, 8], F32)
    t_ccin = gsb("t_ccin", [1, 2], F32)
    t_ccout = gsb("t_ccout", [1, 2], F32)
    t_amax = gsb("t_amax", [128, 1], F32)
    t_amax2 = gsb("t_amax2", [128, 1], F32)
    t_red = gsb("t_red", [128, 2], F32)

    ps_a = g_es.enter_context(nc.psum_tensor("ps_a", [128, TT], F32))
    ps_b = g_es.enter_context(nc.psum_tensor("ps_b", [128, TT], F32))
    ps_c = g_es.enter_context(nc.psum_tensor("ps_c", [128, TT], F32))
    ps_stat = g_es.enter_context(nc.psum_tensor("ps_stat", [1, 2 * TT], F32))
    ps_bc = g_es.enter_context(nc.psum_tensor("ps_bc", [128, TT], F32))
    ps_sm = g_es.enter_context(nc.psum_tensor("ps_sm", [64, TT], F32))
    ps_tr = g_es.enter_context(nc.psum_tensor("ps_tr", [128, 1024], BF16))

    S = Sched()
    SY = Sem(cm, nc, "d_sy")
    GP = Sem(cm, nc, "d_gp")
    SC = Sem(cm, nc, "d_sc")
    VE = Sem(cm, nc, "d_ve")
    TM = Sem(cm, nc, "t_mm")
    VD = Sem(cm, nc, "v_dn")
    SD = Sem(cm, nc, "s_dn")
    GD = Sem(cm, nc, "g_dn")

    def dma(queue_eng, qsem, out, in_, waits=()):
        S.add(queue_eng, lambda e, o=out, i=in_: e.dma_start(out=o, in_=i),
              waits=waits, inc=qsem, dma=True)

    def snap():
        return [(TM, TM.n), (VD, VD.n), (SD, SD.n), (GD, GD.n),
                (SY, SY.n), (GP, GP.n), (SC, SC.n), (VE, VE.n)]

    def csnap():
        return [(TM, TM.n), (VD, VD.n), (SD, SD.n)]

    w_es = ExitStack()

    def wsb(name, shape, dt):
        return w_es.enter_context(nc.sbuf_tensor(name, shape, dt))

    mx_es = ExitStack()

    def msb(name, shape, dt):
        return mx_es.enter_context(nc.sbuf_tensor(name, shape, dt))

    win_es = ExitStack()
    t_win = win_es.enter_context(nc.sbuf_tensor("t_win", [128, NC_D * 4096], BF16, side="right"))
    t_wout = msb("t_wout", [128, NC_I * 1024], BF16)
    t_wbc = msb("t_wbc", [128, NC_I * 32], BF16)
    t_conv = msb("t_conv", [128, NC_I * 4], F32)
    t_d = msb("t_d", [128, NC_I], F32)
    t_decay = msb("t_decay", [16, TT], F32)
    t_dec1 = msb("t_dec1", [16, 1], F32)

    for c in range(NC_D):
        dma("gpsimd", GP, t_win[:, c * 4096:(c + 1) * 4096],
            w_in[c * 128:(c + 1) * 128, :])
    for c in range(NC_I):
        dma("gpsimd", GP, t_wout[:, c * 1024:(c + 1) * 1024],
            w_out[c * 128:(c + 1) * 128, :])
        dma("gpsimd", GP, t_wbc[:, c * 32:c * 32 + 16],
            w_b[c * 128:(c + 1) * 128, :])
        dma("gpsimd", GP, t_wbc[:, c * 32 + 16:c * 32 + 32],
            w_c[c * 128:(c + 1) * 128, :])
    dma("sync", SY, t_conv[:], conv_pk)
    dma("sync", SY, t_d[:], d_pk)
    dma("sync", SY, t_dec1[:], a_vec)
    dma("sync", SY, t_ident[:], ident_bf)
    dma("sync", SY, t_identf[:], ident_f)
    dma("sync", SY, t_ones_col[:], ones_col_bf)
    dma("sync", SY, t_ones16[:], ones16_f)
    dma("sync", SY, t_ones_row[:], ones_row_f)
    dma("sync", SY, t_consts[:], consts)
    wl = [(SY, SY.n), (GP, GP.n)]

    S.add("scalar", lambda e: e.activation(out=t_dec1[:], in_=t_dec1[:],
                                           func=AF.Sigmoid, scale=-1.0),
          waits=wl, inc=SD)
    S.add("vector", lambda e: e.tensor_copy(out=t_decay[:],
                                            in_=t_dec1[:].broadcast_to([16, TT])),
          waits=[(SD, SD.n)], inc=VD)

    dq_mats = [(w_up, D_FFN, D_MODEL, s_wupT),
               (w_gate, D_FFN, D_MODEL, s_wgateT),
               (w_down, D_MODEL, D_FFN, s_wdownT)]
    dq_chunks = []
    for (q_src, q_M, q_K, q_dst) in dq_mats:
        for m in range(q_M // 128):
            dq_chunks.append((q_src, q_M, q_K, q_dst, m))

    dq_ld = {}
    dq_rd = {}
    dq_wqm = {}
    dq_sdc = {}
    dq_scq = {}
    dq_gg = [0]

    def dq_load(g):
        q_src, q_M, q_K, q_dst, m = dq_chunks[g]
        wb = (g % 2) * 3072
        dma("sync", SY, t_wrow[:, wb:wb + q_K],
            q_src[m * 128:(m + 1) * 128, :],
            waits=[(VD, dq_rd.get(g - 2, 0))])
        dq_ld[g] = (SY, SY.n)

    def dq_chain(g):
        q_src, q_M, q_K, q_dst, m = dq_chunks[g]
        NKB = q_K // 32
        wb = (g % 2) * 3072
        w3 = t_wrow[:, wb:wb + q_K].rearrange("p (g b) -> p g b", b=32)
        S.add("vector",
              lambda e, w3=w3, n=NKB: e.tensor_reduce(
                  out=t_bmax[:, 0:n].unsqueeze(2), in_=w3, axis=AX,
                  op=ALU.max, apply_absolute_value=True),
              waits=[dq_ld[g]], inc=VD)
        S.add("vector",
              lambda e, n=NKB: e.tensor_scalar(
                  out=t_sgm[:, 0:n], in0=t_bmax[:, 0:n],
                  scalar1=1.0 / 7.0, scalar2=1e-12, op0=ALU.mult, op1=ALU.max),
              waits=[(VD, VD.n)], inc=VD)
        S.add("vector",
              lambda e, n=NKB: e.reciprocal(out=t_rcs[:, 0:n],
                                            in_=t_sgm[:, 0:n]),
              waits=[(VD, VD.n)], inc=VD)
        rcs_bc = t_rcs[:, 0:NKB].unsqueeze(2).broadcast_to([128, NKB, 32])
        S.add("vector",
              lambda e, w3=w3, rb=rcs_bc: e.tensor_tensor(
                  out=w3, in0=w3, in1=rb, op=ALU.mult),
              waits=[(VD, VD.n)], inc=VD)
        S.add("vector",
              lambda e, wb=wb, K=q_K: e.tensor_scalar(
                  out=t_wrow[:, wb:wb + K], in0=t_wrow[:, wb:wb + K],
                  scalar1=C32, scalar2=None, op0=ALU.add),
              waits=[(VD, VD.n)], inc=VD)
        sgm_bc = t_sgm[:, 0:NKB].unsqueeze(2).broadcast_to([128, NKB, 32])
        S.add("vector",
              lambda e, w3=w3, sb_=sgm_bc, wb=wb, K=q_K: e.scalar_tensor_tensor(
                  out=t_wqm[:, wb:wb + K].rearrange("p (g b) -> p g b", b=32),
                  in0=w3, scalar=-C32, in1=sb_, op0=ALU.add, op1=ALU.mult),
              waits=[(VD, VD.n), (TM, dq_wqm.get(g - 2, 0))], inc=VD)
        dq_rd[g] = VD.n

    def dq_transposes(g):
        q_src, q_M, q_K, q_dst, m = dq_chunks[g]
        NK = q_K // 128
        qb = (g % 2) * 3072
        for grp in range(NK // 8):
            gg = dq_gg[0]
            dq_gg[0] += 1
            ts = (gg % 2) * 1024
            for kk in range(8):
                k = grp * 8 + kk
                S.add("tensor",
                      lambda e, qb=qb, k=k, kk=kk: e.transpose(
                          ps_tr[:, kk * 128:(kk + 1) * 128],
                          t_wqm[:, qb + k * 128:qb + (k + 1) * 128],
                          t_ident[:]),
                      waits=([(VD, dq_rd[g]), (SD, dq_sdc.get(gg - 1, 0))]
                             if kk == 0 else []),
                      inc=TM if kk == 7 else None)
            S.add("scalar",
                  lambda e, ts=ts: e.copy(out=t_trs[:, ts:ts + 1024],
                                          in_=ps_tr[:]),
                  waits=[(TM, TM.n), (SC, dq_scq.get(gg % 2, 0))], inc=SD)
            dq_sdc[gg] = SD.n
            out3 = q_dst[grp * 1024:(grp + 1) * 1024,
                         m * 128:(m + 1) * 128].rearrange(
                             "(k p) q -> p k q", p=128)
            in3 = t_trs[:, ts:ts + 1024].rearrange("p (k q) -> p k q", q=128)
            dma("scalar", SC, out3, in3, waits=[(SD, SD.n)])
            dq_scq[gg % 2] = SC.n
        dq_wqm[g] = TM.n

    def dq_gen():
        n = len(dq_chunks)
        dq_load(0)
        dq_load(1)
        for g in range(n):
            if g >= 1:
                dq_transposes(g - 1)
            dq_chain(g)
            if g + 2 < n:
                dq_load(g + 2)
            yield 1 if dq_chunks[g][2] == D_MODEL else 3
        dq_transposes(n - 1)

    dq_pump = dq_gen()
    dq_alive = [True]

    def pump(units):
        got = 0
        while dq_alive[0] and got < units:
            try:
                got += next(dq_pump)
            except StopIteration:
                dq_alive[0] = False

    def pump_all():
        while dq_alive[0]:
            pump(1000)

    w_es.close()

    ash_es = ExitStack()

    def hsb(name, shape, dt):
        return ash_es.enter_context(nc.sbuf_tensor(name, shape, dt))

    t_xsq = hsb("t_xsq", [128, NC_D * TT], BF16)
    st_mean = hsb("st_mean", [1, TT], F32)
    st_ex2 = hsb("st_ex2", [1, TT], F32)
    st_var = hsb("st_var", [1, TT], F32)
    st_r = hsb("st_r", [1, TT], F32)
    st_tmp = hsb("st_tmp", [1, TT], F32)
    t_rbc = hsb("t_rbc", [128, 2 * TT], BF16)
    t_mbc = hsb("t_mbc", [128, 2 * TT], BF16)

    dq_es = ExitStack()

    def qsb(name, shape, dt):
        return dq_es.enter_context(nc.sbuf_tensor(name, shape, dt))

    t_wrow = qsb("t_wrow", [128, 2 * 3072], F32)
    t_wqm = qsb("t_wqm", [128, 2 * 3072], BF16)
    t_trs = qsb("t_trs", [128, 2 * 1024], BF16)
    t_bmax = qsb("t_bmax", [128, 96], F32)
    t_sgm = qsb("t_sgm", [128, 96], F32)
    t_rcs = qsb("t_rcs", [128, 96], F32)

    def emit_rstd_chain(par, entry_waits=()):
        S.add("vector", lambda e: e.tensor_scalar(out=st_mean[:],
                                                  in0=ps_stat[0:1, 0:TT],
                                                  scalar1=1.0 / 1024.0, scalar2=None,
                                                  op0=ALU.mult),
              waits=list(entry_waits), inc=VD)
        S.add("vector", lambda e: e.tensor_scalar(out=st_ex2[:],
                                                  in0=ps_stat[0:1, TT:2 * TT],
                                                  scalar1=1.0 / 1024.0, scalar2=None,
                                                  op0=ALU.mult), inc=VD)
        psstat_free = VD.n
        S.add("vector", lambda e: e.tensor_tensor(out=st_tmp[:], in0=st_mean[:],
                                                  in1=st_mean[:], op=ALU.mult),
              inc=VD)
        S.add("vector", lambda e: e.tensor_tensor(out=st_var[:], in0=st_ex2[:],
                                                  in1=st_tmp[:], op=ALU.subtract),
              inc=VD)
        S.add("vector", lambda e: e.tensor_scalar(out=st_var[:], in0=st_var[:],
                                                  scalar1=0.0, scalar2=1e-5,
                                                  op0=ALU.max, op1=ALU.add), inc=VD)
        S.add("vector", lambda e: e.reciprocal(out=st_r[:], in_=st_var[:]),
              waits=[(VD, VD.n)], inc=VD)
        S.add("scalar", lambda e: e.activation(out=st_r[:], in_=st_r[:],
                                               func=AF.Sqrt),
              waits=[(VD, VD.n)], inc=SD)
        rdone = (SD, SD.n)
        return rdone, psstat_free

    def emit_rstd_bcast(par, rdone):
        rb = t_rbc[:, par * TT:(par + 1) * TT]
        mb = t_mbc[:, par * TT:(par + 1) * TT]
        S.add("tensor", lambda e: e.matmul(ps_bc[:], t_ones_row[:], st_r[:],
                                           start=True, stop=True),
              waits=[rdone, (SD, SD.n)], inc=TM)
        S.add("scalar", lambda e, rb=rb: e.copy(out=rb, in_=ps_bc[:]),
              waits=[(TM, TM.n), (VD, VD.n)], inc=SD)
        S.add("tensor", lambda e: e.matmul(ps_bc[:], t_ones_row[:], st_mean[:],
                                           start=True, stop=True),
              waits=[(SD, SD.n)], inc=TM)
        S.add("scalar", lambda e, mb=mb: e.copy(out=mb, in_=ps_bc[:]),
              waits=[(TM, TM.n)], inc=SD)
        return (SD, SD.n)

    def emit_rstd(par, entry_waits=()):
        rdone, psfree = emit_rstd_chain(par, entry_waits)
        return emit_rstd_bcast(par, rdone), psfree

    def emit_stats_block(buf, base, mm_waits=(), first_mm_waits=(),
                         sq_tm_wait=0):
        S.add("scalar",
              lambda e, buf=buf, base=base: e.activation(
                  out=t_xsq[:], in_=buf[:, base:base + NC_D * TT],
                  func=AF.Square),
              waits=list(mm_waits) + [(TM, sq_tm_wait)], inc=SD)
        sq = (SD, SD.n)
        for c in range(NC_D):
            S.add("tensor",
                  lambda e, buf=buf, base=base, c=c: e.matmul(
                      ps_stat[0:1, 0:TT], t_ones_col[:],
                      buf[:, base + c * TT:base + (c + 1) * TT],
                      start=(c == 0), stop=(c == NC_D - 1)),
                  waits=((list(mm_waits) + list(first_mm_waits))
                         if c == 0 else []),
                  inc=None)
        for c in range(NC_D):
            S.add("tensor",
                  lambda e, c=c: e.matmul(
                      ps_stat[0:1, TT:2 * TT], t_ones_col[:],
                      t_xsq[:, c * TT:(c + 1) * TT],
                      start=(c == 0), stop=(c == NC_D - 1)),
                  waits=[sq] if c == 0 else [],
                  inc=TM if c == NC_D - 1 else None)

    a1_es = ExitStack()

    def a1b(name, shape, dt):
        return a1_es.enter_context(nc.sbuf_tensor(name, shape, dt))

    t_xbf = a1b("t_xbf", [128, 2 * NC_D * TT], BF16)
    t_xm1 = a1b("t_xm1", [128, 4 * TT], BF16)
    t_sr1 = a1b("t_sr1", [128, 4 * TT], BF16)

    a1_xld = {}
    a1_stats_mm = {}
    a1_rbc = {}
    a1_psfree = {-1: 0}
    a1_xn_v = {-1: 0, -2: 0}
    a1_xr_tm = {-1: 0, -2: 0}

    def a1_loads(i):
        par = i % 2
        base = par * NC_D * TT
        for c in range(NC_D):
            dma("gpsimd", GP, t_xbf[:, base + c * TT:base + (c + 1) * TT],
                x_fm[c * 128:(c + 1) * 128, i * TT:(i + 1) * TT],
                waits=((wl + [(TM, a1_xr_tm[i - 2]), (VD, a1_xn_v[i - 2])])
                       if c == 0 else ()))
        a1_xld[i] = (GP, GP.n)

    def a1_stats(i):
        par = i % 2
        base = par * NC_D * TT
        emit_stats_block(t_xbf, base, mm_waits=[a1_xld[i]],
                         first_mm_waits=[(VD, a1_psfree[i - 1])],
                         sq_tm_wait=a1_stats_mm.get(i - 1, 0))
        a1_stats_mm[i] = TM.n

    def a1_loads_stats(i):
        a1_loads(i)
        a1_stats(i)

    a1_loads_stats(0)
    a1_rbc[0], a1_psfree[0] = emit_rstd(0, entry_waits=[(TM, a1_stats_mm[0])])

    def emit_xn(i):
        par = i % 2
        xnb = par * NC_D * TT
        mbc3 = t_mbc[:, par * TT:(par + 1) * TT].unsqueeze(1).broadcast_to(
            [128, NC_D, TT])
        rbc3 = t_rbc[:, par * TT:(par + 1) * TT].unsqueeze(1).broadcast_to(
            [128, NC_D, TT])
        xb3 = t_xbf[:, xnb:xnb + NC_D * TT].rearrange("p (c t) -> p c t", t=TT)
        S.add("vector", lambda e, a=xb3, m=mbc3: e.tensor_tensor(
            out=a, in0=a, in1=m, op=ALU.subtract),
            waits=[a1_rbc[i], (TM, a1_xr_tm[i - 2])], inc=VD)
        S.add("vector", lambda e, a=xb3, r=rbc3: e.tensor_tensor(
            out=a, in0=a, in1=r, op=ALU.mult), inc=VD)
        a1_xn_v[i] = VD.n
        return (VD, VD.n)

    xn_hoist = {}
    xm_scq = {}
    sr_scq = {}
    for i in range(NT):
        par = i % 2
        xnb = par * NC_D * TT
        if i + 1 < NT:
            a1_loads(i + 1)
        xn_done = xn_hoist.pop(i, None)
        if xn_done is None:
            xn_done = emit_xn(i)
        evac_at = {}
        for j in range(2 * NC_I):
            pbank = ps_a if j % 2 == 0 else ps_b
            bank_wait = ([(SD, evac_at[j - 2])] if j >= 2
                         else [(SD, SD.n)] if j == 0 else [])
            for k in range(NC_D):
                S.add("tensor",
                      lambda e, j=j, k=k, pb=pbank, xnb=xnb: e.matmul(
                          pb[:],
                          t_win[:, k * 4096 + j * 128:k * 4096 + (j + 1) * 128],
                          t_xbf[:, xnb + k * TT:xnb + (k + 1) * TT],
                          start=(k == 0), stop=(k == NC_D - 1)),
                      waits=([xn_done] + bank_wait if k == 0 else []),
                      inc=TM if k == NC_D - 1 else None)
            if j < NC_I:
                sl = (i * NC_I + j) % 4
                S.add("scalar",
                      lambda e, sl=sl, pb=pbank: e.copy(
                          out=t_xm1[:, sl * TT:(sl + 1) * TT], in_=pb[:]),
                      waits=[(TM, TM.n), (SC, xm_scq.get(sl, 0))],
                      inc=SD)
                evac_at[j] = SD.n
                dma("scalar", SC, s_xm[j * 128:(j + 1) * 128, i * TT:(i + 1) * TT],
                    t_xm1[:, sl * TT:(sl + 1) * TT],
                    waits=[(SD, SD.n)])
                xm_scq[sl] = SC.n
            else:
                jj = j - NC_I
                sl = (i * NC_I + jj) % 4
                S.add("scalar",
                      lambda e, sl=sl, pb=pbank: e.activation(
                          out=t_sr1[:, sl * TT:(sl + 1) * TT], in_=pb[:],
                          func=AF.Silu),
                      waits=[(TM, TM.n), (SC, sr_scq.get(sl, 0))], inc=SD)
                evac_at[j] = SD.n
                dma("scalar", SC,
                    s_sres[jj * 128:(jj + 1) * 128, i * TT:(i + 1) * TT],
                    t_sr1[:, sl * TT:(sl + 1) * TT], waits=[(SD, SD.n)])
                sr_scq[sl] = SC.n
            if j == 4 and i + 1 < NT:
                a1_stats(i + 1)
            if j == 6 and i + 1 < NT:
                a1_rdone, a1_psfree[i + 1] = emit_rstd_chain(
                    (i + 1) % 2, entry_waits=[(TM, a1_stats_mm[i + 1])])
            if j == 14 and i + 1 < NT:
                a1_rbc[i + 1] = (SD, 0)
                a1_rbc[i + 1] = (SD,
                                 emit_rstd_bcast((i + 1) % 2, a1_rdone)[1])
            if j == 15 and i + 1 < NT:
                xn_hoist[i + 1] = emit_xn(i + 1)
            if j in (16, 22, 28):
                pump(3)
        a1_xr_tm[i] = TM.n

    pump_all()
    dq3 = (SC, SC.n)
    a1_es.close()
    dq_es.close()
    win_es.close()

    a2_es = ExitStack()

    def a2b(name, shape, dt):
        return a2_es.enter_context(nc.sbuf_tensor(name, shape, dt))

    t_xmh = a2b("t_xmh", [128, 2 * NC_I * (TT + 2)], BF16)
    t_sres = a2b("t_sres", [128, 2 * NC_I * TT], BF16)
    t_xc = a2b("t_xc", [128, 2 * NC_I * TT], BF16)
    t_bt = a2b("t_bt", [16, TT], F32)
    t_ct = a2b("t_ct", [16, TT], F32)
    t_states = a2b("t_states", [16, TT], F32)
    t_carry = a2b("t_carry", [16, 1], F32)
    t_prod = a2b("t_prod", [16, TT], F32)
    t_ys = a2b("t_ys", [1, TT], F32)
    t_ysbc = a2b("t_ysbc", [128, 2 * TT], BF16)
    t_dbf = a2b("t_dbf", [128, NC_I], BF16)
    t_xfp = a2b("t_xfp", [128, NC_D * TT], F32)
    t_x2c = a2b("t_x2c", [128, 2 * TT], F32)
    t_x2bf = a2b("t_x2bf", [128, 2 * NC_D * TT], BF16)
    t_xn2c = a2b("t_xn2c", [128, 2 * TT], BF16)

    ysmm_sd = {-1: 0}
    x2st = {}
    xn2st = {}
    cvv = {-1: 0, -2: 0}
    cvg_rd = {-1: 0, -2: 0}
    cvs = {-1: 0, -2: 0}
    bcmm = {-1: 0, -2: 0}
    ychv = {-1: 0, -2: 0}
    st2mm = {-1: a1_stats_mm[NT - 1], -2: 0}
    xn2v = {-1: 0, -2: 0}
    nrm2_prev = {-1: ((SD, 0), a1_psfree[NT - 1]),
                 -2: ((SD, 0), 0), -3: ((SD, 0), 0)}
    a2_xml = {}
    a2_xmh = {}
    a2_xbf_sd = {}
    prodv = {-1: 0}
    ysmm = {-1: 0}
    wout_tm = {-1: 0, -2: 0}
    evac_sd = {0: 0, 1: 0}

    S.add("vector", lambda e: e.memset(t_carry[:], 0.0), waits=snap(), inc=VD)
    S.add("vector", lambda e: e.memset(t_amax[:], 0.0), inc=VD)
    S.add("vector", lambda e: e.tensor_copy(out=t_dbf[:], in_=t_d[:]), inc=VD)
    a2_start = snap()

    def a2_loads(i):
        par = i % 2
        hb = par * NC_I * (TT + 2)
        sb = par * NC_I * TT
        t0 = i * TT
        lo = max(t0 - 1, 0)
        hi = min(t0 + TT + 1, SEQ)
        for c in range(NC_I):
            off = hb + c * (TT + 2) + (1 if t0 == 0 else 0)
            dma("sync", SY, t_xmh[:, off:off + (hi - lo)],
                s_xm[c * 128:(c + 1) * 128, lo:hi],
                waits=((a2_start if i <= 1 else
                        [(VD, cvv[i - 2]), (SD, cvs[i - 2])]) if c == 0 else ()))
            if t0 == 0:
                S.add("vector", lambda e, c=c, hb=hb: e.memset(
                    t_xmh[:, hb + c * (TT + 2):hb + c * (TT + 2) + 1], 0.0),
                    inc=VD)
            if t0 + TT == SEQ:
                S.add("vector", lambda e, c=c, hb=hb: e.memset(
                    t_xmh[:, hb + c * (TT + 2) + TT + 1:
                          hb + c * (TT + 2) + TT + 2], 0.0),
                    inc=VD)
        for c in range(NC_I):
            dma("gpsimd", VE, t_sres[:, sb + c * TT:sb + (c + 1) * TT],
                s_sres[c * 128:(c + 1) * 128, t0:t0 + TT],
                waits=((a2_start if i <= 1 else [(VD, ychv[i - 2])])
                       if c == 0 else ()))
        a2_xml[i] = [(SY, SY.n), (VE, VE.n), (VD, VD.n)]
        a2_xmh[i] = [(SY, SY.n), (VD, VD.n)]

    silu_sd = {}

    def a2_conv1(i, c0, c1):
        par = i % 2
        hb = par * NC_I * (TT + 2)
        xcb = par * NC_I * TT
        for c in range(c0, c1):
            base = hb + c * (TT + 2)
            S.add("scalar",
                  lambda e, c=c, base=base, xcb=xcb: e.activation(
                      out=t_xc[:, xcb + c * TT:xcb + (c + 1) * TT],
                      in_=t_xmh[:, base + 1:base + 1 + TT], func=AF.Identity,
                      bias=t_conv[:, c * 4 + 3:c * 4 + 4],
                      scale=t_conv[:, c * 4 + 1:c * 4 + 2]),
                  waits=(a2_xmh[i] + [(TM, wout_tm[i - 2]),
                                      (VD, ychv[i - 2])]) if c == c0 else [],
                  inc=SD)
        cvs[i] = SD.n

    def a2_taps(i, c0, c1):
        par = i % 2
        hb = par * NC_I * (TT + 2)
        xcb = par * NC_I * TT
        sconv = (SD, cvs[i])
        for c in range(c0, c1):
            base = hb + c * (TT + 2)
            S.add("vector",
                  lambda e, c=c, base=base, xcb=xcb: e.scalar_tensor_tensor(
                      out=t_xc[:, xcb + c * TT:xcb + (c + 1) * TT],
                      in0=t_xmh[:, base:base + TT],
                      scalar=t_conv[:, c * 4 + 0:c * 4 + 1],
                      in1=t_xc[:, xcb + c * TT:xcb + (c + 1) * TT],
                      op0=ALU.mult, op1=ALU.add),
                  waits=[sconv] if c == c0 else [], inc=VD)
            S.add("vector",
                  lambda e, c=c, base=base, xcb=xcb: e.scalar_tensor_tensor(
                      out=t_xc[:, xcb + c * TT:xcb + (c + 1) * TT],
                      in0=t_xmh[:, base + 2:base + 2 + TT],
                      scalar=t_conv[:, c * 4 + 2:c * 4 + 3],
                      in1=t_xc[:, xcb + c * TT:xcb + (c + 1) * TT],
                      op0=ALU.mult, op1=ALU.add),
                  inc=VD)
        cvv[i] = VD.n

    def a2_silu(i, c0, c1):
        par = i % 2
        xcb = par * NC_I * TT
        S.add("scalar",
              lambda e, xcb=xcb, c0=c0, c1=c1: e.activation(
                  out=t_xc[:, xcb + c0 * TT:xcb + c1 * TT],
                  in_=t_xc[:, xcb + c0 * TT:xcb + c1 * TT], func=AF.Silu),
              waits=[(VD, cvv[i])], inc=SD)
        silu_sd[i] = SD.n

    def a2_chain(i):
        par = i % 2
        xcb = par * NC_I * TT
        silu_done = (SD, silu_sd[i])
        for k in range(NC_I):
            S.add("tensor",
                  lambda e, k=k, xcb=xcb: e.matmul(
                      ps_sm[0:16, :], t_wbc[:, k * 32:k * 32 + 16],
                      t_xc[:, xcb + k * TT:xcb + (k + 1) * TT],
                      start=(k == 0), stop=(k == NC_I - 1)),
                  waits=([silu_done, (SD, ysmm_sd[i - 1])]
                         if k == 0 else []),
                  inc=None)
            S.add("tensor",
                  lambda e, k=k, xcb=xcb: e.matmul(
                      ps_bc[0:16, :], t_wbc[:, k * 32 + 16:k * 32 + 32],
                      t_xc[:, xcb + k * TT:xcb + (k + 1) * TT],
                      start=(k == 0), stop=(k == NC_I - 1)),
                  waits=[(SD, nrm2_prev[i - 3][0][1])] if k == 0 else [],
                  inc=TM if k == NC_I - 1 else None)
        bcmm[i] = TM.n
        S.add("scalar", lambda e: e.copy(out=t_bt[:], in_=ps_sm[0:16, :]),
              waits=[(TM, TM.n), (VD, prodv[i - 1])], inc=SD)
        S.add("scalar", lambda e: e.copy(out=t_ct[:], in_=ps_bc[0:16, :]),
              inc=SD)
        S.add("vector", lambda e: e.tensor_tensor_scan(
            out=t_states[:], data0=t_decay[:], data1=t_bt[:],
            initial=t_carry[:], op0=ALU.mult, op1=ALU.add),
            waits=[(SD, SD.n)], inc=VD)
        S.add("vector", lambda e: e.tensor_copy(out=t_carry[:],
                                                in_=t_states[:, TT - 1:TT]),
              waits=[(VD, VD.n)], inc=VD)
        S.add("vector", lambda e: e.tensor_tensor(out=t_prod[:], in0=t_states[:],
                                                  in1=t_ct[:], op=ALU.mult),
              waits=[(VD, VD.n), (TM, ysmm[i - 1])], inc=VD)
        prodv[i] = VD.n
        S.add("tensor", lambda e: e.matmul(ps_sm[0:1, :], t_ones16[:], t_prod[:],
                                           start=True, stop=True),
              waits=[(VD, VD.n)], inc=TM)
        ysmm[i] = TM.n
        S.add("scalar", lambda e: e.copy(out=t_ys[:], in_=ps_sm[0:1, :]),
              waits=[(TM, TM.n)], inc=SD)
        S.add("tensor", lambda e: e.matmul(ps_bc[:], t_ones_row[:], t_ys[:],
                                           start=True, stop=True),
              waits=[(SD, SD.n)], inc=TM)
        S.add("scalar",
              lambda e, par=par: e.copy(
                  out=t_ysbc[:, par * TT:(par + 1) * TT], in_=ps_bc[:]),
              waits=[(TM, TM.n), (VD, ychv[i - 1])], inc=SD)
        ysmm_sd[i] = SD.n

    def a2_y(i):
        par = i % 2
        xcb = par * NC_I * TT
        xc3 = t_xc[:, xcb:xcb + NC_I * TT].rearrange("p (c t) -> p c t", t=TT)
        dbc3 = t_dbf[:, 0:NC_I].unsqueeze(2).broadcast_to([128, NC_I, TT])
        ysbc3 = t_ysbc[:, par * TT:(par + 1) * TT].unsqueeze(1).broadcast_to(
            [128, NC_I, TT])
        sres3 = t_sres[:, xcb:xcb + NC_I * TT].rearrange("p (c t) -> p c t",
                                                         t=TT)
        S.add("vector", lambda e, a=xc3, d=dbc3: e.tensor_tensor(
            out=a, in0=a, in1=d, op=ALU.mult),
            waits=[(SD, ysmm_sd[i])] + a2_xml[i], inc=VD)
        S.add("vector", lambda e, a=xc3, y=ysbc3: e.tensor_tensor(
            out=a, in0=a, in1=y, op=ALU.add), inc=VD)
        S.add("vector", lambda e, a=xc3, s=sres3: e.tensor_tensor(
            out=a, in0=a, in1=s, op=ALU.mult), inc=VD)
        ychv[i] = VD.n

    a2_rd = {}
    a2_psf = {}

    def a2_stats(i):
        par = i % 2
        xbb = par * NC_D * TT
        emit_stats_block(t_x2bf, xbb, mm_waits=[(SD, a2_xbf_sd[i])],
                         first_mm_waits=[(VD, nrm2_prev[i - 1][1])],
                         sq_tm_wait=st2mm[i - 1])
        st2mm[i] = TM.n

    def a2_rstd1(i):
        a2_rd[i], a2_psf[i] = emit_rstd_chain(
            i % 2, entry_waits=[(TM, st2mm[i])])

    def a2_rstd2(i):
        nrm2_prev[i] = ((SD, emit_rstd_bcast(i % 2, a2_rd[i])[1]), a2_psf[i])

    def a2_xn2(i):
        par = i % 2
        xbb = par * NC_D * TT
        t0 = i * TT
        m3 = t_mbc[:, par * TT:(par + 1) * TT].unsqueeze(1).broadcast_to(
            [128, NC_D, TT])
        r3 = t_rbc[:, par * TT:(par + 1) * TT].unsqueeze(1).broadcast_to(
            [128, NC_D, TT])
        xf3 = t_x2bf[:, xbb:xbb + NC_D * TT].rearrange("p (c t) -> p c t",
                                                       t=TT)
        S.add("vector",
              lambda e, a=xf3, m=m3: e.tensor_tensor(
                  out=a, in0=a, in1=m, op=ALU.subtract),
              waits=[nrm2_prev[i][0]], inc=VD)
        S.add("vector",
              lambda e, a=xf3, r=r3: e.tensor_tensor(
                  out=a, in0=a, in1=r, op=ALU.mult), inc=VD)
        xn2done = (VD, VD.n)
        for c in range(NC_D):
            dma("gpsimd", VE, s_x2n[c * 128:(c + 1) * 128, t0:t0 + TT],
                t_x2bf[:, xbb + c * TT:xbb + (c + 1) * TT],
                waits=[xn2done] if c == 0 else ())
        xn2st[par] = VE.n
        S.add("vector",
              lambda e, xbb=xbb: e.tensor_reduce(
                  out=t_red[:, 0:1], in_=t_x2bf[:, xbb:xbb + NC_D * TT],
                  axis=AX, op=ALU.max, apply_absolute_value=True),
              inc=VD)
        S.add("vector",
              lambda e: e.tensor_tensor(out=t_amax[:], in0=t_amax[:],
                                        in1=t_red[:, 0:1], op=ALU.max),
              waits=[(VD, VD.n)], inc=VD)
        xn2v[i] = VD.n

    a2_loads(0)
    a2_conv1(0, 0, NC_I)
    a2_taps(0, 0, NC_I)
    a2_silu(0, 0, NC_I)
    a2_chain(0)
    a2_y(0)

    for i in range(NT):
        par = i % 2
        xcb = par * NC_I * TT
        xbb = par * NC_D * TT
        t0 = i * TT
        if i + 1 < NT:
            a2_loads(i + 1)
        for c in range(NC_D):
            dma("sync", SY, t_xfp[:, c * TT:(c + 1) * TT],
                x_fm[c * 128:(c + 1) * 128, t0:t0 + TT],
                waits=[(TM, wout_tm[i - 1])] if c == 0 else ())
        xfp_ld = (SY, SY.n)

        psj = [ps_a, ps_b]
        for jj in range(NC_D):
            pb = psj[jj % 2]
            sl = jj % 2
            S.add("tensor",
                  lambda e, pb=pb, jj=jj: e.matmul(
                      pb[:], t_identf[:], t_xfp[:, jj * TT:(jj + 1) * TT],
                      start=True, stop=False),
                  waits=[xfp_ld, (SD, evac_sd.get(sl, 0))]
                        + ([(VD, ychv[i])] if jj == 0 else []),
                  inc=None)
            for k in range(NC_I):
                S.add("tensor",
                      lambda e, k=k, jj=jj, pb=pb, xcb=xcb: e.matmul(
                          pb[:],
                          t_wout[:, k * 1024 + jj * 128:
                                 k * 1024 + (jj + 1) * 128],
                          t_xc[:, xcb + k * TT:xcb + (k + 1) * TT],
                          start=False, stop=(k == NC_I - 1)),
                      inc=TM if k == NC_I - 1 else None)
            mm_done = (TM, TM.n)
            S.add("scalar",
                  lambda e, sl=sl, pb=pb: e.copy(
                      out=t_x2c[:, sl * TT:(sl + 1) * TT], in_=pb[:]),
                  waits=[mm_done, (VE, x2st.get(sl, 0))], inc=SD)
            dma("gpsimd", VE, s_x2[jj * 128:(jj + 1) * 128, t0:t0 + TT],
                t_x2c[:, sl * TT:(sl + 1) * TT], waits=[(SD, SD.n)])
            x2st[sl] = VE.n
            S.add("scalar",
                  lambda e, jj=jj, pb=pb, xbb=xbb: e.copy(
                      out=t_x2bf[:, xbb + jj * TT:xbb + (jj + 1) * TT],
                      in_=pb[:]),
                  waits=([(TM, st2mm[i - 2]), (VD, xn2v[i - 2]),
                          (VE, xn2st.get(par, 0))] if jj == 0 else []),
                  inc=SD)
            evac_sd[sl] = SD.n
            if jj == NC_D - 1:
                a2_xbf_sd[i] = SD.n
            if i + 1 < NT:
                if jj == 0:
                    a2_conv1(i + 1, 0, 8)
                elif jj == 1:
                    a2_conv1(i + 1, 8, NC_I)
                    a2_taps(i + 1, 0, 8)
                elif jj == 2:
                    a2_taps(i + 1, 8, NC_I)
                elif jj == 3:
                    a2_silu(i + 1, 0, 8)
                elif jj == 4:
                    a2_silu(i + 1, 8, NC_I)
                    a2_chain(i + 1)
                elif jj == 6:
                    a2_y(i + 1)
            if i >= 1:
                if jj == 2:
                    a2_stats(i - 1)
                elif jj == 3:
                    a2_rstd1(i - 1)
                elif jj == 5:
                    a2_rstd2(i - 1)
                elif jj == 7:
                    a2_xn2(i - 1)
        wout_tm[i] = TM.n

    a2_stats(NT - 1)
    a2_rstd1(NT - 1)
    a2_rstd2(NT - 1)
    a2_xn2(NT - 1)

    def part_max(src_col_ap, dst_col):
        S.add("tensor", lambda e: e.transpose(ps_sm[0:1, 0:128], src_col_ap,
                                              t_identf[:]),
              waits=[(VD, VD.n)], inc=TM)
        S.add("vector",
              lambda e, d=dst_col: e.tensor_reduce(out=t_gmax[:, d:d + 1],
                                                   in_=ps_sm[0:1, 0:128],
                                                   axis=AX, op=ALU.max),
              waits=[(TM, TM.n)], inc=VD)

    def do_allreduce(col0, col1):
        S.add("vector", lambda e, c=col0: e.tensor_copy(out=t_ccin[:, 0:1],
                                                        in_=t_gmax[:, c:c + 1]),
              waits=[(VD, VD.n), (GD, GD.n)], inc=VD)
        S.add("vector", lambda e, c=col1: e.tensor_copy(out=t_ccin[:, 1:2],
                                                        in_=t_gmax[:, c:c + 1]),
              inc=VD)
        dma("gpsimd", GP, cc_in, t_ccin[:], waits=[(VD, VD.n)])
        S.add("gpsimd",
              lambda e: e.collective_compute(
                  "AllReduce", ALU.max, replica_groups=[list(range(NCORES))],
                  ins=[cc_in.opt()], outs=[cc_out.opt()]),
              waits=[(GP, GP.n)], inc=GD)
        dma("gpsimd", GP, t_ccout[:], cc_out, waits=[(GD, GD.n)])
        return (GP, GP.n)

    def bcast_scalars():
        S.add("tensor", lambda e: e.matmul(ps_bc[:, 0:8], t_ones_row[:], t_sc1[:],
                                           start=True, stop=True),
              waits=[(VD, VD.n), (SD, SD.n)], inc=TM)
        S.add("scalar", lambda e: e.copy(out=t_scpp[:], in_=ps_bc[:, 0:8]),
              waits=[(TM, TM.n)], inc=SD)
        return (SD, SD.n)

    def vop(emit, waits=()):
        S.add("vector", emit, waits=waits, inc=VD)

    part_max(t_amax[:], 0)
    b0 = do_allreduce(0, 0)
    vop(lambda e: e.reciprocal(out=t_sc1[:, 0:1], in_=t_ccout[:, 0:1]),
        waits=[b0, (SD, SD.n)])
    vop(lambda e: e.tensor_scalar(out=t_sc1[:, 0:1], in0=t_sc1[:, 0:1],
                                  scalar1=QMAX, scalar2=None, op0=ALU.mult),
        waits=[(VD, VD.n)])
    vop(lambda e: e.tensor_scalar(out=t_sc1[:, 1:2], in0=t_ccout[:, 0:1],
                                  scalar1=1.0 / QMAX, scalar2=None, op0=ALU.mult))
    bar0 = bcast_scalars()

    a2_es.close()
    ash_es.close()
    mx_es.close()

    c_es = ExitStack()

    def csb(name, shape, dt):
        return c_es.enter_context(nc.sbuf_tensor(name, shape, dt))

    t_wT1 = csb("t_wT1", [128, NC_D * D_FFN], BF16)
    t_wT2 = csb("t_wT2", [128, NC_D * D_FFN], BF16)
    t_xn2t = csb("t_xn2t", [128, NC_D * TT], BF16)
    t_tbc = csb("t_tbc", [128, NC_D * TT], F32)
    t_qx = csb("t_qx", [128, 2 * NC_D * TT], BF16)
    t_upsb = csb("t_upsb", [128, NC_F * TT], BF16)
    t_gatesb = csb("t_gatesb", [128, NC_F * TT], BF16)

    c_start = snap()
    for k in range(NC_D):
        dma("scalar", SC, t_wT1[:, k * D_FFN:(k + 1) * D_FFN],
            s_wupT[k * 128:(k + 1) * 128, :],
            waits=([dq3] + c_start) if k == 0 else ())
        dma("scalar", SC, t_wT2[:, k * D_FFN:(k + 1) * D_FFN],
            s_wgateT[k * 128:(k + 1) * 128, :])
    wTl = (SC, SC.n)
    vop(lambda e: e.memset(t_amax[:], 0.0), waits=[bar0])
    vop(lambda e: e.memset(t_amax2[:], 0.0))

    cq_ld = {}
    cq_done = {}
    cq_mm = {-1: 0, -2: 0}
    redv = {-1: 0}

    def c_prep(i):
        par = i % 2
        t0 = i * TT
        for c in range(NC_D):
            dma("sync", SY, t_xn2t[:, c * TT:(c + 1) * TT],
                s_x2n[c * 128:(c + 1) * 128, t0:t0 + TT],
                waits=([(SD, SD.n), (VD, VD.n), (VE, VE.n)] if c == 0 else ()))
        cq_ld[i] = (SY, SY.n)
        S.add("scalar", lambda e: e.activation(out=t_tbc[:], in_=t_xn2t[:],
                                               func=AF.Identity,
                                               bias=t_consts[:, 0:1],
                                               scale=t_scpp[:, 0:1]),
              waits=[cq_ld[i], bar0], inc=SD)
        S.add("vector",
              lambda e, par=par: e.tensor_scalar(
                  out=t_qx[:, par * NC_D * TT:(par + 1) * NC_D * TT],
                  in0=t_tbc[:], scalar1=-C32, scalar2=None, op0=ALU.add),
              waits=[(SD, SD.n), (TM, cq_mm[i - 2])], inc=VD)
        cq_done[i] = (VD, VD.n)

    c_prep(0)
    evu = {m: 0 for m in range(-2, 0)}
    evg = {m: 0 for m in range(-2, 0)}
    for i in range(NT):
        par = i % 2
        qxb = par * NC_D * TT
        t0 = i * TT
        qx_done = cq_done[i]
        if i > 0:
            evu = {-2: evu[NC_F - 2], -1: evu[NC_F - 1]}
            evg = {-2: evg[NC_F - 2], -1: evg[NC_F - 1]}
        for m in range(NC_F):
            pu = ps_a if m % 2 == 0 else ps_b
            pg = ps_c if m % 2 == 0 else ps_bc
            uw = [(SD, evu[m - 2])]
            gw = [(VD, evg[m - 2])]
            for k in range(NC_D):
                S.add("tensor",
                      lambda e, m=m, k=k, pu=pu, qxb=qxb: e.matmul(
                          pu[:],
                          t_wT1[:, k * D_FFN + m * 128:k * D_FFN + (m + 1) * 128],
                          t_qx[:, qxb + k * TT:qxb + (k + 1) * TT],
                          start=(k == 0), stop=(k == NC_D - 1)),
                      waits=([qx_done, wTl] + uw if k == 0 else []),
                      inc=TM if k == NC_D - 1 else None)
            S.add("scalar",
                  lambda e, m=m, pu=pu: e.copy(out=t_upsb[:, m * TT:(m + 1) * TT],
                                               in_=pu[:]),
                  waits=[(TM, TM.n)] + ([(SC, SC.n), (VD, redv[i - 1])]
                                        if m == 0 else []), inc=SD)
            evu[m] = SD.n
            dma("scalar", SC, s_up[m * 128:(m + 1) * 128, t0:t0 + TT],
                t_upsb[:, m * TT:(m + 1) * TT], waits=[(SD, SD.n)])
            for k in range(NC_D):
                S.add("tensor",
                      lambda e, m=m, k=k, pg=pg, qxb=qxb: e.matmul(
                          pg[:],
                          t_wT2[:, k * D_FFN + m * 128:k * D_FFN + (m + 1) * 128],
                          t_qx[:, qxb + k * TT:qxb + (k + 1) * TT],
                          start=(k == 0), stop=(k == NC_D - 1)),
                      waits=gw if k == 0 else [],
                      inc=TM if k == NC_D - 1 else None)
            S.add("vector",
                  lambda e, m=m, pg=pg: e.tensor_copy(
                      out=t_gatesb[:, m * TT:(m + 1) * TT], in_=pg[:]),
                  waits=[(TM, TM.n)] + ([(SY, SY.n)] if m == 0 else []), inc=VD)
            evg[m] = VD.n
            dma("sync", SY, s_gate[m * 128:(m + 1) * 128, t0:t0 + TT],
                t_gatesb[:, m * TT:(m + 1) * TT], waits=[(VD, VD.n)])
            if m % 3 == 2:
                lo = (m - 2) * TT
                hi = (m + 1) * TT
                vop(lambda e, lo=lo, hi=hi: e.tensor_reduce(
                    out=t_red[:, 0:1], in_=t_upsb[:, lo:hi], axis=AX,
                    op=ALU.max, apply_absolute_value=True),
                    waits=[(SD, evu[m])])
                vop(lambda e: e.tensor_tensor(out=t_amax[:], in0=t_amax[:],
                                              in1=t_red[:, 0:1], op=ALU.max),
                    waits=[(VD, VD.n)])
                vop(lambda e, lo=lo, hi=hi: e.tensor_reduce(
                    out=t_red[:, 1:2], in_=t_gatesb[:, lo:hi], axis=AX,
                    op=ALU.max, apply_absolute_value=True))
                vop(lambda e: e.tensor_tensor(out=t_amax2[:], in0=t_amax2[:],
                                              in1=t_red[:, 1:2], op=ALU.max),
                    waits=[(VD, VD.n)])
            if m == 15 and i + 1 < NT:
                c_prep(i + 1)
        cq_mm[i] = TM.n
        redv[i] = VD.n

    part_max(t_amax[:], 1)
    part_max(t_amax2[:], 2)
    b1 = do_allreduce(1, 2)
    vop(lambda e: e.reciprocal(out=t_sc1[:, 2:3], in_=t_ccout[:, 0:1]),
        waits=[b1, (SD, SD.n)])
    vop(lambda e: e.reciprocal(out=t_sc1[:, 3:4], in_=t_ccout[:, 1:2]),
        waits=[(VD, VD.n)])
    vop(lambda e: e.tensor_scalar(out=t_sc1[:, 2:4], in0=t_sc1[:, 2:4],
                                  scalar1=QMAX, scalar2=None, op0=ALU.mult),
        waits=[(VD, VD.n)])
    vop(lambda e: e.tensor_tensor(out=t_sc1[:, 4:5], in0=t_ccout[:, 1:2],
                                  in1=t_sc1[:, 1:2], op=ALU.mult))
    vop(lambda e: e.tensor_scalar(out=t_sc1[:, 4:5], in0=t_sc1[:, 4:5],
                                  scalar1=1.0 / QMAX, scalar2=None, op0=ALU.mult),
        waits=[(VD, VD.n)])
    vop(lambda e: e.tensor_scalar(out=t_sc1[:, 5:6], in0=t_sc1[:, 4:5],
                                  scalar1=-C32, scalar2=None, op0=ALU.mult),
        waits=[(VD, VD.n)])
    bar1 = bcast_scalars()

    c_es.close()

    d_es = ExitStack()

    def dsb(name, shape, dt):
        return d_es.enter_context(nc.sbuf_tensor(name, shape, dt))

    t_gin = dsb("t_gin", [128, 2 * SUB * TT], BF16)
    t_uin = dsb("t_uin", [128, 2 * SUB * TT], BF16)
    t_tbg = dsb("t_tbg", [128, SUB * TT], F32)
    t_sig = dsb("t_sig", [128, SUB * TT], F32)
    t_tbsi = dsb("t_tbsi", [128, SUB * TT], F32)
    t_sibf = dsb("t_sibf", [128, SUB * TT], BF16)
    t_gqbf = dsb("t_gqbf", [128, SUB * TT], BF16)
    t_pout = dsb("t_pout", [128, 2 * SUB * TT], BF16)
    t_dabs = dsb("t_dabs", [128, 2 * SUB * TT], BF16)

    d_start = snap()
    vop(lambda e: e.memset(t_amax[:], 0.0), waits=[bar1])
    subs = [(i, sub) for i in range(NT) for sub in range(NSUB)]
    d_tb = {-1: 0, -2: 0}
    d_gq = {-1: 0}
    d_pst = {}

    def d1_load(s):
        i, sub = subs[s]
        gb = (s % 2) * SUB * TT
        for m in range(SUB):
            dma("sync", SY, t_gin[:, gb + m * TT:gb + (m + 1) * TT],
                s_gate[(sub * SUB + m) * 128:(sub * SUB + m + 1) * 128,
                       i * TT:(i + 1) * TT],
                waits=(([(SD, d_tb[s - 2])] + (d_start if s == 0 else []))
                       if m == 0 else ()))
        return (SY, SY.n)

    d_gld = {0: d1_load(0)}
    for s in range(len(subs)):
        i, sub = subs[s]
        t0 = i * TT
        m0 = sub * SUB
        gb = (s % 2) * SUB * TT
        pb = (s % 2) * SUB * TT
        if s + 1 < len(subs):
            d_gld[s + 1] = d1_load(s + 1)
        S.add("scalar", lambda e, gb=gb: e.activation(
            out=t_tbg[:], in_=t_gin[:, gb:gb + SUB * TT],
            func=AF.Identity, bias=t_consts[:, 0:1], scale=t_scpp[:, 3:4]),
            waits=[d_gld[s], bar1, (VD, d_gq[s - 1])], inc=SD)
        S.add("scalar", lambda e: e.activation(out=t_sig[:], in_=t_tbg[:],
                                               func=AF.Sigmoid,
                                               bias=t_scpp[:, 5:6],
                                               scale=t_scpp[:, 4:5]),
              inc=SD)
        S.add("scalar", lambda e: e.activation(out=t_tbsi[:], in_=t_sig[:],
                                               func=AF.Identity,
                                               bias=t_consts[:, 0:1],
                                               scale=t_consts[:, 3:4]),
              inc=SD)
        d_tb[s] = SD.n
        sdn = (SD, SD.n)
        vop(lambda e: e.tensor_scalar(out=t_sibf[:], in0=t_tbsi[:],
                                      scalar1=C32 + 65535.0, scalar2=C32,
                                      op0=ALU.min, op1=ALU.subtract),
            waits=[sdn])
        vop(lambda e: e.tensor_scalar(out=t_gqbf[:], in0=t_tbg[:],
                                      scalar1=-C32, scalar2=None, op0=ALU.add))
        d_gq[s] = VD.n
        vop(lambda e, pb=pb: e.tensor_tensor(
            out=t_pout[:, pb:pb + SUB * TT], in0=t_gqbf[:],
            in1=t_sibf[:], op=ALU.mult),
            waits=[(VD, VD.n), (VE, d_pst.get(s % 2, 0))])
        pdn = (VD, VD.n)
        for m in range(SUB):
            dma("gpsimd", VE,
                s_P[(m0 + m) * 128:(m0 + m + 1) * 128, t0:t0 + TT],
                t_pout[:, pb + m * TT:pb + (m + 1) * TT],
                waits=[pdn] if m == 0 else ())
        d_pst[s % 2] = VE.n
        vop(lambda e, pb=pb: e.tensor_reduce(
            out=t_red[:, 0:1], in_=t_pout[:, pb:pb + SUB * TT],
            axis=AX, op=ALU.max, apply_absolute_value=True),
            waits=[pdn])
        vop(lambda e: e.tensor_tensor(out=t_amax[:], in0=t_amax[:],
                                      in1=t_red[:, 0:1], op=ALU.max),
            waits=[(VD, VD.n)])

    part_max(t_amax[:], 3)
    b2 = do_allreduce(3, 3)
    vop(lambda e: e.reciprocal(out=t_sc1[:, 6:7], in_=t_ccout[:, 0:1]),
        waits=[b2, (SD, SD.n)])
    vop(lambda e: e.tensor_scalar(out=t_sc1[:, 6:7], in0=t_sc1[:, 6:7],
                                  scalar1=QMAX, scalar2=None, op0=ALU.mult),
        waits=[(VD, VD.n)])
    vop(lambda e: e.tensor_tensor(out=t_sc1[:, 7:8], in0=t_sc1[:, 4:5],
                                  in1=t_ccout[:, 0:1], op=ALU.mult))
    vop(lambda e: e.tensor_scalar(out=t_sc1[:, 7:8], in0=t_sc1[:, 7:8],
                                  scalar1=1.0 / 65536.0 / QMAX, scalar2=None,
                                  op0=ALU.mult),
        waits=[(VD, VD.n)])
    bar2 = bcast_scalars()

    vop(lambda e: e.memset(t_amax[:], 0.0), waits=[bar2])
    d2_start = [(SD, SD.n), (VE, VE.n), (VD, VD.n)]
    d_tb = {-1: 0, -2: 0}
    d_gq = {-1: 0}
    d_pst = {}

    def d2_load(s):
        i, sub = subs[s]
        gb = (s % 2) * SUB * TT
        for m in range(SUB):
            dma("sync", SY, t_gin[:, gb + m * TT:gb + (m + 1) * TT],
                s_P[(sub * SUB + m) * 128:(sub * SUB + m + 1) * 128,
                    i * TT:(i + 1) * TT],
                waits=(([(SD, d_tb[s - 2])] + (d2_start if s <= 1 else []))
                       if m == 0 else ()))
            dma("scalar", SC, t_uin[:, gb + m * TT:gb + (m + 1) * TT],
                s_up[(sub * SUB + m) * 128:(sub * SUB + m + 1) * 128,
                     i * TT:(i + 1) * TT],
                waits=[(SD, d_tb[s - 2])] if m == 0 else ())
        return [(SY, SY.n), (SC, SC.n)]

    d_gld = {0: d2_load(0)}
    for s in range(len(subs)):
        i, sub = subs[s]
        t0 = i * TT
        m0 = sub * SUB
        gb = (s % 2) * SUB * TT
        pb = (s % 2) * SUB * TT
        if s + 1 < len(subs):
            d_gld[s + 1] = d2_load(s + 1)
        S.add("scalar", lambda e, gb=gb: e.activation(
            out=t_tbg[:], in_=t_gin[:, gb:gb + SUB * TT],
            func=AF.Identity, bias=t_consts[:, 0:1], scale=t_scpp[:, 6:7]),
            waits=d_gld[s] + [bar2, (VD, d_gq[s - 1])], inc=SD)
        S.add("scalar", lambda e, gb=gb: e.activation(
            out=t_tbsi[:], in_=t_uin[:, gb:gb + SUB * TT],
            func=AF.Identity, bias=t_consts[:, 0:1], scale=t_scpp[:, 2:3]),
            inc=SD)
        d_tb[s] = SD.n
        sdn = (SD, SD.n)
        vop(lambda e: e.tensor_scalar(out=t_gqbf[:], in0=t_tbg[:],
                                      scalar1=-C32, scalar2=None, op0=ALU.add),
            waits=[sdn])
        vop(lambda e: e.tensor_scalar(out=t_sibf[:], in0=t_tbsi[:],
                                      scalar1=-C32, scalar2=None, op0=ALU.add))
        d_gq[s] = VD.n
        vop(lambda e, pb=pb: e.tensor_tensor(
            out=t_pout[:, pb:pb + SUB * TT], in0=t_gqbf[:],
            in1=t_sibf[:], op=ALU.mult),
            waits=[(VD, VD.n), (VE, d_pst.get(s % 2, 0))])
        pdn = (VD, VD.n)
        for m in range(SUB):
            dma("gpsimd", VE,
                s_O1[(m0 + m) * 128:(m0 + m + 1) * 128, t0:t0 + TT],
                t_pout[:, pb + m * TT:pb + (m + 1) * TT],
                waits=[pdn] if m == 0 else ())
        d_pst[s % 2] = VE.n
        vop(lambda e, pb=pb: e.tensor_reduce(
            out=t_red[:, 0:1], in_=t_pout[:, pb:pb + SUB * TT],
            axis=AX, op=ALU.max, apply_absolute_value=True),
            waits=[pdn])
        vop(lambda e: e.tensor_tensor(out=t_amax[:], in0=t_amax[:],
                                      in1=t_red[:, 0:1], op=ALU.max),
            waits=[(VD, VD.n)])

    part_max(t_amax[:], 4)
    b3 = do_allreduce(4, 4)
    vop(lambda e: e.reciprocal(out=t_gmax[:, 5:6], in_=t_sc1[:, 2:3]),
        waits=[b3, (SD, SD.n)])
    vop(lambda e: e.tensor_tensor(out=t_gmax[:, 6:7], in0=t_gmax[:, 5:6],
                                  in1=t_sc1[:, 1:2], op=ALU.mult),
        waits=[(VD, VD.n)])
    vop(lambda e: e.reciprocal(out=t_sc1[:, 2:3], in_=t_ccout[:, 0:1]),
        waits=[(VD, VD.n)])
    vop(lambda e: e.tensor_scalar(out=t_sc1[:, 2:3], in0=t_sc1[:, 2:3],
                                  scalar1=QMAX, scalar2=None, op0=ALU.mult),
        waits=[(VD, VD.n)])
    vop(lambda e: e.tensor_tensor(out=t_sc1[:, 3:4], in0=t_sc1[:, 7:8],
                                  in1=t_gmax[:, 6:7], op=ALU.mult))
    vop(lambda e: e.tensor_tensor(out=t_sc1[:, 3:4], in0=t_sc1[:, 3:4],
                                  in1=t_ccout[:, 0:1], op=ALU.mult),
        waits=[(VD, VD.n)])
    vop(lambda e: e.tensor_scalar(out=t_sc1[:, 3:4], in0=t_sc1[:, 3:4],
                                  scalar1=1.0 / QMAX, scalar2=None, op0=ALU.mult),
        waits=[(VD, VD.n)])
    bar3 = bcast_scalars()

    d_es.close()

    e_es = ExitStack()

    def esb(name, shape, dt):
        return e_es.enter_context(nc.sbuf_tensor(name, shape, dt))

    t_wdT = esb("t_wdT", [128, NC_F * 1024], BF16)
    t_o1 = esb("t_o1", [128, NC_F * TT], BF16)
    t_tbe = esb("t_tbe", [128, NC_F * TT], F32)
    t_qo = esb("t_qo", [128, 2 * NC_F * TT], BF16)
    t_xe = esb("t_xe", [128, 2 * TT], F32)
    t_oe = esb("t_oe", [128, 2 * TT], F32)

    e_start = snap()
    for k in range(NC_F):
        dma("scalar", SC, t_wdT[:, k * 1024:(k + 1) * 1024],
            s_wdownT[k * 128:(k + 1) * 128, :],
            waits=([bar3] + e_start) if k == 0 else ())
    wdl = (SC, SC.n)
    eq_ld = {}
    eq_done = {}
    eq_mm = {-1: 0, -2: 0}

    def e_prep(i):
        par = i % 2
        t0 = i * TT
        for m in range(NC_F):
            dma("sync", SY, t_o1[:, m * TT:(m + 1) * TT],
                s_O1[m * 128:(m + 1) * 128, t0:t0 + TT],
                waits=(([(SD, SD.n), (VE, VE.n)]
                        + (e_start if i == 0 else []))
                       if m == 0 else ()))
        eq_ld[i] = (SY, SY.n)
        S.add("scalar", lambda e: e.activation(out=t_tbe[:], in_=t_o1[:],
                                               func=AF.Identity,
                                               bias=t_consts[:, 0:1],
                                               scale=t_scpp[:, 2:3]),
              waits=[eq_ld[i], bar3], inc=SD)
        S.add("vector",
              lambda e, par=par: e.tensor_scalar(
                  out=t_qo[:, par * NC_F * TT:(par + 1) * NC_F * TT],
                  in0=t_tbe[:], scalar1=-C32, scalar2=None, op0=ALU.add),
              waits=[(SD, SD.n), (TM, eq_mm[i - 2])], inc=VD)
        eq_done[i] = (VD, VD.n)

    e_prep(0)
    eve = {m: 0 for m in range(-2, 0)}
    sve = {m: 0 for m in range(-2, 0)}
    for i in range(NT):
        par = i % 2
        qob = par * NC_F * TT
        t0 = i * TT
        qo_done = eq_done[i]
        if i > 0:
            eve = {-2: eve[NC_D - 2], -1: eve[NC_D - 1]}
            sve = {-2: sve[NC_D - 2], -1: sve[NC_D - 1]}
        for j in range(NC_D):
            pb = ps_a if j % 2 == 0 else ps_b
            jw = [(VD, eve[j - 2])]
            for k in range(NC_F):
                S.add("tensor",
                      lambda e, j=j, k=k, pb=pb, qob=qob: e.matmul(
                          pb[:], t_wdT[:, k * 1024 + j * 128:k * 1024 + (j + 1) * 128],
                          t_qo[:, qob + k * TT:qob + (k + 1) * TT],
                          start=(k == 0), stop=(k == NC_F - 1)),
                      waits=([qo_done, wdl] + jw if k == 0 else []),
                      inc=TM if k == NC_F - 1 else None)
            dma("sync", SY, t_xe[:, (j % 2) * TT:(j % 2 + 1) * TT],
                s_x2[j * 128:(j + 1) * 128, t0:t0 + TT],
                waits=jw)
            stw = [(VE, sve[j - 2])]
            S.add("vector",
                  lambda e, pb=pb, j=j: e.scalar_tensor_tensor(
                      out=t_oe[:, (j % 2) * TT:(j % 2 + 1) * TT], in0=pb[:],
                      scalar=t_scpp[:, 3:4],
                      in1=t_xe[:, (j % 2) * TT:(j % 2 + 1) * TT],
                      op0=ALU.mult, op1=ALU.add),
                  waits=[(TM, TM.n), (SY, SY.n)] + stw, inc=VD)
            eve[j] = VD.n
            dma("gpsimd", VE, y_fm[j * 128:(j + 1) * 128, t0:t0 + TT],
                t_oe[:, (j % 2) * TT:(j % 2 + 1) * TT],
                waits=[(VD, VD.n)])
            sve[j] = VE.n
            if j == 1 and i + 1 < NT:
                e_prep(i + 1)
            if j == NC_D - 1:
                eq_mm[i] = TM.n

    if DBG:
        dbg_sc = dout("dbg_sc", [128, 8])
        dbg_s1 = dout("dbg_s1", [1, 8])
        dbg_cc = dout("dbg_cc", [1, 2])
        dma("sync", SY, dbg_sc, t_scpp[:], waits=snap())
        dma("sync", SY, dbg_s1, t_sc1[:])
        dma("sync", SY, dbg_cc, t_ccout[:])

    S.add("sync", lambda e: e.nop(),
          waits=[(VE, VE.n), (SY, SY.n), (SC, SC.n), (GP, GP.n), (VD, VD.n),
                 (SD, SD.n), (TM, TM.n), (GD, GD.n)])

    with nc.Block() as block:
        for eng in ["sync", "scalar", "vector", "tensor", "gpsimd"]:
            @getattr(block, eng)
            def _(e, eng=eng):
                replay(S, eng, e)

    e_es.close()
    g_es.close()
    cm.close()
    return nc


def replay(S, eng_name, e):
    waited = {}
    for op in S.ops:
        if op.eng != eng_name:
            continue
        for (s, v) in op.waits:
            if waited.get(id(s), -1) >= v:
                continue
            e.wait_ge(s.h, v)
            waited[id(s)] = v
        inst = op.emit(e)
        for (s, amt) in op.incs:
            inst.then_inc(s.h, amt)


def kernel(L0_attn_x, L0_attn_W_in, L0_attn_conv_w, L0_attn_conv_b, L0_attn_W_B,
           L0_attn_W_C, L0_attn_A, L0_attn_D, L0_attn_W_out, L0_ffn_w_gate,
           L0_ffn_w_up, L0_ffn_w_down):
    x = np.asarray(L0_attn_x, np.float32)
    B = x.shape[0]
    conv_w = np.asarray(L0_attn_conv_w, np.float32)[:, 0, :]
    conv_b = np.asarray(L0_attn_conv_b, np.float32)
    cw = conv_w.reshape(NC_I, 128, 3).transpose(1, 0, 2)
    cb = conv_b.reshape(NC_I, 128).T[:, :, None]
    conv_pk = np.ascontiguousarray(
        np.concatenate([cw, cb], axis=2).reshape(128, NC_I * 4), np.float32)
    d_pk = np.ascontiguousarray(
        np.asarray(L0_attn_D, np.float32).reshape(NC_I, 128).T)
    a_vec = np.ascontiguousarray(np.asarray(L0_attn_A, np.float32).reshape(16, 1))
    consts = np.zeros((128, 8), np.float32)
    consts[:, 0] = C32
    consts[:, 1] = 1.0 / 1024.0
    consts[:, 2] = 1e-5
    consts[:, 3] = 65536.0
    consts[:, 4] = 1.0

    common = {
        "w_in": np.ascontiguousarray(np.asarray(L0_attn_W_in, np.float32)),
        "conv_pk": conv_pk,
        "w_b": np.ascontiguousarray(np.asarray(L0_attn_W_B, np.float32)),
        "w_c": np.ascontiguousarray(np.asarray(L0_attn_W_C, np.float32)),
        "a_vec": a_vec,
        "d_pk": d_pk,
        "w_out": np.ascontiguousarray(np.asarray(L0_attn_W_out, np.float32)),
        "w_up": np.ascontiguousarray(np.asarray(L0_ffn_w_up, np.float32)),
        "w_gate": np.ascontiguousarray(np.asarray(L0_ffn_w_gate, np.float32)),
        "w_down": np.ascontiguousarray(np.asarray(L0_ffn_w_down, np.float32)),
        "ident_bf": np.eye(128).astype(ml_dtypes.bfloat16),
        "ident_f": np.eye(128, dtype=np.float32),
        "ones_col_bf": np.ones((128, 1), ml_dtypes.bfloat16),
        "ones16_f": np.ones((16, 1), np.float32),
        "ones_row_f": np.ones((1, 128), np.float32),
        "consts": consts,
    }
    in_maps = []
    for b in range(B):
        m = dict(common)
        m["x_fm"] = np.ascontiguousarray(x[b].T)
        in_maps.append(m)

    nc = build_nc()
    res = run_bass_kernel_spmd(nc, in_maps, core_ids=list(range(B)))
    out = np.stack([np.asarray(r["y_fm"]).T for r in res.results], axis=0)
    return np.ascontiguousarray(out.astype(np.float32))




# version 30 (final)
# speedup vs baseline: 1.0526x; 1.0170x over previous
import numpy as np
import ml_dtypes
import concourse.bass as bass
import concourse.mybir as mybir
from concourse.bass_utils import run_bass_kernel_spmd
from contextlib import ExitStack

F32 = mybir.dt.float32
BF16 = mybir.dt.bfloat16
U16 = mybir.dt.uint16
AF = mybir.ActivationFunctionType
ALU = mybir.AluOpType
AX = mybir.AxisListType.X

C32 = 1.5 * 2.0 ** 23
D_MODEL = 1024
D_INNER = 2048
D_FFN = 3072
SEQ = 4096
TT = 512
NT = SEQ // TT
NC_D = D_MODEL // 128
NC_I = D_INNER // 128
NC_F = D_FFN // 128
NCORES = 8
QMAX = 32767.0
DBG = False
SUB = 8
NSUB = NC_F // SUB


class Sem:
    def __init__(self, cm, nc, name):
        self.h = cm.enter_context(nc.semaphore(name))
        self.n = 0

    def inc(self, amt=1):
        self.n += amt
        return self.n


class Op:
    __slots__ = ("eng", "waits", "emit", "incs")

    def __init__(self, eng, waits, emit, incs):
        self.eng, self.waits, self.emit, self.incs = eng, waits, emit, incs


class Sched:
    def __init__(self):
        self.ops = []

    def add(self, eng, emit, waits=(), inc=None, dma=False):
        incs = []
        if inc is not None:
            amt = 16 if dma else 1
            inc.inc(amt)
            incs.append((inc, amt))
        self.ops.append(Op(eng, [(s, v) for (s, v) in waits], emit, incs))


def build_nc():
    nc = bass.Bass("TRN2", target_bir_lowering=False, detect_race_conditions=False)
    cm = ExitStack()

    def din(name, shape, dt=F32):
        return nc.dram_tensor(name, shape, dt, kind="ExternalInput").ap()

    def dout(name, shape, dt=F32):
        return nc.dram_tensor(name, shape, dt, kind="ExternalOutput").ap()

    def dscr(name, shape, dt):
        if DBG:
            return nc.dram_tensor(name, shape, dt, kind="ExternalOutput").ap()
        return nc.dram_tensor(name, shape, dt).ap()

    x_fm = din("x_fm", [D_MODEL, SEQ])
    w_in = din("w_in", [D_MODEL, 2 * D_INNER])
    conv_pk = din("conv_pk", [128, NC_I * 4])
    w_b = din("w_b", [D_INNER, 16])
    w_c = din("w_c", [D_INNER, 16])
    a_vec = din("a_vec", [16, 1])
    d_pk = din("d_pk", [128, NC_I])
    w_out = din("w_out", [D_INNER, D_MODEL])
    w_up = din("w_up", [D_FFN, D_MODEL])
    w_gate = din("w_gate", [D_FFN, D_MODEL])
    w_down = din("w_down", [D_MODEL, D_FFN])
    ident_bf = din("ident_bf", [128, 128], BF16)
    ident_f = din("ident_f", [128, 128], F32)
    ones_col_bf = din("ones_col_bf", [128, 1], BF16)
    ones16_f = din("ones16_f", [16, 1])
    ones_row_f = din("ones_row_f", [1, 128])
    consts = din("consts", [128, 8])

    y_fm = dout("y_fm", [D_MODEL, SEQ])

    s_xm = dscr("s_xm", [D_INNER, SEQ], BF16)
    s_sres = dscr("s_sres", [D_INNER, SEQ], BF16)
    s_x2 = dscr("s_x2", [D_MODEL, SEQ], F32)
    s_x2n = dscr("s_x2n", [D_MODEL, SEQ], BF16)
    s_up = dscr("s_up", [D_FFN, SEQ], BF16)
    s_gate = dscr("s_gate", [D_FFN, SEQ], BF16)
    s_P = dscr("s_P", [D_FFN, SEQ], BF16)
    s_O1 = dscr("s_O1", [D_FFN, SEQ], BF16)
    s_wupT = dscr("s_wupT", [D_MODEL, D_FFN], BF16)
    s_wgateT = dscr("s_wgateT", [D_MODEL, D_FFN], BF16)
    s_wdownT = dscr("s_wdownT", [D_FFN, D_MODEL], BF16)
    cc_in = nc.dram_tensor("cc_in", [1, 2], F32).ap()
    cc_out = nc.dram_tensor("cc_out", [1, 2], F32).ap()

    g_es = ExitStack()

    def gsb(name, shape, dt):
        return g_es.enter_context(nc.sbuf_tensor(name, shape, dt))

    t_ident = gsb("t_ident", [128, 128], BF16)
    t_identf = gsb("t_identf", [128, 128], F32)
    t_ones_col = gsb("t_ones_col", [128, 1], BF16)
    t_ones16 = gsb("t_ones16", [16, 1], F32)
    t_ones_row = gsb("t_ones_row", [1, 128], F32)
    t_consts = gsb("t_consts", [128, 8], F32)
    t_sc1 = gsb("t_sc1", [1, 8], F32)
    t_scpp = gsb("t_scpp", [128, 8], F32)
    t_gmax = gsb("t_gmax", [1, 8], F32)
    t_ccin = gsb("t_ccin", [1, 2], F32)
    t_ccout = gsb("t_ccout", [1, 2], F32)
    t_amax = gsb("t_amax", [128, 1], F32)
    t_amax2 = gsb("t_amax2", [128, 1], F32)
    t_red = gsb("t_red", [128, 2], F32)

    ps_a = g_es.enter_context(nc.psum_tensor("ps_a", [128, TT], F32))
    ps_b = g_es.enter_context(nc.psum_tensor("ps_b", [128, TT], F32))
    ps_c = g_es.enter_context(nc.psum_tensor("ps_c", [128, TT], F32))
    ps_stat = g_es.enter_context(nc.psum_tensor("ps_stat", [1, 2 * TT], F32))
    ps_bc = g_es.enter_context(nc.psum_tensor("ps_bc", [128, TT], F32))
    ps_sm = g_es.enter_context(nc.psum_tensor("ps_sm", [64, TT], F32))
    ps_tr = g_es.enter_context(nc.psum_tensor("ps_tr", [128, 1024], BF16))

    S = Sched()
    SY = Sem(cm, nc, "d_sy")
    GP = Sem(cm, nc, "d_gp")
    SC = Sem(cm, nc, "d_sc")
    VE = Sem(cm, nc, "d_ve")
    TM = Sem(cm, nc, "t_mm")
    VD = Sem(cm, nc, "v_dn")
    SD = Sem(cm, nc, "s_dn")
    GD = Sem(cm, nc, "g_dn")

    def dma(queue_eng, qsem, out, in_, waits=()):
        S.add(queue_eng, lambda e, o=out, i=in_: e.dma_start(out=o, in_=i),
              waits=waits, inc=qsem, dma=True)

    def snap():
        return [(TM, TM.n), (VD, VD.n), (SD, SD.n), (GD, GD.n),
                (SY, SY.n), (GP, GP.n), (SC, SC.n), (VE, VE.n)]

    def csnap():
        return [(TM, TM.n), (VD, VD.n), (SD, SD.n)]

    w_es = ExitStack()

    def wsb(name, shape, dt):
        return w_es.enter_context(nc.sbuf_tensor(name, shape, dt))

    mx_es = ExitStack()

    def msb(name, shape, dt):
        return mx_es.enter_context(nc.sbuf_tensor(name, shape, dt))

    win_es = ExitStack()
    t_win = win_es.enter_context(nc.sbuf_tensor("t_win", [128, NC_D * 4096], BF16, side="right"))
    t_wout = msb("t_wout", [128, NC_I * 1024], BF16)
    t_wbc = msb("t_wbc", [128, NC_I * 32], BF16)
    t_conv = msb("t_conv", [128, NC_I * 4], F32)
    t_d = msb("t_d", [128, NC_I], F32)
    t_decay = msb("t_decay", [16, TT], F32)
    t_dec1 = msb("t_dec1", [16, 1], F32)

    for c in range(NC_D):
        dma("gpsimd", GP, t_win[:, c * 4096:(c + 1) * 4096],
            w_in[c * 128:(c + 1) * 128, :])
    for c in range(NC_I):
        dma("gpsimd", GP, t_wout[:, c * 1024:(c + 1) * 1024],
            w_out[c * 128:(c + 1) * 128, :])
        dma("gpsimd", GP, t_wbc[:, c * 32:c * 32 + 16],
            w_b[c * 128:(c + 1) * 128, :])
        dma("gpsimd", GP, t_wbc[:, c * 32 + 16:c * 32 + 32],
            w_c[c * 128:(c + 1) * 128, :])
    dma("sync", SY, t_conv[:], conv_pk)
    dma("sync", SY, t_d[:], d_pk)
    dma("sync", SY, t_dec1[:], a_vec)
    dma("sync", SY, t_ident[:], ident_bf)
    dma("sync", SY, t_identf[:], ident_f)
    dma("sync", SY, t_ones_col[:], ones_col_bf)
    dma("sync", SY, t_ones16[:], ones16_f)
    dma("sync", SY, t_ones_row[:], ones_row_f)
    dma("sync", SY, t_consts[:], consts)
    wl = [(SY, SY.n), (GP, GP.n)]

    S.add("scalar", lambda e: e.activation(out=t_dec1[:], in_=t_dec1[:],
                                           func=AF.Sigmoid, scale=-1.0),
          waits=wl, inc=SD)
    S.add("vector", lambda e: e.tensor_copy(out=t_decay[:],
                                            in_=t_dec1[:].broadcast_to([16, TT])),
          waits=[(SD, SD.n)], inc=VD)

    dq_mats = [(w_up, D_FFN, D_MODEL, s_wupT),
               (w_gate, D_FFN, D_MODEL, s_wgateT),
               (w_down, D_MODEL, D_FFN, s_wdownT)]
    dq_chunks = []
    for (q_src, q_M, q_K, q_dst) in dq_mats:
        for m in range(q_M // 128):
            dq_chunks.append((q_src, q_M, q_K, q_dst, m))

    dq_ld = {}
    dq_rd = {}
    dq_wqm = {}
    dq_sdc = {}
    dq_scq = {}
    dq_gg = [0]

    def dq_load(g):
        q_src, q_M, q_K, q_dst, m = dq_chunks[g]
        wb = (g % 2) * 3072
        dma("sync", SY, t_wrow[:, wb:wb + q_K],
            q_src[m * 128:(m + 1) * 128, :],
            waits=[(VD, dq_rd.get(g - 2, 0))])
        dq_ld[g] = (SY, SY.n)

    def dq_chain(g):
        q_src, q_M, q_K, q_dst, m = dq_chunks[g]
        NKB = q_K // 32
        wb = (g % 2) * 3072
        w3 = t_wrow[:, wb:wb + q_K].rearrange("p (g b) -> p g b", b=32)
        S.add("vector",
              lambda e, w3=w3, n=NKB: e.tensor_reduce(
                  out=t_bmax[:, 0:n].unsqueeze(2), in_=w3, axis=AX,
                  op=ALU.max, apply_absolute_value=True),
              waits=[dq_ld[g]], inc=VD)
        S.add("vector",
              lambda e, n=NKB: e.tensor_scalar(
                  out=t_sgm[:, 0:n], in0=t_bmax[:, 0:n],
                  scalar1=1.0 / 7.0, scalar2=1e-12, op0=ALU.mult, op1=ALU.max),
              waits=[(VD, VD.n)], inc=VD)
        S.add("vector",
              lambda e, n=NKB: e.reciprocal(out=t_rcs[:, 0:n],
                                            in_=t_sgm[:, 0:n]),
              waits=[(VD, VD.n)], inc=VD)
        rcs_bc = t_rcs[:, 0:NKB].unsqueeze(2).broadcast_to([128, NKB, 32])
        S.add("vector",
              lambda e, w3=w3, rb=rcs_bc: e.tensor_tensor(
                  out=w3, in0=w3, in1=rb, op=ALU.mult),
              waits=[(VD, VD.n)], inc=VD)
        S.add("vector",
              lambda e, wb=wb, K=q_K: e.tensor_scalar(
                  out=t_wrow[:, wb:wb + K], in0=t_wrow[:, wb:wb + K],
                  scalar1=C32, scalar2=None, op0=ALU.add),
              waits=[(VD, VD.n)], inc=VD)
        sgm_bc = t_sgm[:, 0:NKB].unsqueeze(2).broadcast_to([128, NKB, 32])
        S.add("vector",
              lambda e, w3=w3, sb_=sgm_bc, wb=wb, K=q_K: e.scalar_tensor_tensor(
                  out=t_wqm[:, wb:wb + K].rearrange("p (g b) -> p g b", b=32),
                  in0=w3, scalar=-C32, in1=sb_, op0=ALU.add, op1=ALU.mult),
              waits=[(VD, VD.n), (TM, dq_wqm.get(g - 2, 0))], inc=VD)
        dq_rd[g] = VD.n

    def dq_transposes(g):
        q_src, q_M, q_K, q_dst, m = dq_chunks[g]
        NK = q_K // 128
        qb = (g % 2) * 3072
        for grp in range(NK // 8):
            gg = dq_gg[0]
            dq_gg[0] += 1
            ts = (gg % 2) * 1024
            for kk in range(8):
                k = grp * 8 + kk
                S.add("tensor",
                      lambda e, qb=qb, k=k, kk=kk: e.transpose(
                          ps_tr[:, kk * 128:(kk + 1) * 128],
                          t_wqm[:, qb + k * 128:qb + (k + 1) * 128],
                          t_ident[:]),
                      waits=([(VD, dq_rd[g]), (SD, dq_sdc.get(gg - 1, 0))]
                             if kk == 0 else []),
                      inc=TM if kk == 7 else None)
            S.add("scalar",
                  lambda e, ts=ts: e.copy(out=t_trs[:, ts:ts + 1024],
                                          in_=ps_tr[:]),
                  waits=[(TM, TM.n), (SC, dq_scq.get(gg % 2, 0))], inc=SD)
            dq_sdc[gg] = SD.n
            out3 = q_dst[grp * 1024:(grp + 1) * 1024,
                         m * 128:(m + 1) * 128].rearrange(
                             "(k p) q -> p k q", p=128)
            in3 = t_trs[:, ts:ts + 1024].rearrange("p (k q) -> p k q", q=128)
            dma("scalar", SC, out3, in3, waits=[(SD, SD.n)])
            dq_scq[gg % 2] = SC.n
        dq_wqm[g] = TM.n

    def dq_gen():
        n = len(dq_chunks)
        dq_load(0)
        dq_load(1)
        for g in range(n):
            if g >= 1:
                dq_transposes(g - 1)
            dq_chain(g)
            if g + 2 < n:
                dq_load(g + 2)
            yield 1 if dq_chunks[g][2] == D_MODEL else 3
        dq_transposes(n - 1)

    dq_pump = dq_gen()
    dq_alive = [True]

    def pump(units):
        got = 0
        while dq_alive[0] and got < units:
            try:
                got += next(dq_pump)
            except StopIteration:
                dq_alive[0] = False

    def pump_all():
        while dq_alive[0]:
            pump(1000)

    w_es.close()

    ash_es = ExitStack()

    def hsb(name, shape, dt):
        return ash_es.enter_context(nc.sbuf_tensor(name, shape, dt))

    t_xsq = hsb("t_xsq", [128, NC_D * TT], BF16)
    st_mean = hsb("st_mean", [1, TT], F32)
    st_ex2 = hsb("st_ex2", [1, TT], F32)
    st_var = hsb("st_var", [1, TT], F32)
    st_r = hsb("st_r", [1, TT], F32)
    st_tmp = hsb("st_tmp", [1, TT], F32)
    t_rbc = hsb("t_rbc", [128, 2 * TT], BF16)
    t_mbc = hsb("t_mbc", [128, 2 * TT], BF16)

    dq_es = ExitStack()

    def qsb(name, shape, dt):
        return dq_es.enter_context(nc.sbuf_tensor(name, shape, dt))

    t_wrow = qsb("t_wrow", [128, 2 * 3072], F32)
    t_wqm = qsb("t_wqm", [128, 2 * 3072], BF16)
    t_trs = qsb("t_trs", [128, 2 * 1024], BF16)
    t_bmax = qsb("t_bmax", [128, 96], F32)
    t_sgm = qsb("t_sgm", [128, 96], F32)
    t_rcs = qsb("t_rcs", [128, 96], F32)

    def emit_rstd_chain(par, entry_waits=()):
        S.add("vector", lambda e: e.tensor_scalar(out=st_mean[:],
                                                  in0=ps_stat[0:1, 0:TT],
                                                  scalar1=1.0 / 1024.0, scalar2=None,
                                                  op0=ALU.mult),
              waits=list(entry_waits), inc=VD)
        S.add("vector", lambda e: e.tensor_scalar(out=st_ex2[:],
                                                  in0=ps_stat[0:1, TT:2 * TT],
                                                  scalar1=1.0 / 1024.0, scalar2=None,
                                                  op0=ALU.mult), inc=VD)
        psstat_free = VD.n
        S.add("vector", lambda e: e.tensor_tensor(out=st_tmp[:], in0=st_mean[:],
                                                  in1=st_mean[:], op=ALU.mult),
              inc=VD)
        S.add("vector", lambda e: e.tensor_tensor(out=st_var[:], in0=st_ex2[:],
                                                  in1=st_tmp[:], op=ALU.subtract),
              inc=VD)
        S.add("vector", lambda e: e.tensor_scalar(out=st_var[:], in0=st_var[:],
                                                  scalar1=0.0, scalar2=1e-5,
                                                  op0=ALU.max, op1=ALU.add), inc=VD)
        S.add("vector", lambda e: e.reciprocal(out=st_r[:], in_=st_var[:]),
              waits=[(VD, VD.n)], inc=VD)
        S.add("scalar", lambda e: e.activation(out=st_r[:], in_=st_r[:],
                                               func=AF.Sqrt),
              waits=[(VD, VD.n)], inc=SD)
        rdone = (SD, SD.n)
        return rdone, psstat_free

    def emit_rstd_bcast(par, rdone):
        rb = t_rbc[:, par * TT:(par + 1) * TT]
        mb = t_mbc[:, par * TT:(par + 1) * TT]
        S.add("tensor", lambda e: e.matmul(ps_bc[:], t_ones_row[:], st_r[:],
                                           start=True, stop=True),
              waits=[rdone, (SD, SD.n)], inc=TM)
        S.add("scalar", lambda e, rb=rb: e.copy(out=rb, in_=ps_bc[:]),
              waits=[(TM, TM.n), (VD, VD.n)], inc=SD)
        S.add("tensor", lambda e: e.matmul(ps_bc[:], t_ones_row[:], st_mean[:],
                                           start=True, stop=True),
              waits=[(SD, SD.n)], inc=TM)
        S.add("scalar", lambda e, mb=mb: e.copy(out=mb, in_=ps_bc[:]),
              waits=[(TM, TM.n)], inc=SD)
        return (SD, SD.n)

    def emit_rstd(par, entry_waits=()):
        rdone, psfree = emit_rstd_chain(par, entry_waits)
        return emit_rstd_bcast(par, rdone), psfree

    def emit_stats_block(buf, base, mm_waits=(), first_mm_waits=(),
                         sq_tm_wait=0):
        S.add("scalar",
              lambda e, buf=buf, base=base: e.activation(
                  out=t_xsq[:], in_=buf[:, base:base + NC_D * TT],
                  func=AF.Square),
              waits=list(mm_waits) + [(TM, sq_tm_wait)], inc=SD)
        sq = (SD, SD.n)
        for c in range(NC_D):
            S.add("tensor",
                  lambda e, buf=buf, base=base, c=c: e.matmul(
                      ps_stat[0:1, 0:TT], t_ones_col[:],
                      buf[:, base + c * TT:base + (c + 1) * TT],
                      start=(c == 0), stop=(c == NC_D - 1)),
                  waits=((list(mm_waits) + list(first_mm_waits))
                         if c == 0 else []),
                  inc=None)
        for c in range(NC_D):
            S.add("tensor",
                  lambda e, c=c: e.matmul(
                      ps_stat[0:1, TT:2 * TT], t_ones_col[:],
                      t_xsq[:, c * TT:(c + 1) * TT],
                      start=(c == 0), stop=(c == NC_D - 1)),
                  waits=[sq] if c == 0 else [],
                  inc=TM if c == NC_D - 1 else None)

    a1_es = ExitStack()

    def a1b(name, shape, dt):
        return a1_es.enter_context(nc.sbuf_tensor(name, shape, dt))

    t_xbf = a1b("t_xbf", [128, 2 * NC_D * TT], BF16)
    t_xm1 = a1b("t_xm1", [128, 4 * TT], BF16)
    t_sr1 = a1b("t_sr1", [128, 4 * TT], BF16)

    a1_xld = {}
    a1_stats_mm = {}
    a1_rbc = {}
    a1_psfree = {-1: 0}
    a1_xn_v = {-1: 0, -2: 0}
    a1_xr_tm = {-1: 0, -2: 0}

    def a1_loads(i):
        par = i % 2
        base = par * NC_D * TT
        for c in range(NC_D):
            dma("gpsimd", GP, t_xbf[:, base + c * TT:base + (c + 1) * TT],
                x_fm[c * 128:(c + 1) * 128, i * TT:(i + 1) * TT],
                waits=((wl + [(TM, a1_xr_tm[i - 2]), (VD, a1_xn_v[i - 2])])
                       if c == 0 else ()))
        a1_xld[i] = (GP, GP.n)

    def a1_stats(i):
        par = i % 2
        base = par * NC_D * TT
        emit_stats_block(t_xbf, base, mm_waits=[a1_xld[i]],
                         first_mm_waits=[(VD, a1_psfree[i - 1])],
                         sq_tm_wait=a1_stats_mm.get(i - 1, 0))
        a1_stats_mm[i] = TM.n

    def a1_loads_stats(i):
        a1_loads(i)
        a1_stats(i)

    a1_loads_stats(0)
    a1_rbc[0], a1_psfree[0] = emit_rstd(0, entry_waits=[(TM, a1_stats_mm[0])])

    def emit_xn(i):
        par = i % 2
        xnb = par * NC_D * TT
        mbc3 = t_mbc[:, par * TT:(par + 1) * TT].unsqueeze(1).broadcast_to(
            [128, NC_D, TT])
        rbc3 = t_rbc[:, par * TT:(par + 1) * TT].unsqueeze(1).broadcast_to(
            [128, NC_D, TT])
        xb3 = t_xbf[:, xnb:xnb + NC_D * TT].rearrange("p (c t) -> p c t", t=TT)
        S.add("vector", lambda e, a=xb3, m=mbc3: e.tensor_tensor(
            out=a, in0=a, in1=m, op=ALU.subtract),
            waits=[a1_rbc[i], (TM, a1_xr_tm[i - 2])], inc=VD)
        S.add("vector", lambda e, a=xb3, r=rbc3: e.tensor_tensor(
            out=a, in0=a, in1=r, op=ALU.mult), inc=VD)
        a1_xn_v[i] = VD.n
        return (VD, VD.n)

    xn_hoist = {}
    xm_scq = {}
    sr_scq = {}
    for i in range(NT):
        par = i % 2
        xnb = par * NC_D * TT
        if i + 1 < NT:
            a1_loads(i + 1)
        xn_done = xn_hoist.pop(i, None)
        if xn_done is None:
            xn_done = emit_xn(i)
        evac_at = {}
        for j in range(2 * NC_I):
            pbank = ps_a if j % 2 == 0 else ps_b
            bank_wait = ([(SD, evac_at[j - 2])] if j >= 2
                         else [(SD, SD.n)] if j == 0 else [])
            for k in range(NC_D):
                S.add("tensor",
                      lambda e, j=j, k=k, pb=pbank, xnb=xnb: e.matmul(
                          pb[:],
                          t_win[:, k * 4096 + j * 128:k * 4096 + (j + 1) * 128],
                          t_xbf[:, xnb + k * TT:xnb + (k + 1) * TT],
                          start=(k == 0), stop=(k == NC_D - 1)),
                      waits=([xn_done] + bank_wait if k == 0 else []),
                      inc=TM if k == NC_D - 1 else None)
            if j < NC_I:
                sl = (i * NC_I + j) % 4
                S.add("scalar",
                      lambda e, sl=sl, pb=pbank: e.copy(
                          out=t_xm1[:, sl * TT:(sl + 1) * TT], in_=pb[:]),
                      waits=[(TM, TM.n), (SC, xm_scq.get(sl, 0))],
                      inc=SD)
                evac_at[j] = SD.n
                dma("scalar", SC, s_xm[j * 128:(j + 1) * 128, i * TT:(i + 1) * TT],
                    t_xm1[:, sl * TT:(sl + 1) * TT],
                    waits=[(SD, SD.n)])
                xm_scq[sl] = SC.n
            else:
                jj = j - NC_I
                sl = (i * NC_I + jj) % 4
                S.add("scalar",
                      lambda e, sl=sl, pb=pbank: e.activation(
                          out=t_sr1[:, sl * TT:(sl + 1) * TT], in_=pb[:],
                          func=AF.Silu),
                      waits=[(TM, TM.n), (SC, sr_scq.get(sl, 0))], inc=SD)
                evac_at[j] = SD.n
                dma("scalar", SC,
                    s_sres[jj * 128:(jj + 1) * 128, i * TT:(i + 1) * TT],
                    t_sr1[:, sl * TT:(sl + 1) * TT], waits=[(SD, SD.n)])
                sr_scq[sl] = SC.n
            if j == 4 and i + 1 < NT:
                a1_stats(i + 1)
            if j == 6 and i + 1 < NT:
                a1_rdone, a1_psfree[i + 1] = emit_rstd_chain(
                    (i + 1) % 2, entry_waits=[(TM, a1_stats_mm[i + 1])])
            if j == 14 and i + 1 < NT:
                a1_rbc[i + 1] = (SD, 0)
                a1_rbc[i + 1] = (SD,
                                 emit_rstd_bcast((i + 1) % 2, a1_rdone)[1])
            if j == 15 and i + 1 < NT:
                xn_hoist[i + 1] = emit_xn(i + 1)
            if j in (16, 22, 28):
                pump(3)
        a1_xr_tm[i] = TM.n

    pump_all()
    dq3 = (SC, SC.n)
    a1_es.close()
    dq_es.close()
    win_es.close()

    a2_es = ExitStack()

    def a2b(name, shape, dt):
        return a2_es.enter_context(nc.sbuf_tensor(name, shape, dt))

    t_xmh = a2b("t_xmh", [128, 2 * NC_I * (TT + 2)], BF16)
    t_sres = a2b("t_sres", [128, 2 * NC_I * TT], BF16)
    t_xc = a2b("t_xc", [128, 2 * NC_I * TT], BF16)
    t_bt = a2b("t_bt", [16, TT], F32)
    t_ct = a2b("t_ct", [16, TT], F32)
    t_states = a2b("t_states", [16, TT], F32)
    t_carry = a2b("t_carry", [16, 1], F32)
    t_prod = a2b("t_prod", [16, TT], F32)
    t_ys = a2b("t_ys", [1, TT], F32)
    t_ysbc = a2b("t_ysbc", [128, 2 * TT], BF16)
    t_dbf = a2b("t_dbf", [128, NC_I], BF16)
    t_xfp = a2b("t_xfp", [128, NC_D * TT], F32)
    t_x2c = a2b("t_x2c", [128, 2 * TT], F32)
    t_x2bf = a2b("t_x2bf", [128, 2 * NC_D * TT], BF16)
    t_xn2c = a2b("t_xn2c", [128, 2 * TT], BF16)

    ysmm_sd = {-1: 0}
    x2st = {}
    xn2st = {}
    cvv = {-1: 0, -2: 0}
    cvg_rd = {-1: 0, -2: 0}
    cvs = {-1: 0, -2: 0}
    bcmm = {-1: 0, -2: 0}
    ychv = {-1: 0, -2: 0}
    st2mm = {-1: a1_stats_mm[NT - 1], -2: 0}
    xn2v = {-1: 0, -2: 0}
    nrm2_prev = {-1: ((SD, 0), a1_psfree[NT - 1]),
                 -2: ((SD, 0), 0), -3: ((SD, 0), 0)}
    a2_xml = {}
    a2_xmh = {}
    a2_xbf_sd = {}
    prodv = {-1: 0}
    ysmm = {-1: 0}
    wout_tm = {-1: 0, -2: 0}
    evac_sd = {0: 0, 1: 0}

    S.add("vector", lambda e: e.memset(t_carry[:], 0.0), waits=snap(), inc=VD)
    S.add("vector", lambda e: e.memset(t_amax[:], 0.0), inc=VD)
    S.add("vector", lambda e: e.tensor_copy(out=t_dbf[:], in_=t_d[:]), inc=VD)
    a2_start = snap()

    def a2_loads(i):
        par = i % 2
        hb = par * NC_I * (TT + 2)
        sb = par * NC_I * TT
        t0 = i * TT
        lo = max(t0 - 1, 0)
        hi = min(t0 + TT + 1, SEQ)
        for c in range(NC_I):
            off = hb + c * (TT + 2) + (1 if t0 == 0 else 0)
            dma("sync", SY, t_xmh[:, off:off + (hi - lo)],
                s_xm[c * 128:(c + 1) * 128, lo:hi],
                waits=((a2_start if i <= 1 else
                        [(VD, cvv[i - 2]), (SD, cvs[i - 2])]) if c == 0 else ()))
            if t0 == 0:
                S.add("vector", lambda e, c=c, hb=hb: e.memset(
                    t_xmh[:, hb + c * (TT + 2):hb + c * (TT + 2) + 1], 0.0),
                    inc=VD)
            if t0 + TT == SEQ:
                S.add("vector", lambda e, c=c, hb=hb: e.memset(
                    t_xmh[:, hb + c * (TT + 2) + TT + 1:
                          hb + c * (TT + 2) + TT + 2], 0.0),
                    inc=VD)
        for c in range(NC_I):
            dma("gpsimd", VE, t_sres[:, sb + c * TT:sb + (c + 1) * TT],
                s_sres[c * 128:(c + 1) * 128, t0:t0 + TT],
                waits=((a2_start if i <= 1 else [(VD, ychv[i - 2])])
                       if c == 0 else ()))
        a2_xml[i] = [(SY, SY.n), (VE, VE.n), (VD, VD.n)]
        a2_xmh[i] = [(SY, SY.n), (VD, VD.n)]

    silu_sd = {}

    def a2_conv1(i, c0, c1):
        par = i % 2
        hb = par * NC_I * (TT + 2)
        xcb = par * NC_I * TT
        for c in range(c0, c1):
            base = hb + c * (TT + 2)
            S.add("scalar",
                  lambda e, c=c, base=base, xcb=xcb: e.activation(
                      out=t_xc[:, xcb + c * TT:xcb + (c + 1) * TT],
                      in_=t_xmh[:, base + 1:base + 1 + TT], func=AF.Identity,
                      bias=t_conv[:, c * 4 + 3:c * 4 + 4],
                      scale=t_conv[:, c * 4 + 1:c * 4 + 2]),
                  waits=(a2_xmh[i] + [(TM, wout_tm[i - 2]),
                                      (VD, ychv[i - 2])]) if c == c0 else [],
                  inc=SD)
        cvs[i] = SD.n

    def a2_taps(i, c0, c1):
        par = i % 2
        hb = par * NC_I * (TT + 2)
        xcb = par * NC_I * TT
        sconv = (SD, cvs[i])
        for c in range(c0, c1):
            base = hb + c * (TT + 2)
            S.add("vector",
                  lambda e, c=c, base=base, xcb=xcb: e.scalar_tensor_tensor(
                      out=t_xc[:, xcb + c * TT:xcb + (c + 1) * TT],
                      in0=t_xmh[:, base:base + TT],
                      scalar=t_conv[:, c * 4 + 0:c * 4 + 1],
                      in1=t_xc[:, xcb + c * TT:xcb + (c + 1) * TT],
                      op0=ALU.mult, op1=ALU.add),
                  waits=[sconv] if c == c0 else [], inc=VD)
            S.add("vector",
                  lambda e, c=c, base=base, xcb=xcb: e.scalar_tensor_tensor(
                      out=t_xc[:, xcb + c * TT:xcb + (c + 1) * TT],
                      in0=t_xmh[:, base + 2:base + 2 + TT],
                      scalar=t_conv[:, c * 4 + 2:c * 4 + 3],
                      in1=t_xc[:, xcb + c * TT:xcb + (c + 1) * TT],
                      op0=ALU.mult, op1=ALU.add),
                  inc=VD)
        cvv[i] = VD.n

    def a2_silu(i, c0, c1):
        par = i % 2
        xcb = par * NC_I * TT
        S.add("scalar",
              lambda e, xcb=xcb, c0=c0, c1=c1: e.activation(
                  out=t_xc[:, xcb + c0 * TT:xcb + c1 * TT],
                  in_=t_xc[:, xcb + c0 * TT:xcb + c1 * TT], func=AF.Silu),
              waits=[(VD, cvv[i])], inc=SD)
        silu_sd[i] = SD.n

    def a2_chain(i):
        par = i % 2
        xcb = par * NC_I * TT
        silu_done = (SD, silu_sd[i])
        for k in range(NC_I):
            S.add("tensor",
                  lambda e, k=k, xcb=xcb: e.matmul(
                      ps_sm[0:16, :], t_wbc[:, k * 32:k * 32 + 16],
                      t_xc[:, xcb + k * TT:xcb + (k + 1) * TT],
                      start=(k == 0), stop=(k == NC_I - 1)),
                  waits=([silu_done, (SD, ysmm_sd[i - 1])]
                         if k == 0 else []),
                  inc=None)
            S.add("tensor",
                  lambda e, k=k, xcb=xcb: e.matmul(
                      ps_bc[0:16, :], t_wbc[:, k * 32 + 16:k * 32 + 32],
                      t_xc[:, xcb + k * TT:xcb + (k + 1) * TT],
                      start=(k == 0), stop=(k == NC_I - 1)),
                  waits=[(SD, nrm2_prev[i - 3][0][1])] if k == 0 else [],
                  inc=TM if k == NC_I - 1 else None)
        bcmm[i] = TM.n
        S.add("scalar", lambda e: e.copy(out=t_bt[:], in_=ps_sm[0:16, :]),
              waits=[(TM, TM.n), (VD, prodv[i - 1])], inc=SD)
        S.add("scalar", lambda e: e.copy(out=t_ct[:], in_=ps_bc[0:16, :]),
              inc=SD)
        S.add("vector", lambda e: e.tensor_tensor_scan(
            out=t_states[:], data0=t_decay[:], data1=t_bt[:],
            initial=t_carry[:], op0=ALU.mult, op1=ALU.add),
            waits=[(SD, SD.n)], inc=VD)
        S.add("vector", lambda e: e.tensor_copy(out=t_carry[:],
                                                in_=t_states[:, TT - 1:TT]),
              waits=[(VD, VD.n)], inc=VD)
        S.add("vector", lambda e: e.tensor_tensor(out=t_prod[:], in0=t_states[:],
                                                  in1=t_ct[:], op=ALU.mult),
              waits=[(VD, VD.n), (TM, ysmm[i - 1])], inc=VD)
        prodv[i] = VD.n
        S.add("tensor", lambda e: e.matmul(ps_sm[0:1, :], t_ones16[:], t_prod[:],
                                           start=True, stop=True),
              waits=[(VD, VD.n)], inc=TM)
        ysmm[i] = TM.n
        S.add("scalar", lambda e: e.copy(out=t_ys[:], in_=ps_sm[0:1, :]),
              waits=[(TM, TM.n)], inc=SD)
        S.add("tensor", lambda e: e.matmul(ps_bc[:], t_ones_row[:], t_ys[:],
                                           start=True, stop=True),
              waits=[(SD, SD.n)], inc=TM)
        S.add("scalar",
              lambda e, par=par: e.copy(
                  out=t_ysbc[:, par * TT:(par + 1) * TT], in_=ps_bc[:]),
              waits=[(TM, TM.n), (VD, ychv[i - 1])], inc=SD)
        ysmm_sd[i] = SD.n

    def a2_y(i):
        par = i % 2
        xcb = par * NC_I * TT
        xc3 = t_xc[:, xcb:xcb + NC_I * TT].rearrange("p (c t) -> p c t", t=TT)
        dbc3 = t_dbf[:, 0:NC_I].unsqueeze(2).broadcast_to([128, NC_I, TT])
        ysbc3 = t_ysbc[:, par * TT:(par + 1) * TT].unsqueeze(1).broadcast_to(
            [128, NC_I, TT])
        sres3 = t_sres[:, xcb:xcb + NC_I * TT].rearrange("p (c t) -> p c t",
                                                         t=TT)
        S.add("vector", lambda e, a=xc3, d=dbc3: e.tensor_tensor(
            out=a, in0=a, in1=d, op=ALU.mult),
            waits=[(SD, ysmm_sd[i])] + a2_xml[i], inc=VD)
        S.add("vector", lambda e, a=xc3, y=ysbc3: e.tensor_tensor(
            out=a, in0=a, in1=y, op=ALU.add), inc=VD)
        S.add("vector", lambda e, a=xc3, s=sres3: e.tensor_tensor(
            out=a, in0=a, in1=s, op=ALU.mult), inc=VD)
        ychv[i] = VD.n

    a2_rd = {}
    a2_psf = {}

    def a2_stats(i):
        par = i % 2
        xbb = par * NC_D * TT
        emit_stats_block(t_x2bf, xbb, mm_waits=[(SD, a2_xbf_sd[i])],
                         first_mm_waits=[(VD, nrm2_prev[i - 1][1])],
                         sq_tm_wait=st2mm[i - 1])
        st2mm[i] = TM.n

    def a2_rstd1(i):
        a2_rd[i], a2_psf[i] = emit_rstd_chain(
            i % 2, entry_waits=[(TM, st2mm[i])])

    def a2_rstd2(i):
        nrm2_prev[i] = ((SD, emit_rstd_bcast(i % 2, a2_rd[i])[1]), a2_psf[i])

    def a2_xn2(i):
        par = i % 2
        xbb = par * NC_D * TT
        t0 = i * TT
        m3 = t_mbc[:, par * TT:(par + 1) * TT].unsqueeze(1).broadcast_to(
            [128, NC_D, TT])
        r3 = t_rbc[:, par * TT:(par + 1) * TT].unsqueeze(1).broadcast_to(
            [128, NC_D, TT])
        xf3 = t_x2bf[:, xbb:xbb + NC_D * TT].rearrange("p (c t) -> p c t",
                                                       t=TT)
        S.add("vector",
              lambda e, a=xf3, m=m3: e.tensor_tensor(
                  out=a, in0=a, in1=m, op=ALU.subtract),
              waits=[nrm2_prev[i][0]], inc=VD)
        S.add("vector",
              lambda e, a=xf3, r=r3: e.tensor_tensor(
                  out=a, in0=a, in1=r, op=ALU.mult), inc=VD)
        xn2done = (VD, VD.n)
        for c in range(NC_D):
            dma("gpsimd", VE, s_x2n[c * 128:(c + 1) * 128, t0:t0 + TT],
                t_x2bf[:, xbb + c * TT:xbb + (c + 1) * TT],
                waits=[xn2done] if c == 0 else ())
        xn2st[par] = VE.n
        S.add("vector",
              lambda e, xbb=xbb: e.tensor_reduce(
                  out=t_red[:, 0:1], in_=t_x2bf[:, xbb:xbb + NC_D * TT],
                  axis=AX, op=ALU.max, apply_absolute_value=True),
              inc=VD)
        S.add("vector",
              lambda e: e.tensor_tensor(out=t_amax[:], in0=t_amax[:],
                                        in1=t_red[:, 0:1], op=ALU.max),
              waits=[(VD, VD.n)], inc=VD)
        xn2v[i] = VD.n

    a2_loads(0)
    a2_conv1(0, 0, NC_I)
    a2_taps(0, 0, NC_I)
    a2_silu(0, 0, NC_I)
    a2_chain(0)
    a2_y(0)

    for i in range(NT):
        par = i % 2
        xcb = par * NC_I * TT
        xbb = par * NC_D * TT
        t0 = i * TT
        if i + 1 < NT:
            a2_loads(i + 1)
        for c in range(NC_D):
            dma("sync", SY, t_xfp[:, c * TT:(c + 1) * TT],
                x_fm[c * 128:(c + 1) * 128, t0:t0 + TT],
                waits=[(TM, wout_tm[i - 1])] if c == 0 else ())
        xfp_ld = (SY, SY.n)

        psj = [ps_a, ps_b]
        for jj in range(NC_D):
            pb = psj[jj % 2]
            sl = jj % 2
            S.add("tensor",
                  lambda e, pb=pb, jj=jj: e.matmul(
                      pb[:], t_identf[:], t_xfp[:, jj * TT:(jj + 1) * TT],
                      start=True, stop=False),
                  waits=[xfp_ld, (SD, evac_sd.get(sl, 0))]
                        + ([(VD, ychv[i])] if jj == 0 else []),
                  inc=None)
            for k in range(NC_I):
                S.add("tensor",
                      lambda e, k=k, jj=jj, pb=pb, xcb=xcb: e.matmul(
                          pb[:],
                          t_wout[:, k * 1024 + jj * 128:
                                 k * 1024 + (jj + 1) * 128],
                          t_xc[:, xcb + k * TT:xcb + (k + 1) * TT],
                          start=False, stop=(k == NC_I - 1)),
                      inc=TM if k == NC_I - 1 else None)
            mm_done = (TM, TM.n)
            S.add("scalar",
                  lambda e, sl=sl, pb=pb: e.copy(
                      out=t_x2c[:, sl * TT:(sl + 1) * TT], in_=pb[:]),
                  waits=[mm_done, (VE, x2st.get(sl, 0))], inc=SD)
            dma("gpsimd", VE, s_x2[jj * 128:(jj + 1) * 128, t0:t0 + TT],
                t_x2c[:, sl * TT:(sl + 1) * TT], waits=[(SD, SD.n)])
            x2st[sl] = VE.n
            S.add("scalar",
                  lambda e, jj=jj, pb=pb, xbb=xbb: e.copy(
                      out=t_x2bf[:, xbb + jj * TT:xbb + (jj + 1) * TT],
                      in_=pb[:]),
                  waits=([(TM, st2mm[i - 2]), (VD, xn2v[i - 2]),
                          (VE, xn2st.get(par, 0))] if jj == 0 else []),
                  inc=SD)
            evac_sd[sl] = SD.n
            if jj == NC_D - 1:
                a2_xbf_sd[i] = SD.n
            if i + 1 < NT:
                if jj == 0:
                    a2_conv1(i + 1, 0, 8)
                elif jj == 1:
                    a2_conv1(i + 1, 8, NC_I)
                    a2_taps(i + 1, 0, 8)
                elif jj == 2:
                    a2_taps(i + 1, 8, NC_I)
                elif jj == 3:
                    a2_silu(i + 1, 0, 8)
                elif jj == 4:
                    a2_silu(i + 1, 8, NC_I)
                    a2_chain(i + 1)
                elif jj == 6:
                    a2_y(i + 1)
            if i >= 1:
                if jj == 2:
                    a2_stats(i - 1)
                elif jj == 3:
                    a2_rstd1(i - 1)
                elif jj == 5:
                    a2_rstd2(i - 1)
                elif jj == 7:
                    a2_xn2(i - 1)
        wout_tm[i] = TM.n

    a2_stats(NT - 1)
    a2_rstd1(NT - 1)
    a2_rstd2(NT - 1)
    a2_xn2(NT - 1)

    def part_max(src_col_ap, dst_col):
        S.add("tensor", lambda e: e.transpose(ps_sm[0:1, 0:128], src_col_ap,
                                              t_identf[:]),
              waits=[(VD, VD.n)], inc=TM)
        S.add("vector",
              lambda e, d=dst_col: e.tensor_reduce(out=t_gmax[:, d:d + 1],
                                                   in_=ps_sm[0:1, 0:128],
                                                   axis=AX, op=ALU.max),
              waits=[(TM, TM.n)], inc=VD)

    def do_allreduce(col0, col1):
        S.add("vector", lambda e, c=col0: e.tensor_copy(out=t_ccin[:, 0:1],
                                                        in_=t_gmax[:, c:c + 1]),
              waits=[(VD, VD.n), (GD, GD.n)], inc=VD)
        S.add("vector", lambda e, c=col1: e.tensor_copy(out=t_ccin[:, 1:2],
                                                        in_=t_gmax[:, c:c + 1]),
              inc=VD)
        dma("gpsimd", GP, cc_in, t_ccin[:], waits=[(VD, VD.n)])
        S.add("gpsimd",
              lambda e: e.collective_compute(
                  "AllReduce", ALU.max, replica_groups=[list(range(NCORES))],
                  ins=[cc_in.opt()], outs=[cc_out.opt()]),
              waits=[(GP, GP.n)], inc=GD)
        dma("gpsimd", GP, t_ccout[:], cc_out, waits=[(GD, GD.n)])
        return (GP, GP.n)

    def bcast_scalars():
        S.add("tensor", lambda e: e.matmul(ps_bc[:, 0:8], t_ones_row[:], t_sc1[:],
                                           start=True, stop=True),
              waits=[(VD, VD.n), (SD, SD.n)], inc=TM)
        S.add("scalar", lambda e: e.copy(out=t_scpp[:], in_=ps_bc[:, 0:8]),
              waits=[(TM, TM.n)], inc=SD)
        return (SD, SD.n)

    def vop(emit, waits=()):
        S.add("vector", emit, waits=waits, inc=VD)

    part_max(t_amax[:], 0)
    b0 = do_allreduce(0, 0)
    vop(lambda e: e.reciprocal(out=t_sc1[:, 0:1], in_=t_ccout[:, 0:1]),
        waits=[b0, (SD, SD.n)])
    vop(lambda e: e.tensor_scalar(out=t_sc1[:, 0:1], in0=t_sc1[:, 0:1],
                                  scalar1=QMAX, scalar2=None, op0=ALU.mult),
        waits=[(VD, VD.n)])
    vop(lambda e: e.tensor_scalar(out=t_sc1[:, 1:2], in0=t_ccout[:, 0:1],
                                  scalar1=1.0 / QMAX, scalar2=None, op0=ALU.mult))
    bar0 = bcast_scalars()

    a2_es.close()
    ash_es.close()
    mx_es.close()

    c_es = ExitStack()

    def csb(name, shape, dt):
        return c_es.enter_context(nc.sbuf_tensor(name, shape, dt))

    t_wT1 = csb("t_wT1", [128, NC_D * D_FFN], BF16)
    t_wT2 = csb("t_wT2", [128, NC_D * D_FFN], BF16)
    t_xn2t = csb("t_xn2t", [128, NC_D * TT], BF16)
    t_tbc = csb("t_tbc", [128, NC_D * TT], F32)
    t_qx = csb("t_qx", [128, 2 * NC_D * TT], BF16)
    t_upsb = csb("t_upsb", [128, NC_F * TT], BF16)
    t_gatesb = csb("t_gatesb", [128, NC_F * TT], BF16)

    c_start = snap()
    for k in range(NC_D):
        dma("scalar", SC, t_wT1[:, k * D_FFN:(k + 1) * D_FFN],
            s_wupT[k * 128:(k + 1) * 128, :],
            waits=([dq3] + c_start) if k == 0 else ())
        dma("scalar", SC, t_wT2[:, k * D_FFN:(k + 1) * D_FFN],
            s_wgateT[k * 128:(k + 1) * 128, :])
    wTl = (SC, SC.n)
    vop(lambda e: e.memset(t_amax[:], 0.0), waits=[bar0])
    vop(lambda e: e.memset(t_amax2[:], 0.0))

    cq_ld = {}
    cq_done = {}
    cq_mm = {-1: 0, -2: 0}
    redv = {-1: 0}

    def c_prep(i):
        par = i % 2
        t0 = i * TT
        for c in range(NC_D):
            dma("sync", SY, t_xn2t[:, c * TT:(c + 1) * TT],
                s_x2n[c * 128:(c + 1) * 128, t0:t0 + TT],
                waits=([(SD, SD.n), (VD, VD.n), (VE, VE.n)] if c == 0 else ()))
        cq_ld[i] = (SY, SY.n)
        S.add("scalar", lambda e: e.activation(out=t_tbc[:], in_=t_xn2t[:],
                                               func=AF.Identity,
                                               bias=t_consts[:, 0:1],
                                               scale=t_scpp[:, 0:1]),
              waits=[cq_ld[i], bar0], inc=SD)
        S.add("vector",
              lambda e, par=par: e.tensor_scalar(
                  out=t_qx[:, par * NC_D * TT:(par + 1) * NC_D * TT],
                  in0=t_tbc[:], scalar1=-C32, scalar2=None, op0=ALU.add),
              waits=[(SD, SD.n), (TM, cq_mm[i - 2])], inc=VD)
        cq_done[i] = (VD, VD.n)

    c_prep(0)
    evu = {m: 0 for m in range(-2, 0)}
    evg = {m: 0 for m in range(-2, 0)}
    for i in range(NT):
        par = i % 2
        qxb = par * NC_D * TT
        t0 = i * TT
        qx_done = cq_done[i]
        if i > 0:
            evu = {-2: evu[NC_F - 2], -1: evu[NC_F - 1]}
            evg = {-2: evg[NC_F - 2], -1: evg[NC_F - 1]}
        for m in range(NC_F):
            pu = ps_a if m % 2 == 0 else ps_b
            pg = ps_c if m % 2 == 0 else ps_bc
            uw = [(SD, evu[m - 2])]
            gw = [(VD, evg[m - 2])]
            for k in range(NC_D):
                S.add("tensor",
                      lambda e, m=m, k=k, pu=pu, qxb=qxb: e.matmul(
                          pu[:],
                          t_wT1[:, k * D_FFN + m * 128:k * D_FFN + (m + 1) * 128],
                          t_qx[:, qxb + k * TT:qxb + (k + 1) * TT],
                          start=(k == 0), stop=(k == NC_D - 1)),
                      waits=([qx_done, wTl] + uw if k == 0 else []),
                      inc=TM if k == NC_D - 1 else None)
            S.add("scalar",
                  lambda e, m=m, pu=pu: e.copy(out=t_upsb[:, m * TT:(m + 1) * TT],
                                               in_=pu[:]),
                  waits=[(TM, TM.n)] + ([(SC, SC.n), (VD, redv[i - 1])]
                                        if m == 0 else []), inc=SD)
            evu[m] = SD.n
            dma("scalar", SC, s_up[m * 128:(m + 1) * 128, t0:t0 + TT],
                t_upsb[:, m * TT:(m + 1) * TT], waits=[(SD, SD.n)])
            for k in range(NC_D):
                S.add("tensor",
                      lambda e, m=m, k=k, pg=pg, qxb=qxb: e.matmul(
                          pg[:],
                          t_wT2[:, k * D_FFN + m * 128:k * D_FFN + (m + 1) * 128],
                          t_qx[:, qxb + k * TT:qxb + (k + 1) * TT],
                          start=(k == 0), stop=(k == NC_D - 1)),
                      waits=gw if k == 0 else [],
                      inc=TM if k == NC_D - 1 else None)
            S.add("vector",
                  lambda e, m=m, pg=pg: e.tensor_copy(
                      out=t_gatesb[:, m * TT:(m + 1) * TT], in_=pg[:]),
                  waits=[(TM, TM.n)] + ([(SY, SY.n)] if m == 0 else []), inc=VD)
            evg[m] = VD.n
            dma("sync", SY, s_gate[m * 128:(m + 1) * 128, t0:t0 + TT],
                t_gatesb[:, m * TT:(m + 1) * TT], waits=[(VD, VD.n)])
            if m % 3 == 2:
                lo = (m - 2) * TT
                hi = (m + 1) * TT
                vop(lambda e, lo=lo, hi=hi: e.tensor_reduce(
                    out=t_red[:, 0:1], in_=t_upsb[:, lo:hi], axis=AX,
                    op=ALU.max, apply_absolute_value=True),
                    waits=[(SD, evu[m])])
                vop(lambda e: e.tensor_tensor(out=t_amax[:], in0=t_amax[:],
                                              in1=t_red[:, 0:1], op=ALU.max),
                    waits=[(VD, VD.n)])
                vop(lambda e, lo=lo, hi=hi: e.tensor_reduce(
                    out=t_red[:, 1:2], in_=t_gatesb[:, lo:hi], axis=AX,
                    op=ALU.max, apply_absolute_value=True))
                vop(lambda e: e.tensor_tensor(out=t_amax2[:], in0=t_amax2[:],
                                              in1=t_red[:, 1:2], op=ALU.max),
                    waits=[(VD, VD.n)])
            if m == 15 and i + 1 < NT:
                c_prep(i + 1)
        cq_mm[i] = TM.n
        redv[i] = VD.n

    part_max(t_amax[:], 1)
    part_max(t_amax2[:], 2)
    b1 = do_allreduce(1, 2)
    vop(lambda e: e.reciprocal(out=t_sc1[:, 2:3], in_=t_ccout[:, 0:1]),
        waits=[b1, (SD, SD.n)])
    vop(lambda e: e.reciprocal(out=t_sc1[:, 3:4], in_=t_ccout[:, 1:2]),
        waits=[(VD, VD.n)])
    vop(lambda e: e.tensor_scalar(out=t_sc1[:, 2:4], in0=t_sc1[:, 2:4],
                                  scalar1=QMAX, scalar2=None, op0=ALU.mult),
        waits=[(VD, VD.n)])
    vop(lambda e: e.tensor_tensor(out=t_sc1[:, 4:5], in0=t_ccout[:, 1:2],
                                  in1=t_sc1[:, 1:2], op=ALU.mult))
    vop(lambda e: e.tensor_scalar(out=t_sc1[:, 4:5], in0=t_sc1[:, 4:5],
                                  scalar1=1.0 / QMAX, scalar2=None, op0=ALU.mult),
        waits=[(VD, VD.n)])
    vop(lambda e: e.tensor_scalar(out=t_sc1[:, 5:6], in0=t_sc1[:, 4:5],
                                  scalar1=-C32, scalar2=None, op0=ALU.mult),
        waits=[(VD, VD.n)])
    bar1 = bcast_scalars()

    c_es.close()

    d_es = ExitStack()

    def dsb(name, shape, dt):
        return d_es.enter_context(nc.sbuf_tensor(name, shape, dt))

    t_gin = dsb("t_gin", [128, 2 * SUB * TT], BF16)
    t_uin = dsb("t_uin", [128, 2 * SUB * TT], BF16)
    t_tbg = dsb("t_tbg", [128, SUB * TT], F32)
    t_sig = dsb("t_sig", [128, SUB * TT], F32)
    t_tbsi = dsb("t_tbsi", [128, SUB * TT], F32)
    t_sibf = dsb("t_sibf", [128, SUB * TT], BF16)
    t_gqbf = dsb("t_gqbf", [128, SUB * TT], BF16)
    t_pout = dsb("t_pout", [128, 2 * SUB * TT], BF16)
    t_dabs = dsb("t_dabs", [128, 2 * SUB * TT], BF16)

    d_start = snap()
    vop(lambda e: e.memset(t_amax[:], 0.0), waits=[bar1])
    subs = [(i, sub) for i in range(NT) for sub in range(NSUB)]
    d_tb = {-1: 0, -2: 0}
    d_gq = {-1: 0}
    d_pst = {}

    def d1_load(s):
        i, sub = subs[s]
        gb = (s % 2) * SUB * TT
        for m in range(SUB):
            dma("sync", SY, t_gin[:, gb + m * TT:gb + (m + 1) * TT],
                s_gate[(sub * SUB + m) * 128:(sub * SUB + m + 1) * 128,
                       i * TT:(i + 1) * TT],
                waits=(([(SD, d_tb[s - 2])] + (d_start if s == 0 else []))
                       if m == 0 else ()))
        return (SY, SY.n)

    d_gld = {0: d1_load(0)}
    for s in range(len(subs)):
        i, sub = subs[s]
        t0 = i * TT
        m0 = sub * SUB
        gb = (s % 2) * SUB * TT
        pb = (s % 2) * SUB * TT
        if s + 1 < len(subs):
            d_gld[s + 1] = d1_load(s + 1)
        S.add("scalar", lambda e, gb=gb: e.activation(
            out=t_tbg[:], in_=t_gin[:, gb:gb + SUB * TT],
            func=AF.Identity, bias=t_consts[:, 0:1], scale=t_scpp[:, 3:4]),
            waits=[d_gld[s], bar1, (VD, d_gq[s - 1])], inc=SD)
        S.add("scalar", lambda e: e.activation(out=t_sig[:], in_=t_tbg[:],
                                               func=AF.Sigmoid,
                                               bias=t_scpp[:, 5:6],
                                               scale=t_scpp[:, 4:5]),
              inc=SD)
        S.add("scalar", lambda e: e.activation(out=t_tbsi[:], in_=t_sig[:],
                                               func=AF.Identity,
                                               bias=t_consts[:, 0:1],
                                               scale=t_consts[:, 3:4]),
              inc=SD)
        d_tb[s] = SD.n
        sdn = (SD, SD.n)
        vop(lambda e: e.tensor_scalar(out=t_sibf[:], in0=t_tbsi[:],
                                      scalar1=C32 + 65535.0, scalar2=C32,
                                      op0=ALU.min, op1=ALU.subtract),
            waits=[sdn])
        vop(lambda e: e.tensor_scalar(out=t_gqbf[:], in0=t_tbg[:],
                                      scalar1=-C32, scalar2=None, op0=ALU.add))
        d_gq[s] = VD.n
        vop(lambda e, pb=pb: e.tensor_tensor(
            out=t_pout[:, pb:pb + SUB * TT], in0=t_gqbf[:],
            in1=t_sibf[:], op=ALU.mult),
            waits=[(VD, VD.n), (VE, d_pst.get(s % 2, 0))])
        pdn = (VD, VD.n)
        for m in range(SUB):
            dma("gpsimd", VE,
                s_P[(m0 + m) * 128:(m0 + m + 1) * 128, t0:t0 + TT],
                t_pout[:, pb + m * TT:pb + (m + 1) * TT],
                waits=[pdn] if m == 0 else ())
        d_pst[s % 2] = VE.n
        vop(lambda e, pb=pb: e.tensor_reduce(
            out=t_red[:, 0:1], in_=t_pout[:, pb:pb + SUB * TT],
            axis=AX, op=ALU.max, apply_absolute_value=True),
            waits=[pdn])
        vop(lambda e: e.tensor_tensor(out=t_amax[:], in0=t_amax[:],
                                      in1=t_red[:, 0:1], op=ALU.max),
            waits=[(VD, VD.n)])

    part_max(t_amax[:], 3)
    b2 = do_allreduce(3, 3)
    vop(lambda e: e.reciprocal(out=t_sc1[:, 6:7], in_=t_ccout[:, 0:1]),
        waits=[b2, (SD, SD.n)])
    vop(lambda e: e.tensor_scalar(out=t_sc1[:, 6:7], in0=t_sc1[:, 6:7],
                                  scalar1=QMAX, scalar2=None, op0=ALU.mult),
        waits=[(VD, VD.n)])
    vop(lambda e: e.tensor_tensor(out=t_sc1[:, 7:8], in0=t_sc1[:, 4:5],
                                  in1=t_ccout[:, 0:1], op=ALU.mult))
    vop(lambda e: e.tensor_scalar(out=t_sc1[:, 7:8], in0=t_sc1[:, 7:8],
                                  scalar1=1.0 / 65536.0 / QMAX, scalar2=None,
                                  op0=ALU.mult),
        waits=[(VD, VD.n)])
    bar2 = bcast_scalars()

    vop(lambda e: e.memset(t_amax[:], 0.0), waits=[bar2])
    d2_start = [(SD, SD.n), (VE, VE.n), (VD, VD.n)]
    d_tb = {-1: 0, -2: 0}
    d_gq = {-1: 0}
    d_pst = {}

    def d2_load(s):
        i, sub = subs[s]
        gb = (s % 2) * SUB * TT
        for m in range(SUB):
            dma("sync", SY, t_gin[:, gb + m * TT:gb + (m + 1) * TT],
                s_P[(sub * SUB + m) * 128:(sub * SUB + m + 1) * 128,
                    i * TT:(i + 1) * TT],
                waits=(([(SD, d_tb[s - 2])] + (d2_start if s <= 1 else []))
                       if m == 0 else ()))
            dma("scalar", SC, t_uin[:, gb + m * TT:gb + (m + 1) * TT],
                s_up[(sub * SUB + m) * 128:(sub * SUB + m + 1) * 128,
                     i * TT:(i + 1) * TT],
                waits=[(SD, d_tb[s - 2])] if m == 0 else ())
        return [(SY, SY.n), (SC, SC.n)]

    d_gld = {0: d2_load(0)}
    for s in range(len(subs)):
        i, sub = subs[s]
        t0 = i * TT
        m0 = sub * SUB
        gb = (s % 2) * SUB * TT
        pb = (s % 2) * SUB * TT
        if s + 1 < len(subs):
            d_gld[s + 1] = d2_load(s + 1)
        S.add("scalar", lambda e, gb=gb: e.activation(
            out=t_tbg[:], in_=t_gin[:, gb:gb + SUB * TT],
            func=AF.Identity, bias=t_consts[:, 0:1], scale=t_scpp[:, 6:7]),
            waits=d_gld[s] + [bar2, (VD, d_gq[s - 1])], inc=SD)
        S.add("scalar", lambda e, gb=gb: e.activation(
            out=t_tbsi[:], in_=t_uin[:, gb:gb + SUB * TT],
            func=AF.Identity, bias=t_consts[:, 0:1], scale=t_scpp[:, 2:3]),
            inc=SD)
        d_tb[s] = SD.n
        sdn = (SD, SD.n)
        vop(lambda e: e.tensor_scalar(out=t_gqbf[:], in0=t_tbg[:],
                                      scalar1=-C32, scalar2=None, op0=ALU.add),
            waits=[sdn])
        vop(lambda e: e.tensor_scalar(out=t_sibf[:], in0=t_tbsi[:],
                                      scalar1=-C32, scalar2=None, op0=ALU.add))
        d_gq[s] = VD.n
        vop(lambda e, pb=pb: e.tensor_tensor(
            out=t_pout[:, pb:pb + SUB * TT], in0=t_gqbf[:],
            in1=t_sibf[:], op=ALU.mult),
            waits=[(VD, VD.n), (VE, d_pst.get(s % 2, 0))])
        pdn = (VD, VD.n)
        for m in range(SUB):
            dma("gpsimd", VE,
                s_O1[(m0 + m) * 128:(m0 + m + 1) * 128, t0:t0 + TT],
                t_pout[:, pb + m * TT:pb + (m + 1) * TT],
                waits=[pdn] if m == 0 else ())
        d_pst[s % 2] = VE.n
        vop(lambda e, pb=pb: e.tensor_reduce(
            out=t_red[:, 0:1], in_=t_pout[:, pb:pb + SUB * TT],
            axis=AX, op=ALU.max, apply_absolute_value=True),
            waits=[pdn])
        vop(lambda e: e.tensor_tensor(out=t_amax[:], in0=t_amax[:],
                                      in1=t_red[:, 0:1], op=ALU.max),
            waits=[(VD, VD.n)])

    part_max(t_amax[:], 4)
    b3 = do_allreduce(4, 4)
    vop(lambda e: e.reciprocal(out=t_gmax[:, 5:6], in_=t_sc1[:, 2:3]),
        waits=[b3, (SD, SD.n)])
    vop(lambda e: e.tensor_tensor(out=t_gmax[:, 6:7], in0=t_gmax[:, 5:6],
                                  in1=t_sc1[:, 1:2], op=ALU.mult),
        waits=[(VD, VD.n)])
    vop(lambda e: e.reciprocal(out=t_sc1[:, 2:3], in_=t_ccout[:, 0:1]),
        waits=[(VD, VD.n)])
    vop(lambda e: e.tensor_scalar(out=t_sc1[:, 2:3], in0=t_sc1[:, 2:3],
                                  scalar1=QMAX, scalar2=None, op0=ALU.mult),
        waits=[(VD, VD.n)])
    vop(lambda e: e.tensor_tensor(out=t_sc1[:, 3:4], in0=t_sc1[:, 7:8],
                                  in1=t_gmax[:, 6:7], op=ALU.mult))
    vop(lambda e: e.tensor_tensor(out=t_sc1[:, 3:4], in0=t_sc1[:, 3:4],
                                  in1=t_ccout[:, 0:1], op=ALU.mult),
        waits=[(VD, VD.n)])
    vop(lambda e: e.tensor_scalar(out=t_sc1[:, 3:4], in0=t_sc1[:, 3:4],
                                  scalar1=1.0 / QMAX, scalar2=None, op0=ALU.mult),
        waits=[(VD, VD.n)])
    bar3 = bcast_scalars()

    d_es.close()

    e_es = ExitStack()

    def esb(name, shape, dt):
        return e_es.enter_context(nc.sbuf_tensor(name, shape, dt))

    t_wdT = esb("t_wdT", [128, NC_F * 1024], BF16)
    t_o1 = esb("t_o1", [128, NC_F * TT], BF16)
    t_tbe = esb("t_tbe", [128, NC_F * TT], F32)
    t_qo = esb("t_qo", [128, 2 * NC_F * TT], BF16)
    t_xe = esb("t_xe", [128, 2 * TT], F32)
    t_oe = esb("t_oe", [128, 2 * TT], F32)

    e_start = snap()
    for k in range(NC_F):
        dma("scalar", SC, t_wdT[:, k * 1024:(k + 1) * 1024],
            s_wdownT[k * 128:(k + 1) * 128, :],
            waits=([bar3] + e_start) if k == 0 else ())
    wdl = (SC, SC.n)
    eq_ld = {}
    eq_done = {}
    eq_qo = {}
    eq_mm = {-1: 0, -2: 0}

    def e_ld(i):
        t0 = i * TT
        for m in range(NC_F):
            dma("sync", SY, t_o1[:, m * TT:(m + 1) * TT],
                s_O1[m * 128:(m + 1) * 128, t0:t0 + TT],
                waits=(([(SD, SD.n), (VE, VE.n)]
                        + (e_start if i == 0 else []))
                       if m == 0 else ()))
        eq_ld[i] = (SY, SY.n)

    def e_prep(i, b):
        par = i % 2
        lo = b * SUB * TT
        hi = (b + 1) * SUB * TT
        S.add("scalar", lambda e, lo=lo, hi=hi: e.activation(
            out=t_tbe[:, lo:hi], in_=t_o1[:, lo:hi],
            func=AF.Identity, bias=t_consts[:, 0:1], scale=t_scpp[:, 2:3]),
              waits=[eq_ld[i], bar3, (VD, eq_qo.get((i - 1, b), 0))],
              inc=SD)
        S.add("vector",
              lambda e, par=par, lo=lo, hi=hi: e.tensor_scalar(
                  out=t_qo[:, par * NC_F * TT + lo:par * NC_F * TT + hi],
                  in0=t_tbe[:, lo:hi], scalar1=-C32, scalar2=None,
                  op0=ALU.add),
              waits=[(SD, SD.n), (TM, eq_mm[i - 2])], inc=VD)
        eq_qo[(i, b)] = VD.n
        if b == NSUB - 1:
            eq_done[i] = (VD, VD.n)

    e_ld(0)
    for b in range(NSUB):
        e_prep(0, b)
    eve = {m: 0 for m in range(-2, 0)}
    sve = {m: 0 for m in range(-2, 0)}
    for i in range(NT):
        par = i % 2
        qob = par * NC_F * TT
        t0 = i * TT
        qo_done = eq_done[i]
        if i > 0:
            eve = {-2: eve[NC_D - 2], -1: eve[NC_D - 1]}
            sve = {-2: sve[NC_D - 2], -1: sve[NC_D - 1]}
        for j in range(NC_D):
            pb = ps_a if j % 2 == 0 else ps_b
            jw = [(VD, eve[j - 2])]
            for k in range(NC_F):
                S.add("tensor",
                      lambda e, j=j, k=k, pb=pb, qob=qob: e.matmul(
                          pb[:], t_wdT[:, k * 1024 + j * 128:k * 1024 + (j + 1) * 128],
                          t_qo[:, qob + k * TT:qob + (k + 1) * TT],
                          start=(k == 0), stop=(k == NC_F - 1)),
                      waits=([qo_done, wdl] + jw if k == 0 else []),
                      inc=TM if k == NC_F - 1 else None)
            dma("sync", SY, t_xe[:, (j % 2) * TT:(j % 2 + 1) * TT],
                s_x2[j * 128:(j + 1) * 128, t0:t0 + TT],
                waits=jw)
            stw = [(VE, sve[j - 2])]
            S.add("vector",
                  lambda e, pb=pb, j=j: e.scalar_tensor_tensor(
                      out=t_oe[:, (j % 2) * TT:(j % 2 + 1) * TT], in0=pb[:],
                      scalar=t_scpp[:, 3:4],
                      in1=t_xe[:, (j % 2) * TT:(j % 2 + 1) * TT],
                      op0=ALU.mult, op1=ALU.add),
                  waits=[(TM, TM.n), (SY, SY.n)] + stw, inc=VD)
            eve[j] = VD.n
            dma("gpsimd", VE, y_fm[j * 128:(j + 1) * 128, t0:t0 + TT],
                t_oe[:, (j % 2) * TT:(j % 2 + 1) * TT],
                waits=[(VD, VD.n)])
            sve[j] = VE.n
            if i + 1 < NT:
                if j == 1:
                    e_ld(i + 1)
                    e_prep(i + 1, 0)
                elif j in (2, 3):
                    e_prep(i + 1, j - 1)
            if j == NC_D - 1:
                eq_mm[i] = TM.n

    if DBG:
        dbg_sc = dout("dbg_sc", [128, 8])
        dbg_s1 = dout("dbg_s1", [1, 8])
        dbg_cc = dout("dbg_cc", [1, 2])
        dma("sync", SY, dbg_sc, t_scpp[:], waits=snap())
        dma("sync", SY, dbg_s1, t_sc1[:])
        dma("sync", SY, dbg_cc, t_ccout[:])

    S.add("sync", lambda e: e.nop(),
          waits=[(VE, VE.n), (SY, SY.n), (SC, SC.n), (GP, GP.n), (VD, VD.n),
                 (SD, SD.n), (TM, TM.n), (GD, GD.n)])

    with nc.Block() as block:
        for eng in ["sync", "scalar", "vector", "tensor", "gpsimd"]:
            @getattr(block, eng)
            def _(e, eng=eng):
                replay(S, eng, e)

    e_es.close()
    g_es.close()
    cm.close()
    return nc


def replay(S, eng_name, e):
    waited = {}
    for op in S.ops:
        if op.eng != eng_name:
            continue
        for (s, v) in op.waits:
            if waited.get(id(s), -1) >= v:
                continue
            e.wait_ge(s.h, v)
            waited[id(s)] = v
        inst = op.emit(e)
        for (s, amt) in op.incs:
            inst.then_inc(s.h, amt)


def kernel(L0_attn_x, L0_attn_W_in, L0_attn_conv_w, L0_attn_conv_b, L0_attn_W_B,
           L0_attn_W_C, L0_attn_A, L0_attn_D, L0_attn_W_out, L0_ffn_w_gate,
           L0_ffn_w_up, L0_ffn_w_down):
    x = np.asarray(L0_attn_x, np.float32)
    B = x.shape[0]
    conv_w = np.asarray(L0_attn_conv_w, np.float32)[:, 0, :]
    conv_b = np.asarray(L0_attn_conv_b, np.float32)
    cw = conv_w.reshape(NC_I, 128, 3).transpose(1, 0, 2)
    cb = conv_b.reshape(NC_I, 128).T[:, :, None]
    conv_pk = np.ascontiguousarray(
        np.concatenate([cw, cb], axis=2).reshape(128, NC_I * 4), np.float32)
    d_pk = np.ascontiguousarray(
        np.asarray(L0_attn_D, np.float32).reshape(NC_I, 128).T)
    a_vec = np.ascontiguousarray(np.asarray(L0_attn_A, np.float32).reshape(16, 1))
    consts = np.zeros((128, 8), np.float32)
    consts[:, 0] = C32
    consts[:, 1] = 1.0 / 1024.0
    consts[:, 2] = 1e-5
    consts[:, 3] = 65536.0
    consts[:, 4] = 1.0

    common = {
        "w_in": np.ascontiguousarray(np.asarray(L0_attn_W_in, np.float32)),
        "conv_pk": conv_pk,
        "w_b": np.ascontiguousarray(np.asarray(L0_attn_W_B, np.float32)),
        "w_c": np.ascontiguousarray(np.asarray(L0_attn_W_C, np.float32)),
        "a_vec": a_vec,
        "d_pk": d_pk,
        "w_out": np.ascontiguousarray(np.asarray(L0_attn_W_out, np.float32)),
        "w_up": np.ascontiguousarray(np.asarray(L0_ffn_w_up, np.float32)),
        "w_gate": np.ascontiguousarray(np.asarray(L0_ffn_w_gate, np.float32)),
        "w_down": np.ascontiguousarray(np.asarray(L0_ffn_w_down, np.float32)),
        "ident_bf": np.eye(128).astype(ml_dtypes.bfloat16),
        "ident_f": np.eye(128, dtype=np.float32),
        "ones_col_bf": np.ones((128, 1), ml_dtypes.bfloat16),
        "ones16_f": np.ones((16, 1), np.float32),
        "ones_row_f": np.ones((1, 128), np.float32),
        "consts": consts,
    }
    in_maps = []
    for b in range(B):
        m = dict(common)
        m["x_fm"] = np.ascontiguousarray(x[b].T)
        in_maps.append(m)

    nc = build_nc()
    res = run_bass_kernel_spmd(nc, in_maps, core_ids=list(range(B)))
    out = np.stack([np.asarray(r["y_fm"]).T for r in res.results], axis=0)
    return np.ascontiguousarray(out.astype(np.float32))


